# version 01
# baseline (speedup 1.0000x reference)
import math
import os
import numpy as np

import concourse.bass as bass
import concourse.mybir as mybir
from concourse.bass_utils import run_bass_kernel_spmd

F32 = mybir.dt.float32
BF16 = mybir.dt.bfloat16
AF = mybir.ActivationFunctionType
ALU = mybir.AluOpType
AXL = mybir.AxisListType

NCORES = 8
D = 1024
NDC = D // 128
NT = 4096
SEQ = 8192
BIG_NEG = -1.0e9

L0_E, L0_H, L0_HC = 16, 1408, 11
L0_SH, L0_SHC = 2816, 22
L1_E, L1_H, L1_HC = 8, 2048, 16
NUM_BUCKETS, MAX_DISTANCE = 32, 128

N_SUBLAYERS = int(os.environ.get("MK_N_SUBLAYERS", "6"))
ERF_FN_NAME = "Tanh" if os.environ.get("MK_ERF_TANH") == "1" else "Erf"
MK_QUARTERS = int(os.environ.get("MK_QUARTERS", "0"))
SIM_RACES = os.environ.get("MK_SIM", "0") == "1"
MK_DBG = os.environ.get("MK_DBG", "0") == "1"

QN = 1024
NQ = NT // QN
QSL = QN // 512


class Prog:
    ENGINES = ("tensor", "vector", "scalar", "gpsimd", "sync")

    def __init__(self, nc):
        self.nc = nc
        self.q = {e: [] for e in self.ENGINES}
        self.cnt = {}
        self.h = {}

    def sem(self, name):
        if name in self.h:
            return
        g = self.nc.semaphore(name)
        self.h[name] = g.__enter__()
        self.cnt[name] = 0

    def bump(self, name, n=1):
        self.cnt[name] += n
        return self.cnt[name]

    def mark(self, name):
        return self.cnt[name]

    def emit(self, eng, fn):
        self.q[eng].append(fn)

    def dma(self, qeng, out, in_, sem_name, **kw):
        h = self.h[sem_name]
        v = self.bump(sem_name, 16)

        def f(eng, out=out, in_=in_, h=h, kw=kw):
            eng.dma_start(out=out, in_=in_, **kw).then_inc(h, 16)
        self.emit(qeng, f)
        return v

    def wait(self, eng_name, sem_name, val):
        if val <= 0:
            return
        h = self.h[sem_name]

        def f(eng, h=h, val=val):
            eng.wait_ge(h, val)
        self.emit(eng_name, f)

    def waits(self, eng_name, **marks):
        for s, v in marks.items():
            self.wait(eng_name, s, v)

    def finalize(self, block):
        for e in self.ENGINES:
            fns = self.q[e]

            def body(eng, fns=fns):
                for f in fns:
                    f(eng)
            getattr(block, e)(body)


class Builder:
    def __init__(self):
        nc = bass.Bass("TRN2", target_bir_lowering=False,
                       detect_race_conditions=False)
        self.nc = nc
        self.ctx = []
        p = Prog(nc)
        self.p = p
        for s in ("ld", "st", "wt", "mm", "sc", "ve", "cc", "g2"):
            p.sem(s)

        def ein(name, shape):
            return nc.dram_tensor(name, list(shape), F32, kind="ExternalInput")
        self.inputs = {}
        i = self.inputs
        i["x0"] = ein("x0", [D, NT])
        i["hflag"] = ein("hflag", [1, 1])
        i["gate_w"] = ein("gate_w", [128, NDC, L0_E])
        i["gate_b"] = ein("gate_b", [1, L0_E])
        i["w1r"] = ein("w1r", [L0_E, L0_HC, 128, 1024])
        i["w3r"] = ein("w3r", [L0_E, L0_HC, 128, 1024])
        i["w2"] = ein("w2", [L0_E, L0_H, D])
        i["b1r"] = ein("b1r", [L0_E, 128, L0_HC])
        i["b3r"] = ein("b3r", [L0_E, 128, L0_HC])
        i["b2"] = ein("b2", [L0_E, D])
        i["sw1r"] = ein("sw1r", [L0_SHC, 128, 1024])
        i["sw3r"] = ein("sw3r", [L0_SHC, 128, 1024])
        i["sw2"] = ein("sw2", [L0_SH, D])
        i["sb1r"] = ein("sb1r", [128, L0_SHC])
        i["sb3r"] = ein("sb3r", [128, L0_SHC])
        i["sb2"] = ein("sb2", [1, D])
        i["WQr"] = ein("WQr", [128, NDC, D])
        i["WKr"] = ein("WKr", [128, NDC, D])
        i["WVr"] = ein("WVr", [128, NDC, D])
        i["wqkr"] = ein("wqkr", [128, NDC, 128])
        i["bqk"] = ein("bqk", [128, 1])
        i["gbcols"] = ein("gbcols", [128, 4])
        i["wvr"] = ein("wvr", [128, NDC, D])
        i["bvrow"] = ein("bvrow", [1, D])
        i["relb"] = ein("relb", [1, NUM_BUCKETS])
        i["W1r"] = ein("W1r", [L1_E, L1_HC, 128, 1024])
        i["W2r"] = ein("W2r", [L1_E, L1_HC, 128, 1024])
        i["b1_1r"] = ein("b1_1r", [L1_E, 128, L1_HC])
        i["b2_1"] = ein("b2_1", [L1_E, D])
        i["Wgr"] = ein("Wgr", [128, NDC, L1_E])
        i["bg_1"] = ein("bg_1", [1, L1_E])

        self.out = nc.dram_tensor("out", [D, NT], F32, kind="ExternalOutput")
        if MK_DBG:
            self.dbg_xt = nc.dram_tensor("dbg_xt", [128, NDC * QN], F32, kind="ExternalOutput")
            self.dbg_cw = nc.dram_tensor("dbg_cw", [16, QN], F32, kind="ExternalOutput")
            self.dbg_yacc = nc.dram_tensor("dbg_yacc", [128, NDC * QN], F32, kind="ExternalOutput")
            self.dbg_stat = nc.dram_tensor("dbg_stat", [1, 4 * QN], F32, kind="ExternalOutput")
            self.dbg_done = [False]
        self.xA = nc.dram_tensor("xA", [D, NT], F32, kind="Internal")
        self.xB = nc.dram_tensor("xB", [D, NT], F32, kind="Internal")

        self.eye_f = nc.inline_tensor(np.eye(128, dtype=np.float32), "eye_f")
        bmnp = np.zeros((128, 640), dtype=np.float32)
        for r in range(128):
            bmnp[r, r:r + 512] = 1.0
        self.bandmask_c = nc.inline_tensor(bmnp, "bandmask")
        oh = np.zeros((16, 16 * 128), dtype=np.float32)
        for e in range(16):
            oh[e, e * 128:(e + 1) * 128] = 1.0
        self.onehot_c = nc.inline_tensor(oh, "onehot16")
        oh2 = np.zeros((NUM_BUCKETS, 2 * NUM_BUCKETS), dtype=np.float32)
        for j, (_d, bj, bp) in enumerate([(1, 1, 0), (2, 2, 1), (3, 3, 2), (4, 4, 3), (5, 5, 4), (6, 6, 5), (7, 7, 6), (8, 8, 7), (9, 9, 8), (10, 10, 9), (11, 11, 10), (12, 12, 11), (13, 13, 12), (14, 14, 13), (15, 15, 14), (16, 16, 15), (19, 17, 16), (21, 18, 17), (24, 19, 18), (27, 20, 19), (31, 21, 20), (35, 22, 21), (40, 23, 22), (46, 24, 23), (52, 25, 24), (59, 26, 25), (67, 27, 26), (77, 28, 27), (87, 29, 28), (99, 30, 29), (113, 31, 30)]):
            oh2[bj, j] = 1.0
            oh2[bp, NUM_BUCKETS + j] = 1.0
        self.bktoh_c = nc.inline_tensor(oh2, "bktoh")

    _uid = 0

    def _un(self, name):
        Builder._uid += 1
        return "%s_%d" % (name, Builder._uid)

    def sb(self, name, shape, dtype):
        g = self.nc.sbuf_tensor(self._un(name), list(shape), dtype)
        t = g.__enter__()
        self.ctx.append(g)
        return t

    def ps(self, name, shape, dtype=F32):
        g = self.nc.psum_tensor(self._un(name), list(shape), dtype)
        t = g.__enter__()
        self.ctx.append(g)
        return t

    def scope(self):
        return len(self.ctx)

    def pop_to(self, n):
        while len(self.ctx) > n:
            self.ctx.pop().__exit__(None, None, None)

    def build_consts(self):
        p = self.p
        self.ones_col = self.sb("ones_col", [128, 1], BF16)
        self.onesf_col = self.sb("onesf_col", [128, 1], F32)
        self.onesf_row = self.sb("onesf_row", [1, 128], F32)
        self.ones_row = self.sb("ones_row", [1, 512], BF16)
        self.eps_c = self.sb("eps_c", [1, 1], F32)
        self.hf_c = self.sb("hf_c", [1, 1], F32)
        self.ident = self.sb("ident", [128, 128], F32)
        self.ident_bf = self.sb("ident_bf", [128, 128], BF16)
        self.onehot = self.sb("onehot", [16, 16 * 128], F32)
        self.bandm = self.sb("bandm", [128, 640], F32)
        p.dma("sync", self.ident[:, :], self.eye_f.ap()[:, :], "ld")
        p.dma("sync", self.onehot[:, :], self.onehot_c.ap()[:, :], "ld")
        p.dma("sync", self.bandm[:, :], self.bandmask_c.ap()[:, :], "ld")
        p.sem("ldg")
        p.dma("gpsimd", self.ident_bf[:, :], self.eye_f.ap()[:, :], "ldg")
        p.dma("sync", self.hf_c[:, :], self.inputs["hflag"].ap()[:, :], "ld")

        def g_init(g):
            h = p.h["g2"]
            g.memset(self.ones_col[:, :], 1.0).then_inc(h, 1)
            g.memset(self.onesf_col[:, :], 1.0).then_inc(h, 1)
            g.memset(self.onesf_row[:, :], 1.0).then_inc(h, 1)
            g.memset(self.ones_row[:, :], 1.0).then_inc(h, 1)
            g.memset(self.eps_c[:, :], 1e-5).then_inc(h, 1)
        p.emit("gpsimd", g_init)
        p.bump("g2", 5)
        self.init_marks = dict(g2=p.mark("g2"), ld=p.mark("ld"), ldg=p.mark("ldg"))

    def join_all(self):
        p = self.p
        marks = {s: p.mark(s) for s in list(p.cnt)}
        for e in Prog.ENGINES:
            for s, v in marks.items():
                p.wait(e, s, v)

    def emit_stats(self, src, t0, n, stat):
        p = self.p
        SS, SQ, SM, SA = 0, n, 2 * n, 3 * n
        xs = self.sb("st_x", [128, 2 * n], BF16)
        sq = self.sb("st_sq", [128, 2 * n], BF16)
        pss = [self.ps("st_ps%d" % k, [1, 512]) for k in range(4)]
        nsl = n // 512

        def z(v):
            v.memset(stat[0:1, 0:2 * n], 0.0).then_inc(p.h["ve"], 1)
        p.emit("vector", z)
        zv = p.bump("ve")

        ld_m = {}
        mm_done = {}
        sc_done = {}
        facc_marks = []
        pi = 0

        p.sem("stx0"); p.sem("stx1")
        p.waits("gpsimd", mm=p.mark("mm"), sc=p.mark("sc"), ve=p.mark("ve"))

        def emit_load(c):
            if c >= 2:
                p.wait("gpsimd", "mm", mm_done[c - 2])
                p.wait("gpsimd", "sc", sc_done[c - 2])
            ld_m[c] = (("stx%d" % (c % 2)),
                       p.dma("gpsimd", xs[:, (c % 2) * n:(c % 2) * n + n],
                             src.ap()[c * 128:(c + 1) * 128, t0:t0 + n], "stx%d" % (c % 2)))

        emit_load(0)
        emit_load(1)
        p.wait("vector", "ve", zv)
        for c in range(NDC):
            xc = xs[:, (c % 2) * n:(c % 2) * n + n]
            sqc = sq[:, (c % 2) * n:(c % 2) * n + n]
            p.wait("scalar", *ld_m[c])

            def fsq(s, xc=xc, sqc=sqc):
                s.activation(sqc, xc, AF.Square).then_inc(p.h["sc"], 1)
            p.emit("scalar", fsq)
            sc_done[c] = p.bump("sc")

            p.wait("tensor", *ld_m[c])
            p.wait("tensor", "sc", sc_done[c])
            for which, buf in ((SS, xc), (SQ, sqc)):
                for sl in range(nsl):
                    pp = pss[pi % 4]
                    if pi >= 4:
                        p.wait("tensor", "ve", facc_marks[pi - 4])

                    def fmm(t, pp=pp, buf=buf, sl=sl):
                        t.matmul(pp[:, :], self.ones_col[:, 0:1],
                                 buf[:, sl * 512:(sl + 1) * 512],
                                 start=True, stop=True).then_inc(p.h["mm"], 1)
                    p.emit("tensor", fmm)
                    mv = p.bump("mm")
                    p.wait("vector", "mm", mv)

                    def facc(v, pp=pp, which=which, sl=sl):
                        v.tensor_add(stat[0:1, which + sl * 512:which + (sl + 1) * 512],
                                     stat[0:1, which + sl * 512:which + (sl + 1) * 512],
                                     pp[0:1, :]).then_inc(p.h["ve"], 1)
                    p.emit("vector", facc)
                    facc_marks.append(p.bump("ve"))
                    pi += 1
            mm_done[c] = p.mark("mm")
            if c + 2 < NDC:
                emit_load(c + 2)

        p.wait("scalar", "ve", p.mark("ve"))
        h = p.h["sc"]

        def f1(s):
            s.mul(stat[0:1, SM:SM + n], stat[0:1, SS:SS + n], 1.0 / 1024.0).then_inc(h, 1)
        p.emit("scalar", f1)
        v = p.bump("sc")
        p.wait("scalar", "sc", v)

        def f2(s):
            s.square(stat[0:1, SS:SS + n], stat[0:1, SM:SM + n]).then_inc(h, 1)
        p.emit("scalar", f2)
        v = p.bump("sc")
        p.wait("scalar", "sc", v)

        def f3(s):
            s.mul(stat[0:1, SQ:SQ + n], stat[0:1, SQ:SQ + n], 1.0 / 1024.0).then_inc(h, 1)
        p.emit("scalar", f3)
        v = p.bump("sc")
        p.wait("vector", "sc", v)

        def f4(v_):
            v_.tensor_sub(stat[0:1, SQ:SQ + n], stat[0:1, SQ:SQ + n],
                          stat[0:1, SS:SS + n]).then_inc(p.h["ve"], 1)
        p.emit("vector", f4)
        vv = p.bump("ve")
        p.wait("scalar", "ve", vv)

        def f5(s):
            s.activation(stat[0:1, SA:SA + n], stat[0:1, SQ:SQ + n], AF.Ln,
                         bias=self.eps_c[0:1, 0:1]).then_inc(h, 1)
        p.emit("scalar", f5)
        v = p.bump("sc")
        p.wait("scalar", "sc", v)

        def f6(s):
            s.activation(stat[0:1, SA:SA + n], stat[0:1, SA:SA + n], AF.Exp,
                         scale=-0.5).then_inc(h, 1)
        p.emit("scalar", f6)
        sc_mark = p.bump("sc")
        return SM, SA, sc_mark

    def emit_xtilde(self, src, xt, t0, n, stat, SM, SA, sc_mark,
                    xt_stride=None, xt_off=0):
        p = self.p
        mb = self.sb("xt_mb", [128, 2 * n], F32)
        xstg = self.sb("xt_stg", [128, 2 * n], F32)
        pb = [self.ps("xt_pb%d" % k, [128, 512]) for k in range(2)]
        nsl = n // 512
        if xt_stride is None:
            xt_stride = n
        p.wait("tensor", "sc", sc_mark)
        p.wait("tensor", "g2", self.init_marks["g2"])
        cp_marks = []
        k = 0
        for which, soff in ((0, SM), (1, SA)):
            for sl in range(nsl):
                pp = pb[k % 2]
                if k >= 2:
                    p.wait("tensor", "ve", cp_marks[k - 2])

                def fmm(t, pp=pp, soff=soff, sl=sl):
                    t.matmul(pp[:, :], self.onesf_row[0:1, :],
                             stat[0:1, soff + sl * 512:soff + (sl + 1) * 512],
                             start=True, stop=True).then_inc(p.h["mm"], 1)
                p.emit("tensor", fmm)
                mv = p.bump("mm")
                p.wait("vector", "mm", mv)

                def fcp(v, pp=pp, which=which, sl=sl):
                    v.tensor_copy(mb[:, which * n + sl * 512:which * n + (sl + 1) * 512],
                                  pp[:, :]).then_inc(p.h["ve"], 1)
                p.emit("vector", fcp)
                cp_marks.append(p.bump("ve"))
                k += 1

        ld_m = {}
        res_marks = []
        p.sem("xtg0"); p.sem("xtg1")
        p.waits("sync", mm=p.mark("mm"), sc=p.mark("sc"), ve=p.mark("ve"))

        def emit_load(c):
            if c >= 2:
                p.wait("sync", "ve", res_marks[c - 2])
            ld_m[c] = (("xtg%d" % (c % 2)),
                       p.dma("sync", xstg[:, (c % 2) * n:(c % 2) * n + n],
                             src.ap()[c * 128:(c + 1) * 128, t0:t0 + n], "xtg%d" % (c % 2)))

        emit_load(0)
        emit_load(1)
        p.wait("vector", "ve", cp_marks[-1])
        for c in range(NDC):
            xc = xstg[:, (c % 2) * n:(c % 2) * n + n]
            p.wait("vector", *ld_m[c])

            def fxt1(v, xc=xc):
                v.tensor_sub(xc, xc, mb[:, 0:n]).then_inc(p.h["ve"], 1)
            p.emit("vector", fxt1)
            v1 = p.bump("ve")
            p.wait("vector", "ve", v1)

            def fxt2(v, xc=xc, c=c):
                v.tensor_mul(xt[:, c * xt_stride + xt_off:c * xt_stride + xt_off + n],
                             xc, mb[:, n:2 * n]).then_inc(p.h["ve"], 1)
            p.emit("vector", fxt2)
            res_marks.append(p.bump("ve"))
            if c + 2 < NDC:
                emit_load(c + 2)
        return p.mark("ve")


    def emit_bcast(self, srcrow, dst, n, pbank, extra_tensor_wait=None):
        p = self.p
        nsl = (n + 511) // 512
        last = 0
        for sl in range(nsl):
            w = min(512, n - sl * 512)
            if extra_tensor_wait is not None:
                p.wait("tensor", extra_tensor_wait[0], extra_tensor_wait[1])
            if sl >= 1:
                p.wait("tensor", "ve", last)

            def fm(t, sl=sl, w=w):
                t.matmul(pbank[:, 0:w], self.onesf_row[0:1, :],
                         srcrow[0:1, sl * 512:sl * 512 + w],
                         start=True, stop=True).then_inc(p.h["mm"], 1)
            p.emit("tensor", fm)
            mv = p.bump("mm")
            p.wait("vector", "mm", mv)

            def fc(v, sl=sl, w=w):
                v.tensor_copy(dst[:, sl * 512:sl * 512 + w],
                              pbank[:, 0:w]).then_inc(p.h["ve"], 1)
            p.emit("vector", fc)
            last = p.bump("ve")
        return last

    def emit_attn(self, layer, src, dst):
        sc_all = self.scope()
        if layer == 0:
            self._attn0(src, dst)
        else:
            self._attn1(src, dst)
        self.pop_to(sc_all)

    def emit_xt_full(self, src):
        xt = self.sb("a_xt", [128, NDC * NT], BF16)
        xt_ve = 0
        for q in range(NQ):
            scq = self.scope()
            stat = self.sb("a_stat", [1, 4 * QN], F32)
            SM, SA, st_sc = self.emit_stats(src, q * QN, QN, stat)
            xt_ve = self.emit_xtilde(src, xt, q * QN, QN, stat, SM, SA, st_sc,
                                     xt_stride=NT, xt_off=q * QN)
            self.pop_to(scq)
        return xt, xt_ve

    def _attn0(self, src, dst):
        p = self.p
        i = self.inputs
        nc = self.nc
        HKN = 4608

        K_hbm = nc.dram_tensor(self._un("a0_K"), [NDC, 128, HKN], BF16, kind="Internal")
        V_hbm = nc.dram_tensor(self._un("a0_V"), [HKN, D], BF16, kind="Internal")
        kx_in = nc.dram_tensor(self._un("a0_kxi"), [NDC, 128, 512], BF16, kind="Internal")
        kx_out = nc.dram_tensor(self._un("a0_kxo"), [2, NDC, 128, 512], BF16, kind="Internal")
        vx_in = nc.dram_tensor(self._un("a0_vxi"), [512, D], BF16, kind="Internal")
        vx_out = nc.dram_tensor(self._un("a0_vxo"), [2, 512, D], BF16, kind="Internal")
        vs_in = nc.dram_tensor(self._un("a0_vsi"), [1, D], F32, kind="Internal")
        vs_out = nc.dram_tensor(self._un("a0_vso"), [1, D], F32, kind="Internal")

        for snm in (["awl", "a0x", "a0h0", "a0h1", "kw0", "kw1", "vw0", "vw1"]
                    + ["ks%d" % x for x in range(4)] + ["vs%d" % x for x in range(2)]
                    + ["xoa%d" % x for x in range(4)] + ["sna%d" % x for x in range(4)]):
            p.sem(snm)

        Vb = self.sb("a0_vbc", [128, D], F32)
        hfc = self.sb("a0_hf", [128, 2], F32)
        Q_fm = self.sb("a0_q", [128, NDC * NT], BF16)
        scp = self.scope()
        xt, xt_ve = self.emit_xt_full(src)
        scpW = self.scope()
        WQs = self.sb("a0_wq", [128, NDC * D], BF16)
        WKs = self.sb("a0_wk", [128, NDC * D], BF16)
        WVs = self.sb("a0_wv", [128, NDC * D], BF16)
        kst = self.sb("a0_kst", [128, 4 * 512], BF16)
        vst = self.sb("a0_vst", [128, 2 * D], BF16)
        vsum = self.sb("a0_vsum", [1, D], F32)
        psp = [self.ps("a0_pp%d" % k2, [128, 512]) for k2 in range(4)]
        psv = [self.ps("a0_pv%d" % k2, [1, 512]) for k2 in range(2)]

        p.waits("gpsimd", mm=p.mark("mm"), sc=p.mark("sc"), ve=p.mark("ve"))
        p.dma("gpsimd", WQs[:, :], i["WQr"].ap()[:, :, :], "awl")
        p.dma("gpsimd", WKs[:, :], i["WKr"].ap()[:, :, :], "awl")
        lv = p.dma("gpsimd", WVs[:, :], i["WVr"].ap()[:, :, :], "awl")
        p.waits("tensor", awl=lv, ve=xt_ve, g2=self.init_marks["g2"],
                ld=self.init_marks["ld"], ldg=self.init_marks["ldg"])

        hh1 = self.emit_bcast(self.hf_c[0:1, 0:1], hfc[:, 0:1], 1, psp[0])
        p.wait("vector", "ve", hh1)

        def fihf(v):
            v.tensor_scalar(hfc[:, 1:2], hfc[:, 0:1], -1.0, 1.0,
                            ALU.mult, ALU.add).then_inc(p.h["ve"], 1)
        p.emit("vector", fihf)
        hf_ve = p.bump("ve")

        q_sc = {}
        gi = 0
        for dq in range(NDC):
            for ts in range(NT // 512):
                pp = psp[gi % 2]
                if gi >= 2:
                    p.wait("tensor", "sc", q_sc[gi - 2])
                elif gi == 0:
                    p.wait("tensor", "ve", hh1)

                def fq(t, dq=dq, ts=ts, pp=pp):
                    ins = None
                    for c in range(NDC):
                        ins = t.matmul(pp[:, :],
                                       WQs[:, c * D + dq * 128:c * D + (dq + 1) * 128],
                                       xt[:, c * NT + ts * 512:c * NT + ts * 512 + 512],
                                       start=(c == 0), stop=(c == NDC - 1))
                    ins.then_inc(p.h["mm"], 1)
                p.emit("tensor", fq)
                mv = p.bump("mm")
                p.wait("scalar", "mm", mv)

                def fqe(s2, dq=dq, ts=ts, pp=pp):
                    s2.activation(Q_fm[:, dq * NT + ts * 512:dq * NT + ts * 512 + 512],
                                  pp[:, :], AF.Copy, scale=1.0 / 32.0).then_inc(p.h["sc"], 1)
                p.emit("scalar", fqe)
                q_sc[gi] = p.bump("sc")
                gi += 1

        k_ve = {}
        k_st = {}
        gi = 0
        for dq in range(NDC):
            for ts in range(NT // 512):
                pp = psp[2 + gi % 2]
                slot = gi % 4
                if gi >= 2:
                    p.wait("tensor", "ve", k_ve[gi - 2])

                def fk(t, dq=dq, ts=ts, pp=pp):
                    ins = None
                    for c in range(NDC):
                        ins = t.matmul(pp[:, :],
                                       WKs[:, c * D + dq * 128:c * D + (dq + 1) * 128],
                                       xt[:, c * NT + ts * 512:c * NT + ts * 512 + 512],
                                       start=(c == 0), stop=(c == NDC - 1))
                    ins.then_inc(p.h["mm"], 1)
                p.emit("tensor", fk)
                mv = p.bump("mm")
                p.wait("vector", "mm", mv)
                if gi >= 4:
                    p.wait("vector", "ks%d" % slot, k_st[gi - 4])

                def fke(v, slot=slot, pp=pp):
                    v.tensor_copy(kst[:, slot * 512:(slot + 1) * 512],
                                  pp[:, :]).then_inc(p.h["ve"], 1)
                p.emit("vector", fke)
                k_ve[gi] = p.bump("ve")
                p.wait("sync", "ve", k_ve[gi])
                k_st[gi] = p.dma("sync",
                                 bass.AP(K_hbm, (dq * 128) * HKN + 256 + ts * 512,
                                         [[HKN, 128], [1, 512]]),
                                 kst[:, slot * 512:(slot + 1) * 512], "ks%d" % slot)
                gi += 1

        p.wait("tensor", "ve", k_ve[gi - 1])
        p.wait("tensor", "sc", q_sc[len(q_sc) - 1])
        v_ve = {}
        v_st = {}
        vs_mm_marks = {}
        NTT = NT // 128
        for tt in range(NTT):
            pp2 = [psp[tt % 2], psp[2 + tt % 2]]
            if tt >= 2:
                p.wait("tensor", "ve", v_ve[tt - 2])

            def fv(t, tt=tt, pp2=pp2):
                ins = None
                for sl in range(2):
                    for c in range(NDC):
                        ins = t.matmul(pp2[sl][:, :],
                                       xt[:, c * NT + tt * 128:c * NT + (tt + 1) * 128],
                                       WVs[:, c * D + sl * 512:c * D + sl * 512 + 512],
                                       start=(c == 0), stop=(c == NDC - 1))
                ins.then_inc(p.h["mm"], 1)
            p.emit("tensor", fv)
            mv = p.bump("mm")
            p.wait("vector", "mm", mv)
            if tt >= 2:
                p.wait("vector", "vs%d" % (tt % 2), v_st[tt - 2])
                p.wait("vector", "mm", vs_mm_marks[tt - 2])

            def fve(v, tt=tt, pp2=pp2):
                v.tensor_copy(vst[:, (tt % 2) * D:(tt % 2) * D + 512], pp2[0][:, :])
                v.tensor_copy(vst[:, (tt % 2) * D + 512:(tt % 2) * D + 1024],
                              pp2[1][:, :]).then_inc(p.h["ve"], 1)
            p.emit("vector", fve)
            v_ve[tt] = p.bump("ve")
            p.wait("sync", "ve", v_ve[tt])
            v_st[tt] = p.dma("sync",
                             bass.AP(V_hbm, (256 + tt * 128) * D, [[D, 128], [1, D]]),
                             vst[:, (tt % 2) * D:(tt % 2) * D + D], "vs%d" % (tt % 2))
            p.wait("tensor", "ve", v_ve[tt])

            def fvs(t, tt=tt):
                ins = None
                for sl in range(2):
                    ins = t.matmul(psv[sl][:, :], self.ones_col[:, 0:1],
                                   vst[:, (tt % 2) * D + sl * 512:(tt % 2) * D + (sl + 1) * 512],
                                   start=(tt == 0), stop=(tt == NTT - 1))
                ins.then_inc(p.h["mm"], 1)
            p.emit("tensor", fvs)
            vs_mm_marks[tt] = p.bump("mm")
        p.wait("vector", "mm", vs_mm_marks[NTT - 1])

        def fvsum(v):
            v.tensor_copy(vsum[0:1, 0:512], psv[0][0:1, :])
            v.tensor_copy(vsum[0:1, 512:1024], psv[1][0:1, :]).then_inc(p.h["ve"], 1)
        p.emit("vector", fvsum)
        vsum_ve = p.bump("ve")

        for x in range(4):
            p.wait("gpsimd", "ks%d" % x, p.mark("ks%d" % x))
        for x in range(2):
            p.wait("gpsimd", "vs%d" % x, p.mark("vs%d" % x))
        p.wait("gpsimd", "ve", vsum_ve)
        for c in range(NDC):
            p.dma("gpsimd", kx_in.ap()[c, :, 0:256],
                  bass.AP(K_hbm, (c * 128) * HKN + 256, [[HKN, 128], [1, 256]]), "a0x")
            p.dma("gpsimd", kx_in.ap()[c, :, 256:512],
                  bass.AP(K_hbm, (c * 128) * HKN + 4096, [[HKN, 128], [1, 256]]), "a0x")
        p.dma("gpsimd", vx_in.ap()[0:256, :],
              bass.AP(V_hbm, 256 * D, [[D, 256], [1, D]]), "a0x")
        p.dma("gpsimd", vx_in.ap()[256:512, :],
              bass.AP(V_hbm, 4096 * D, [[D, 256], [1, D]]), "a0x")
        xm = p.dma("gpsimd", vs_in.ap()[:, :], vsum[0:1, :], "a0x")
        p.wait("gpsimd", "a0x", xm)
        rg = [[2 * g, 2 * g + 1] for g in range(4)]

        def fcc(g):
            h = p.h["cc"]
            g.collective_compute("AllGather", ALU.bypass, replica_groups=rg,
                                 ins=[kx_in.ap()], outs=[kx_out.ap()]).then_inc(h, 1)
            g.collective_compute("AllGather", ALU.bypass, replica_groups=rg,
                                 ins=[vx_in.ap()], outs=[vx_out.ap()]).then_inc(h, 1)
            g.collective_compute("AllReduce", ALU.add, replica_groups=rg,
                                 ins=[vs_in.ap()], outs=[vs_out.ap()]).then_inc(h, 1)
        p.emit("gpsimd", fcc)
        cc_done = p.bump("cc", 3)

        self.pop_to(scpW)
        hst = self.sb("a0_hst", [128, NDC * 256], BF16)
        vhst = self.sb("a0_vhst", [128, 2 * D], BF16)
        vsg = self.sb("a0_vsg", [1, D], F32)
        psh = self.ps("a0_psh", [128, 512])
        p.wait("sync", "cc", cc_done)
        for side in range(2):
            smark = p.mark("a0h1")
            hl = 0
            for c in range(NDC):
                hl = p.dma("sync", hst[:, c * 256:(c + 1) * 256],
                           kx_out.ap()[side, c, :, (1 - side) * 256:(1 - side) * 256 + 256], "a0h0")
            p.wait("vector", "a0h0", hl)
            p.wait("vector", "ve", hf_ve)
            p.wait("vector", "a0h1", smark)
            hv = 0
            for c in range(NDC):
                def fh(v, c=c, side=side):
                    v.tensor_scalar(hst[:, c * 256:(c + 1) * 256],
                                    hst[:, c * 256:(c + 1) * 256],
                                    hfc[:, side:side + 1], None,
                                    ALU.mult).then_inc(p.h["ve"], 1)
                p.emit("vector", fh)
                hv = p.bump("ve")
            p.wait("sync", "ve", hv)
            base = 0 if side == 0 else 4352
            for c in range(NDC):
                p.dma("sync", bass.AP(K_hbm, (c * 128) * HKN + base, [[HKN, 128], [1, 256]]),
                      hst[:, c * 256:(c + 1) * 256], "a0h1")
            smark = p.mark("a0h1")
            lds = 0
            for r2 in range(2):
                lds = p.dma("sync", vhst[:, r2 * D:(r2 + 1) * D],
                            vx_out.ap()[side, (1 - side) * 256 + r2 * 128:(1 - side) * 256 + (r2 + 1) * 128, :], "a0h0")
            p.wait("vector", "a0h0", lds)
            p.wait("vector", "a0h1", smark)
            vvm = 0
            for r2 in range(2):
                def fvh(v, r2=r2, side=side):
                    v.tensor_scalar(vhst[:, r2 * D:(r2 + 1) * D],
                                    vhst[:, r2 * D:(r2 + 1) * D],
                                    hfc[:, side:side + 1], None,
                                    ALU.mult).then_inc(p.h["ve"], 1)
                p.emit("vector", fvh)
                vvm = p.bump("ve")
            p.wait("sync", "ve", vvm)
            for r2 in range(2):
                p.dma("sync", bass.AP(V_hbm, (base + r2 * 128) * D, [[D, 128], [1, D]]),
                      vhst[:, r2 * D:(r2 + 1) * D], "a0h1")
        vg = p.dma("sync", vsg[:, :], vs_out.ap()[:, :], "a0h0")
        self.emit_bcast(vsg[0:1, :], Vb[:, :], D, psh,
                        extra_tensor_wait=("a0h0", vg))
        vb_ve = p.mark("ve")

        self.pop_to(scp)

        halo_done = p.mark("a0h1")
        KW = self.sb("a0_KW", [128, 2 * NDC * 640], BF16)
        VW = self.sb("a0_VW", [128, 2 * 5 * D], BF16)
        PW = self.sb("a0_PW", [128, 640], F32)
        WT = self.sb("a0_WT", [128, 640], BF16)
        WTT = self.sb("a0_WTT", [128, 5 * 128], BF16)
        TMP = self.sb("a0_TMP", [128, D], F32)
        OUT = self.sb("a0_OUT", [128, D], BF16)
        asc = self.sb("a0_asc", [128, 6], F32)
        xon = self.sb("a0_xo", [128, 4 * 128], F32)
        xnn = self.sb("a0_xn", [128, 4 * 128], F32)
        ps_s = self.ps("a0_ps_s", [128, 640])
        ps_o = [self.ps("a0_ps_o%d" % k2, [128, D]) for k2 in range(2)]
        ps_t = [self.ps("a0_ps_t%d" % k2, [128, 128], BF16) for k2 in range(2)]

        NQT = NT // 128
        kw_m = {}
        vw_m = {}
        out_mm = {}
        sc_rel = {}
        ev_rel = {}
        xo_m = {}
        st_m = {}
        xa_rel = {}
        pst_rel = [0, 0]

        def emit_win_load(qt):
            slot = qt % 2
            if qt >= 2:
                p.wait("gpsimd", "mm", out_mm[qt - 2])
            w0 = qt * 128
            v = 0
            for c in range(NDC):
                v = p.dma("gpsimd",
                          KW[:, slot * NDC * 640 + c * 640:slot * NDC * 640 + (c + 1) * 640],
                          bass.AP(K_hbm, (c * 128) * HKN + w0, [[HKN, 128], [1, 640]]),
                          "kw%d" % slot)
            kw_m[qt] = v
            v = 0
            for kt in range(5):
                v = p.dma("gpsimd",
                          VW[:, slot * 5 * D + kt * D:slot * 5 * D + (kt + 1) * D],
                          bass.AP(V_hbm, (w0 + kt * 128) * D, [[D, 128], [1, D]]),
                          "vw%d" % slot)
            vw_m[qt] = v

        p.wait("gpsimd", "a0h1", halo_done)
        p.wait("tensor", "ve", vb_ve)
        emit_win_load(0)
        emit_win_load(1)

        def vop(fn):
            def g2(v, fn=fn):
                fn(v).then_inc(p.h["ve"], 1)
            p.emit("vector", g2)
            vv = p.bump("ve")
            p.wait("vector", "ve", vv)
            return vv

        for qt in range(NQT):
            slot = qt % 2
            p.waits("tensor", **{("kw%d" % slot): kw_m[qt]})
            if qt >= 1:
                p.wait("tensor", "sc", sc_rel[qt - 1])

            def fsc(t, slot=slot, qt=qt):
                ins = None
                for c in range(NDC):
                    base = slot * NDC * 640 + c * 640
                    t.matmul(ps_s[:, 0:512],
                             Q_fm[:, c * NT + qt * 128:c * NT + (qt + 1) * 128],
                             KW[:, base:base + 512],
                             start=(c == 0), stop=(c == NDC - 1))
                for c in range(NDC):
                    base = slot * NDC * 640 + c * 640
                    ins = t.matmul(ps_s[:, 512:640],
                                   Q_fm[:, c * NT + qt * 128:c * NT + (qt + 1) * 128],
                                   KW[:, base + 512:base + 640],
                                   start=(c == 0), stop=(c == NDC - 1))
                ins.then_inc(p.h["mm"], 1)
            p.emit("tensor", fsc)
            sc_mm = p.bump("mm")

            p.wait("vector", "mm", sc_mm)
            vop(lambda v: v.tensor_reduce(asc[:, 0:1], ps_s[:, :], AXL.X, ALU.max,
                                          negate=True))
            m1n_ve = vop(lambda v: v.tensor_scalar_min(asc[:, 0:1], asc[:, 0:1], 0.0))
            p.wait("scalar", "ve", m1n_ve)

            def fexp(s2):
                s2.activation(asc[:, 1:2], asc[:, 0:1], AF.Exp)
                s2.activation(PW[:, :], ps_s[:, :], AF.Exp,
                              bias=asc[:, 0:1]).then_inc(p.h["sc"], 1)
            p.emit("scalar", fexp)
            sc_rel[qt] = p.bump("sc")
            p.wait("vector", "sc", sc_rel[qt])
            p.wait("vector", "mm", getattr(self, "_a0_last_wtr", 0))
            wt_ve = vop(lambda v: v.scalar_tensor_tensor(WT[:, :], PW[:, :], asc[:, 1:2],
                                                 self.bandm[:, :], ALU.subtract, ALU.mult,
                                                 accum_out=asc[:, 2:3]))
            vop(lambda v: v.scalar_tensor_tensor(asc[:, 3:4], asc[:, 1:2], float(SEQ),
                                                 asc[:, 2:3], ALU.mult, ALU.add))
            vop(lambda v: v.reciprocal(asc[:, 4:5], asc[:, 3:4]))

            p.wait("tensor", "ve", wt_ve)
            for kt in range(5):
                p.wait("tensor", "ve", pst_rel[kt % 2])

                def ftr(t, kt=kt):
                    t.transpose(ps_t[kt % 2][:, :], WT[:, kt * 128:(kt + 1) * 128],
                                self.ident_bf[:, :]).then_inc(p.h["mm"], 1)
                p.emit("tensor", ftr)
                mtv = p.bump("mm")
                p.wait("vector", "mm", mtv)

                def fwc(v, kt=kt):
                    v.tensor_copy(WTT[:, kt * 128:(kt + 1) * 128],
                                  ps_t[kt % 2][:, :]).then_inc(p.h["ve"], 1)
                p.emit("vector", fwc)
                pst_rel[kt % 2] = p.bump("ve")
                if kt == 4:
                    self._a0_last_wtr = mtv

            p.waits("tensor", **{("vw%d" % slot): vw_m[qt]})
            p.wait("tensor", "ve", pst_rel[0])
            p.wait("tensor", "ve", pst_rel[1])
            if qt >= 2:
                p.wait("tensor", "ve", ev_rel[qt - 2])
            po = ps_o[qt % 2]

            def fo(t, slot=slot, po=po):
                ins = None
                for kt in range(5):
                    for sl in range(2):
                        ins = t.matmul(po[:, sl * 512:(sl + 1) * 512],
                                       WTT[:, kt * 128:(kt + 1) * 128],
                                       VW[:, slot * 5 * D + kt * D + sl * 512:slot * 5 * D + kt * D + (sl + 1) * 512],
                                       start=(kt == 0), stop=(kt == 4))
                ins.then_inc(p.h["mm"], 1)
            p.emit("tensor", fo)
            out_mm[qt] = p.bump("mm")
            if qt + 2 < NQT:
                emit_win_load(qt + 2)

            p.wait("vector", "mm", out_mm[qt])
            p.wait("vector", "ve", vb_ve)
            p.wait("vector", "mm", getattr(self, "_a0_last_ot", 0))
            vop(lambda v, po=po: v.scalar_tensor_tensor(TMP[:, :], Vb[:, :], asc[:, 1:2],
                                                        po[:, :], ALU.mult, ALU.add))
            ev_rel[qt] = vop(lambda v: v.tensor_scalar(OUT[:, :], TMP[:, :], asc[:, 4:5],
                                                       None, ALU.mult))

            for dc in range(NDC):
                idx = qt * NDC + dc
                sl4 = idx % 4
                if idx >= 4:
                    p.wait("sync", "ve", xa_rel[idx - 4])
                xo_m[idx] = p.dma("sync", xon[:, sl4 * 128:(sl4 + 1) * 128],
                                  src.ap()[dc * 128:(dc + 1) * 128, qt * 128:(qt + 1) * 128],
                                  "xoa%d" % sl4)
                p.wait("tensor", "ve", pst_rel[dc % 2])
                p.wait("tensor", "ve", ev_rel[qt])

                def fot(t, dc=dc):
                    t.transpose(ps_t[dc % 2][:, :], OUT[:, dc * 128:(dc + 1) * 128],
                                self.ident_bf[:, :]).then_inc(p.h["mm"], 1)
                p.emit("tensor", fot)
                mo = p.bump("mm")
                p.wait("vector", "mm", mo)
                p.wait("vector", "xoa%d" % sl4, xo_m[idx])
                if idx >= 4:
                    p.wait("vector", "sna%d" % sl4, st_m[idx - 4])

                def fxa(v, dc=dc, sl4=sl4):
                    v.tensor_add(xnn[:, sl4 * 128:(sl4 + 1) * 128],
                                 ps_t[dc % 2][:, :],
                                 xon[:, sl4 * 128:(sl4 + 1) * 128]).then_inc(p.h["ve"], 1)
                p.emit("vector", fxa)
                xa_rel[idx] = p.bump("ve")
                pst_rel[dc % 2] = xa_rel[idx]
                p.wait("sync", "ve", xa_rel[idx])
                st_m[idx] = p.dma("sync",
                                  dst.ap()[dc * 128:(dc + 1) * 128, qt * 128:(qt + 1) * 128],
                                  xnn[:, sl4 * 128:(sl4 + 1) * 128], "sna%d" % sl4)
                if dc == NDC - 1:
                    self._a0_last_ot = mo
                if dc == NDC - 1:
                    self._a1_last_ot = mo

    _BKT_TH = [1, 2, 3, 4, 5, 6, 7, 8, 9, 10, 11, 12, 13, 14, 15, 16, 19, 21, 24, 27, 31, 35, 40, 46, 52, 59, 67, 77, 87, 99, 113]
    _BKT_BJ = [1, 2, 3, 4, 5, 6, 7, 8, 9, 10, 11, 12, 13, 14, 15, 16, 17, 18, 19, 20, 21, 22, 23, 24, 25, 26, 27, 28, 29, 30, 31]
    _BKT_BP = [0, 1, 2, 3, 4, 5, 6, 7, 8, 9, 10, 11, 12, 13, 14, 15, 16, 17, 18, 19, 20, 21, 22, 23, 24, 25, 26, 27, 28, 29, 30]

    def _attn1(self, src, dst):
        p = self.p
        i = self.inputs
        nc = self.nc
        MU = math.sqrt(0.5)
        SIG = math.sqrt(0.25 * math.pi)
        ERF_SC = 1.0 / (SIG * math.sqrt(2.0))
        ERF_B = -MU / (SIG * math.sqrt(2.0))

        kx_in = nc.dram_tensor(self._un("a1_kxi"), [128, NT], BF16, kind="Internal")
        kx_out = nc.dram_tensor(self._un("a1_kxo"), [2, 128, NT], BF16, kind="Internal")
        vx_in = nc.dram_tensor(self._un("a1_vxi"), [NT, D], BF16, kind="Internal")
        vx_out = nc.dram_tensor(self._un("a1_vxo"), [2, NT, D], BF16, kind="Internal")

        for snm in (["a1wl", "a1x", "a1kv", "a1h"]
                    + ["xoa%d" % x for x in range(4)] + ["sna%d" % x for x in range(4)]
                    + ["vs%d" % x for x in range(2)] + ["ks%d" % x for x in range(2)]):
            p.sem(snm)

        Q_fm = self.sb("a1_q", [128, NT], BF16)
        K_all = self.sb("a1_k", [128, SEQ], BF16)
        BANK = self.sb("a1_bank", [128, 6 * 128], F32)
        erfb = self.sb("a1_erfb", [128, 1], F32)
        hfc = self.sb("a1_hf", [128, 2], F32)

        scp = self.scope()
        xt, xt_ve = self.emit_xt_full(src)
        wqks = self.sb("a1_wqk", [128, NDC * 128], BF16)
        wvs = self.sb("a1_wv", [128, NDC * D], BF16)
        gbc = self.sb("a1_gbc", [128, 4], F32)
        bqkc = self.sb("a1_bqk", [128, 1], F32)
        bvr = self.sb("a1_bvr", [1, D], BF16)
        qkst = self.sb("a1_qkst", [128, 2 * 512], F32)
        kstg = self.sb("a1_kstg", [128, 2 * 512], BF16)
        vst = self.sb("a1_vst", [128, 2 * D], BF16)
        psp = [self.ps("a1_pp%d" % k2, [128, 512]) for k2 in range(4)]

        p.waits("gpsimd", mm=p.mark("mm"), sc=p.mark("sc"), ve=p.mark("ve"))
        p.dma("gpsimd", wqks[:, :], i["wqkr"].ap()[:, :, :], "a1wl")
        p.dma("gpsimd", wvs[:, :], i["wvr"].ap()[:, :, :], "a1wl")
        p.dma("gpsimd", gbc[:, :], i["gbcols"].ap()[:, :], "a1wl")
        p.dma("gpsimd", bqkc[:, :], i["bqk"].ap()[:, :], "a1wl")
        lw = p.dma("gpsimd", bvr[:, :], i["bvrow"].ap()[:, :], "a1wl")
        p.waits("tensor", a1wl=lw, ve=xt_ve, g2=self.init_marks["g2"],
                ld=self.init_marks["ld"], ldg=self.init_marks["ldg"])
        p.waits("scalar", a1wl=lw)
        p.waits("vector", a1wl=lw)

        hh1 = self.emit_bcast(self.hf_c[0:1, 0:1], hfc[:, 0:1], 1, psp[0])
        p.wait("vector", "ve", hh1)

        def fihf(v):
            v.tensor_scalar(hfc[:, 1:2], hfc[:, 0:1], -1.0, 1.0,
                            ALU.mult, ALU.add).then_inc(p.h["ve"], 1)
        p.emit("vector", fihf)
        p.bump("ve")

        def fmub(g):
            g.memset(erfb[:, :], ERF_B).then_inc(p.h["g2"], 1)
        p.emit("gpsimd", fmub)
        mub_g2 = p.bump("g2")

        qk_sc = {}
        qk_ve = {}
        k_st = {}
        for ts in range(NT // 512):
            pp = psp[ts % 2]
            slot = ts % 2
            if ts >= 2:
                p.wait("tensor", "ve", qk_ve[ts - 2])
            elif ts == 0:
                p.wait("tensor", "ve", hh1)

            def fqk(t, ts=ts, pp=pp):
                ins = None
                for c in range(NDC):
                    ins = t.matmul(pp[:, :],
                                   wqks[:, c * 128:(c + 1) * 128],
                                   xt[:, c * NT + ts * 512:c * NT + ts * 512 + 512],
                                   start=(c == 0), stop=(c == NDC - 1))
                ins.then_inc(p.h["mm"], 1)
            p.emit("tensor", fqk)
            mv = p.bump("mm")
            p.wait("scalar", "mm", mv)
            if ts >= 2:
                p.wait("scalar", "ve", qk_ve[ts - 2])

            def fqke(s2, slot=slot, pp=pp):
                s2.activation(qkst[:, slot * 512:(slot + 1) * 512], pp[:, :],
                              AF.Silu, bias=bqkc[:, 0:1]).then_inc(p.h["sc"], 1)
            p.emit("scalar", fqke)
            qk_sc[ts] = p.bump("sc")
            p.wait("vector", "sc", qk_sc[ts])
            if ts >= 2:
                p.wait("vector", "ks%d" % slot, k_st[ts - 2])

            def fqv(v, ts=ts, slot=slot):
                v.tensor_scalar(Q_fm[:, ts * 512:(ts + 1) * 512],
                                qkst[:, slot * 512:(slot + 1) * 512],
                                gbc[:, 0:1], gbc[:, 1:2], ALU.mult, ALU.add)
                v.tensor_scalar(kstg[:, slot * 512:(slot + 1) * 512],
                                qkst[:, slot * 512:(slot + 1) * 512],
                                gbc[:, 2:3], gbc[:, 3:4], ALU.mult,
                                ALU.add).then_inc(p.h["ve"], 1)
            p.emit("vector", fqv)
            qk_ve[ts] = p.bump("ve")
            p.wait("sync", "ve", qk_ve[ts])
            k_st[ts] = p.dma("sync", kx_in.ap()[:, ts * 512:(ts + 1) * 512],
                             kstg[:, slot * 512:(slot + 1) * 512], "ks%d" % slot)

        v_ve = {}
        v_st = {}
        NTT = NT // 128
        for tt in range(NTT):
            pp2 = [psp[2], psp[3]] if tt % 2 == 0 else [psp[0], psp[1]]
            if tt >= 2:
                p.wait("tensor", "sc", v_ve[tt - 2])
            elif tt == 1:
                p.wait("tensor", "ve", qk_ve[NT // 512 - 1])

            def fv(t, tt=tt, pp2=pp2):
                ins = None
                for sl in range(2):
                    for c in range(NDC):
                        t.matmul(pp2[sl][:, :],
                                 xt[:, c * NT + tt * 128:c * NT + (tt + 1) * 128],
                                 wvs[:, c * D + sl * 512:c * D + sl * 512 + 512],
                                 start=(c == 0), stop=False)
                    ins = t.matmul(pp2[sl][:, :], self.ones_row[0:1, 0:128],
                                   bvr[0:1, sl * 512:(sl + 1) * 512],
                                   start=False, stop=True)
                ins.then_inc(p.h["mm"], 1)
            p.emit("tensor", fv)
            mv = p.bump("mm")
            p.wait("scalar", "mm", mv)
            if tt >= 2:
                p.wait("scalar", "vs%d" % (tt % 2), v_st[tt - 2])

            def fve(s2, tt=tt, pp2=pp2):
                s2.activation(vst[:, (tt % 2) * D:(tt % 2) * D + 512],
                              pp2[0][:, :], AF.Silu)
                s2.activation(vst[:, (tt % 2) * D + 512:(tt % 2) * D + 1024],
                              pp2[1][:, :], AF.Silu).then_inc(p.h["sc"], 1)
            p.emit("scalar", fve)
            v_ve[tt] = p.bump("sc")
            p.wait("sync", "sc", v_ve[tt])
            v_st[tt] = p.dma("sync",
                             bass.AP(vx_in, (tt * 128) * D, [[D, 128], [1, D]]),
                             vst[:, (tt % 2) * D:(tt % 2) * D + D], "vs%d" % (tt % 2))


        for x in range(2):
            p.wait("gpsimd", "ks%d" % x, p.mark("ks%d" % x))
            p.wait("gpsimd", "vs%d" % x, p.mark("vs%d" % x))
        rg = [[2 * g, 2 * g + 1] for g in range(4)]

        def fcc(g):
            h = p.h["cc"]
            g.collective_compute("AllGather", ALU.bypass, replica_groups=rg,
                                 ins=[kx_in.ap()], outs=[kx_out.ap()]).then_inc(h, 1)
            g.collective_compute("AllGather", ALU.bypass, replica_groups=rg,
                                 ins=[vx_in.ap()], outs=[vx_out.ap()]).then_inc(h, 1)
        p.emit("gpsimd", fcc)
        cc_done = p.bump("cc", 2)

        self.pop_to(scp)
        p.wait("sync", "cc", cc_done)
        ka = p.dma("sync", K_all[:, 0:NT], kx_out.ap()[0, :, :], "a1h")
        ka = p.dma("sync", K_all[:, NT:SEQ], kx_out.ap()[1, :, :], "a1h")
        V_all = self.sb("a1_vall", [128, (SEQ // 128) * D], BF16)
        va = p.dma("sync", V_all[:, :],
                   bass.AP(vx_out, 0, [[D, 128], [128 * D, SEQ // 128], [1, D]]), "a1h")

        scbk = self.scope()
        pspb = [self.ps("a1_pb%d" % k2, [128, 512]) for k2 in range(2)]
        relsc = self.sb("a1_rel", [1, 3 * NUM_BUCKETS], F32)
        drow = self.sb("a1_drow", [1, NUM_BUCKETS + 2], F32)
        dcols = self.sb("a1_dcols", [128, NUM_BUCKETS + 2], F32)
        dtile = self.sb("a1_dt", [128, 2 * 128], F32)
        base6 = self.sb("a1_b6", [128, 6 * 128], F32)
        ohj = self.sb("a1_ohj", [NUM_BUCKETS, 2 * NUM_BUCKETS], F32)

        ldr = p.dma("sync", relsc[0:1, 0:NUM_BUCKETS], i["relb"].ap()[:, :], "a1h")
        ldo = p.dma("sync", ohj[:, :], self.bktoh_c.ap()[:, :], "a1h")
        relc = self.sb("a1_relc", [NUM_BUCKETS, 1], F32)
        ldc = p.dma("sync", relc[:, :], bass.AP(i["relb"], 0, [[1, NUM_BUCKETS], [1, 1]]), "a1h")

        NJ2 = len(self._BKT_TH)
        p.wait("tensor", "a1h", ldc)
        pb = pspb[0]

        def fga(t):
            t.matmul(pb[0:1, 0:2 * NUM_BUCKETS], relc[:, 0:1], ohj[:, :],
                     start=True, stop=True).then_inc(p.h["mm"], 1)
        p.emit("tensor", fga)
        mga = p.bump("mm")
        p.wait("vector", "mm", mga)
        p.wait("vector", "a1h", ldc)

        def fgb(v):
            v.tensor_copy(relsc[0:1, NUM_BUCKETS:3 * NUM_BUCKETS],
                          pb[0:1, 0:2 * NUM_BUCKETS]).then_inc(p.h["ve"], 1)
        p.emit("vector", fgb)
        vv = p.bump("ve")
        p.wait("vector", "ve", vv)
        SQ128 = float(math.sqrt(128.0))

        def fdr1(v):
            v.tensor_sub(drow[0:1, 0:NJ2], relsc[0:1, NUM_BUCKETS:NUM_BUCKETS + NJ2],
                         relsc[0:1, 2 * NUM_BUCKETS:2 * NUM_BUCKETS + NJ2]).then_inc(p.h["ve"], 1)
        p.emit("vector", fdr1)
        vv = p.bump("ve")
        p.wait("vector", "ve", vv)

        def fdr2(v):
            v.tensor_scalar(drow[0:1, 0:NJ2], drow[0:1, 0:NJ2], SQ128, None,
                            ALU.mult).then_inc(p.h["ve"], 1)
        p.emit("vector", fdr2)
        vv = p.bump("ve")
        p.wait("vector", "ve", vv)

        def fb0(v):
            v.tensor_scalar(drow[0:1, NJ2:NJ2 + 1], relsc[0:1, 0:1], SQ128, None, ALU.mult)
            v.tensor_scalar(drow[0:1, NJ2 + 1:NJ2 + 2],
                            relsc[0:1, NUM_BUCKETS - 1:NUM_BUCKETS], SQ128, None,
                            ALU.mult).then_inc(p.h["ve"], 1)
        p.emit("vector", fb0)
        dr_ve = p.bump("ve")
        NJx = NJ2 + 2
        bc_ve = self.emit_bcast(drow[0:1, 0:NJx], dcols[:, 0:NJx],
                                NJx, pspb[1], extra_tensor_wait=("ve", dr_ve))

        def fio(g):
            h = p.h["g2"]
            g.iota(dtile[:, 0:128], [[1, 128]], base=0, channel_multiplier=-1,
                   allow_small_or_imprecise_dtypes=True).then_inc(h, 1)
            g.iota(dtile[:, 128:256], [[1, 128]], base=128, channel_multiplier=-1,
                   allow_small_or_imprecise_dtypes=True).then_inc(h, 1)
        p.emit("gpsimd", fio)
        io_g2 = p.bump("g2", 2)

        p.wait("vector", "g2", io_g2)
        p.wait("vector", "ve", bc_ve)
        for w2 in range(2):
            def finit(v, w2=w2):
                v.tensor_scalar(base6[:, w2 * 128:(w2 + 1) * 128],
                                dtile[:, w2 * 128:(w2 + 1) * 128], -1e30,
                                dcols[:, NJ2:NJ2 + 1], ALU.is_ge,
                                ALU.mult).then_inc(p.h["ve"], 1)
            p.emit("vector", finit)
            vv = p.bump("ve")
            p.wait("vector", "ve", vv)
            for j in range(NJ2):
                def facc1(v, w2=w2, j=j):
                    v.tensor_scalar(base6[:, 512 + 128 * w2:512 + 128 * (w2 + 1)],
                                    dtile[:, w2 * 128:(w2 + 1) * 128],
                                    float(self._BKT_TH[j]), dcols[:, j:j + 1],
                                    ALU.is_ge, ALU.mult).then_inc(p.h["ve"], 1)
                p.emit("vector", facc1)
                vv = p.bump("ve")
                p.wait("vector", "ve", vv)

                def facc2(v, w2=w2):
                    v.tensor_add(base6[:, w2 * 128:(w2 + 1) * 128],
                                 base6[:, w2 * 128:(w2 + 1) * 128],
                                 base6[:, 512 + 128 * w2:512 + 128 * (w2 + 1)]).then_inc(p.h["ve"], 1)
                p.emit("vector", facc2)
                vv = p.bump("ve")
                p.wait("vector", "ve", vv)
        def fca1(v):
            v.tensor_scalar(base6[:, 512:640], dtile[:, 0:128], 0.0, BIG_NEG,
                            ALU.is_lt, ALU.mult).then_inc(p.h["ve"], 1)
        p.emit("vector", fca1)
        vv = p.bump("ve")
        p.wait("vector", "ve", vv)

        def fca2(v):
            v.tensor_add(base6[:, 0:128], base6[:, 0:128],
                         base6[:, 512:640]).then_inc(p.h["ve"], 1)
        p.emit("vector", fca2)
        vv = p.bump("ve")
        p.wait("vector", "ve", vv)

        def fct(v):
            v.tensor_scalar(base6[:, 256:384], self.ident[:, :], 0.0,
                            dcols[:, NJ2 + 1:NJ2 + 2], ALU.mult, ALU.add).then_inc(p.h["ve"], 1)
        p.emit("vector", fct)
        vv = p.bump("ve")

        def fbn(g):
            g.memset(base6[:, 384:512], BIG_NEG).then_inc(p.h["g2"], 1)
        p.emit("gpsimd", fbn)
        bn_g2 = p.bump("g2")
        p.wait("vector", "g2", bn_g2)
        p.wait("vector", "ve", vv)

        pairs = [(3, 0), (3, 1), (3, 2), (0, 2), (1, 2), (2, 2)]
        DIAG0, C1280, CONST0, BIGN0 = 0, 128, 256, 384
        offs = [DIAG0, C1280, CONST0, BIGN0]
        for si, (a0, b0) in enumerate(pairs):
            def fsel1(v, si=si, a0=a0):
                v.tensor_scalar(BANK[:, si * 128:(si + 1) * 128],
                                base6[:, offs[a0]:offs[a0] + 128],
                                hfc[:, 1:2], None, ALU.mult).then_inc(p.h["ve"], 1)
            p.emit("vector", fsel1)
            vv = p.bump("ve")
            p.wait("vector", "ve", vv)

            def fsel2(v, b0=b0):
                v.tensor_scalar(base6[:, 640:768],
                                base6[:, offs[b0]:offs[b0] + 128],
                                hfc[:, 0:1], None, ALU.mult).then_inc(p.h["ve"], 1)
            p.emit("vector", fsel2)
            vv = p.bump("ve")
            p.wait("vector", "ve", vv)

            def fsel3(v, si=si):
                v.tensor_add(BANK[:, si * 128:(si + 1) * 128],
                             BANK[:, si * 128:(si + 1) * 128],
                             base6[:, 640:768]).then_inc(p.h["ve"], 1)
            p.emit("vector", fsel3)
            vv = p.bump("ve")
            p.wait("vector", "ve", vv)
        bank_ve = p.mark("ve")

        self.pop_to(scbk)

        SIMF = self.sb("a1_simf", [128, 4 * 128], F32)
        PSI = self.sb("a1_psi", [128, 4 * 128], F32)
        WTB = self.sb("a1_wtb", [128, 4 * 128], BF16)
        OUT = self.sb("a1_out", [128, D], BF16)
        xon = self.sb("a1_xo", [128, 4 * 128], F32)
        xnn = self.sb("a1_xn", [128, 4 * 128], F32)
        ps_sim = [self.ps("a1_ps_s%d" % k2, [128, 128]) for k2 in range(2)]
        ps_o = [self.ps("a1_ps_o%d" % k2, [128, D]) for k2 in range(2)]
        ps_t = [self.ps("a1_ps_t%d" % k2, [128, 128], BF16) for k2 in range(2)]

        p.waits("tensor", a1h=p.mark("a1h"), ve=bank_ve, g2=p.mark("g2"))
        p.waits("scalar", g2=mub_g2)

        NQT = NT // 128
        LA = 3
        xo_m = {}
        st_m = {}
        xa_rel = {}
        pst_rel = [0, 0]
        ev_rel = {}

        def vop(fn):
            def g2f(v, fn=fn):
                fn(v).then_inc(p.h["ve"], 1)
            p.emit("vector", g2f)
            vv2 = p.bump("ve")
            return vv2

        sim_last = [0, 0]
        for qt in range(NQT):
            extent = 33 + qt
            po = ps_o[qt % 2]
            sim_rel = {}
            wtb_ve = {}
            psi_sc = {}

            def emit_sim(kt):
                r = extent - 1 - kt
                si = 0 if r == 0 else 1 if r == 1 else 2 if r <= 31 else \
                     3 if r == 32 else 4 if r == 33 else 5
                pss = ps_sim[kt % 2]
                p.wait("tensor", "ve", sim_rel[kt - 2] if kt >= 2 else sim_last[kt % 2])

                def fsim(t, kt=kt, pss=pss, qt=qt):
                    t.matmul(pss[:, :], K_all[:, kt * 128:(kt + 1) * 128],
                             Q_fm[:, qt * 128:(qt + 1) * 128],
                             start=True, stop=True).then_inc(p.h["mm"], 1)
                p.emit("tensor", fsim)
                mv2 = p.bump("mm")
                p.wait("vector", "mm", mv2)
                st4 = kt % 4
                if kt >= 4:
                    p.wait("vector", "sc", psi_sc[kt - 4])

                def fadd(v, st4=st4, si=si, pss=pss):
                    v.tensor_add(SIMF[:, st4 * 128:(st4 + 1) * 128], pss[:, :],
                                 BANK[:, si * 128:(si + 1) * 128]).then_inc(p.h["ve"], 1)
                p.emit("vector", fadd)
                sim_rel[kt] = p.bump("ve")
                sim_last[kt % 2] = sim_rel[kt]
                p.wait("scalar", "ve", sim_rel[kt])
                if kt >= 4:
                    p.wait("scalar", "ve", wtb_ve[kt - 4])

                def ferf(s2, st4=st4):
                    s2.activation(PSI[:, st4 * 128:(st4 + 1) * 128],
                                  SIMF[:, st4 * 128:(st4 + 1) * 128],
                                  getattr(AF, ERF_FN_NAME),
                                  bias=erfb[:, 0:1], scale=ERF_SC).then_inc(p.h["sc"], 1)
                p.emit("scalar", ferf)
                psi_sc[kt] = p.bump("sc")
                p.wait("vector", "sc", psi_sc[kt])
                if kt >= 4:
                    p.wait("vector", "mm", out_done[kt - 4])

                def fwtb(v, st4=st4):
                    v.tensor_scalar(WTB[:, st4 * 128:(st4 + 1) * 128],
                                    PSI[:, st4 * 128:(st4 + 1) * 128],
                                    0.5, 0.5, ALU.mult, ALU.add).then_inc(p.h["ve"], 1)
                p.emit("vector", fwtb)
                wtb_ve[kt] = p.bump("ve")

            out_done = {}

            def emit_out(kt):
                st4 = kt % 4
                p.wait("tensor", "ve", wtb_ve[kt])
                if kt == 0 and qt >= 2:
                    p.wait("tensor", "ve", ev_rel[qt - 2])

                def fo(t, kt=kt, st4=st4, po=po, extent=extent):
                    ins = None
                    for sl in range(2):
                        ins = t.matmul(po[:, sl * 512:(sl + 1) * 512],
                                       WTB[:, st4 * 128:(st4 + 1) * 128],
                                       V_all[:, kt * D + sl * 512:kt * D + (sl + 1) * 512],
                                       start=(kt == 0), stop=(kt == extent - 1))
                    ins.then_inc(p.h["mm"], 1)
                p.emit("tensor", fo)
                out_done[kt] = p.bump("mm")

            for j in range(extent + LA):
                if j < extent:
                    emit_sim(j)
                if j >= LA:
                    emit_out(j - LA)

            p.wait("vector", "mm", out_done[extent - 1])
            p.wait("vector", "mm", getattr(self, "_a1_last_ot", 0))
            ev_rel[qt] = vop(lambda v, po=po: v.tensor_copy(OUT[:, :], po[:, :]))
            p.wait("vector", "ve", ev_rel[qt])

            for dc in range(NDC):
                idx = qt * NDC + dc
                sl4 = idx % 4
                if idx >= 4:
                    p.wait("sync", "ve", xa_rel[idx - 4])
                xo_m[idx] = p.dma("sync", xon[:, sl4 * 128:(sl4 + 1) * 128],
                                  src.ap()[dc * 128:(dc + 1) * 128, qt * 128:(qt + 1) * 128],
                                  "xoa%d" % sl4)
                p.wait("tensor", "ve", pst_rel[dc % 2])
                p.wait("tensor", "ve", ev_rel[qt])

                def fot(t, dc=dc):
                    t.transpose(ps_t[dc % 2][:, :], OUT[:, dc * 128:(dc + 1) * 128],
                                self.ident_bf[:, :]).then_inc(p.h["mm"], 1)
                p.emit("tensor", fot)
                mo = p.bump("mm")
                p.wait("vector", "mm", mo)
                p.wait("vector", "xoa%d" % sl4, xo_m[idx])
                if idx >= 4:
                    p.wait("vector", "sna%d" % sl4, st_m[idx - 4])

                def fxa(v, dc=dc, sl4=sl4):
                    v.tensor_add(xnn[:, sl4 * 128:(sl4 + 1) * 128],
                                 ps_t[dc % 2][:, :],
                                 xon[:, sl4 * 128:(sl4 + 1) * 128]).then_inc(p.h["ve"], 1)
                p.emit("vector", fxa)
                xa_rel[idx] = p.bump("ve")
                pst_rel[dc % 2] = xa_rel[idx]
                p.wait("sync", "ve", xa_rel[idx])
                st_m[idx] = p.dma("sync",
                                  dst.ap()[dc * 128:(dc + 1) * 128, qt * 128:(qt + 1) * 128],
                                  xnn[:, sl4 * 128:(sl4 + 1) * 128], "sna%d" % sl4)
                if dc == NDC - 1:
                    self._a1_last_ot = mo

    def emit_ffn(self, layer, src, dst):
        for q in range(MK_QUARTERS if MK_QUARTERS else NQ):
            sc0 = self.scope()
            stat = self.sb("st_stat", [1, 4 * QN], F32)
            xt = self.sb("f_xt", [128, NDC * QN], BF16)
            sc1 = self.scope()
            SM, SA, st_sc = self.emit_stats(src, q * QN, QN, stat)
            self.pop_to(sc1)
            xt_ve = self.emit_xtilde(src, xt, q * QN, QN, stat, SM, SA, st_sc)
            self.pop_to(sc1)
            if MK_DBG and not self.dbg_done[0]:
                self.p.wait("gpsimd", "ve", xt_ve)
                self.p.dma("gpsimd", self.dbg_xt.ap()[:, :], xt[:, :], "wt")
                self.p.dma("gpsimd", self.dbg_stat.ap()[:, :], stat[0:1, :], "wt")
            self._ffn_quarter(layer, src, dst, q, xt, xt_ve)
            self.pop_to(sc0)
            self.join_all()

    def _ffn_quarter(self, layer, src, dst, q, xt, xt_ve):
        p = self.p
        i = self.inputs
        t0 = q * QN
        if layer == 0:
            NE, HC, NPE = L0_E, L0_HC, L0_E + 2
        else:
            NE, HC, NPE = L1_E, L1_HC, L1_E

        cwrow = self.sb("f_cwrow", [NE, QN], F32)
        cwfm = self.sb("f_cwfm", [NE, QN], BF16)
        gate_scope = self.scope()
        gwsb = self.sb("f_gw", [128, NDC, 16], BF16)
        gbsb = self.sb("f_gb", [1, NE], BF16)
        gsc = self.sb("f_gsc", [128, 8], F32)
        gst = self.sb("f_gst", [128, NE], F32)
        gtmp = self.sb("f_gtmp", [128, NE], F32)
        gtmp2 = self.sb("f_gtmp2", [128, NE], F32)
        gp = self.ps("f_gp", [128, NE])
        gtp = self.ps("f_gtp", [NE, 128])

        p.waits("gpsimd", mm=p.mark("mm"), sc=p.mark("sc"), ve=p.mark("ve"))
        p.sem("gld")
        gw_in = (i["gate_w"] if layer == 0 else i["Wgr"])
        lw = 0
        for c in range(NDC):
            lw = p.dma("gpsimd", gwsb[:, c, 0:NE], gw_in.ap()[:, c, :], "gld")
        lb = p.dma("gpsimd", gbsb[:, :],
                   (i["gate_b"] if layer == 0 else i["bg_1"]).ap()[:, :], "gld")
        p.waits("tensor", gld=lb, ve=xt_ve, g2=self.init_marks["g2"],
                ld=self.init_marks["ld"], ldg=self.init_marks["ldg"])

        NTT = QN // 128
        tile_done = {"ve": 0, "sc": 0}
        for tt in range(NTT):
            p.wait("tensor", "ve", tile_done["ve"])
            p.wait("tensor", "sc", tile_done["sc"])

            def fmm(t, tt=tt):
                for c in range(NDC):
                    t.matmul(gp[:, :], xt[:, c * QN + tt * 128:c * QN + (tt + 1) * 128],
                             gwsb[:, c, 0:NE], start=(c == 0), stop=False)
                t.matmul(gp[:, :], self.ones_row[0:1, 0:128], gbsb[0:1, :],
                         start=False, stop=True).then_inc(p.h["mm"], 1)
            p.emit("tensor", fmm)
            mv = p.bump("mm")

            p.wait("vector", "mm", mv)

            def fm1(v):
                v.tensor_reduce(gsc[:, 0:1], gp[:, :], AXL.X, ALU.max,
                                negate=True).then_inc(p.h["ve"], 1)
            p.emit("vector", fm1)
            v1 = p.bump("ve")

            if layer == 0:
                p.wait("scalar", "ve", v1)

                def fexp(s):
                    s.activation(gst[:, :], gp[:, :], AF.Exp, bias=gsc[:, 0:1],
                                 accum_out=gsc[:, 1:2]).then_inc(p.h["sc"], 1)
                p.emit("scalar", fexp)
                s1 = p.bump("sc")
                tile_done["sc"] = s1
                p.wait("vector", "sc", s1)

                def vop(fn):
                    def g(v, fn=fn):
                        fn(v).then_inc(p.h["ve"], 1)
                    p.emit("vector", g)
                    vv = p.bump("ve")
                    p.wait("vector", "ve", vv)
                    return vv

                vop(lambda v: v.reciprocal(gsc[:, 2:3], gsc[:, 1:2]))
                vop(lambda v: v.tensor_scalar(gst[:, :], gst[:, :], gsc[:, 2:3], None, ALU.mult))
                vop(lambda v: v.reduce_max(gsc[:, 3:4], gst[:, :], axis=AXL.X))
                vop(lambda v: v.tensor_scalar(gtmp[:, :], gst[:, :], gsc[:, 3:4], BIG_NEG,
                                              ALU.is_ge, ALU.mult))
                vop(lambda v: v.tensor_add(gtmp[:, :], gtmp[:, :], gst[:, :]))
                vop(lambda v: v.reduce_max(gsc[:, 4:5], gtmp[:, :], axis=AXL.X))
                vop(lambda v: v.tensor_scalar(gtmp2[:, :], gst[:, :], gsc[:, 4:5], None, ALU.is_ge))
                vc = vop(lambda v: v.tensor_mul(gtmp[:, :], gtmp2[:, :], gst[:, :]))
            else:
                p.wait("vector", "ve", v1)

                def vop(fn):
                    def g(v, fn=fn):
                        fn(v).then_inc(p.h["ve"], 1)
                    p.emit("vector", g)
                    vv = p.bump("ve")
                    p.wait("vector", "ve", vv)
                    return vv

                vop(lambda v: v.tensor_scalar(gtmp[:, :], gp[:, :], gsc[:, 0:1], None, ALU.add))
                vop(lambda v: v.tensor_scalar(gst[:, :], gtmp[:, :], 0.0, BIG_NEG, ALU.is_ge, ALU.mult))
                vop(lambda v: v.tensor_add(gst[:, :], gst[:, :], gtmp[:, :]))
                vs = vop(lambda v: v.tensor_reduce(gsc[:, 1:2], gst[:, :], AXL.X, ALU.max))
                p.wait("scalar", "ve", vs)

                def fden(s):
                    s.activation(gsc[:, 5:6], gsc[:, 1:2], AF.Exp).then_inc(p.h["sc"], 1)
                p.emit("scalar", fden)
                p.bump("sc")

                def fexp(s):
                    s.activation(gst[:, :], gtmp[:, :], AF.Exp).then_inc(p.h["sc"], 1)
                p.emit("scalar", fexp)
                se = p.bump("sc")
                tile_done["sc"] = se
                p.wait("vector", "sc", se)

                vop(lambda v: v.tensor_scalar(gsc[:, 6:7], gsc[:, 5:6], 1.0, None, ALU.add))
                vop(lambda v: v.reciprocal(gsc[:, 7:8], gsc[:, 6:7]))
                vop(lambda v: v.tensor_scalar(gtmp2[:, :], gtmp[:, :], gsc[:, 1:2], None, ALU.is_ge))
                vop(lambda v: v.tensor_mul(gtmp2[:, :], gtmp2[:, :], gst[:, :]))
                vc = vop(lambda v: v.tensor_scalar(gtmp2[:, :], gtmp2[:, :], gsc[:, 7:8], None, ALU.mult))

            p.wait("tensor", "ve", vc)
            trsrc = gtmp if layer == 0 else gtmp2

            def ftr(t, trsrc=trsrc):
                t.transpose(gtp[:, :], trsrc[:, :], self.ident[:, :]).then_inc(p.h["mm"], 1)
            p.emit("tensor", ftr)
            mt = p.bump("mm")
            p.wait("vector", "mm", mt)

            def fcpr(v, tt=tt):
                v.tensor_copy(cwrow[:, tt * 128:(tt + 1) * 128], gtp[0:NE, :])
                v.tensor_copy(cwfm[:, tt * 128:(tt + 1) * 128],
                              gtp[0:NE, :]).then_inc(p.h["ve"], 1)
            p.emit("vector", fcpr)
            tile_done["ve"] = p.bump("ve")
        gating_ve = p.mark("ve")
        if MK_DBG and not self.dbg_done[0]:
            p.wait("gpsimd", "ve", gating_ve)
            p.dma("gpsimd", self.dbg_cw.ap()[0:NE, :], cwrow[:, :], "wt")
        self.pop_to(gate_scope)

        y_acc = self.sb("f_yacc", [128, NDC * QN], F32)
        RING = 4
        W1C = self.sb("f_w1c", [128, RING * 1024], BF16)
        W3C = self.sb("f_w3c", [128, RING * 1024], BF16) if layer == 0 else None
        W2C = self.sb("f_w2c0", [128, HC * 1024], BF16)
        B1C = self.sb("f_b1c", [128, NPE * HC], F32)
        B3C = self.sb("f_b3c", [128, NPE * HC], F32) if layer == 0 else None
        B2S = self.sb("f_b2s", [NE, D], BF16)
        SB2S = self.sb("f_sb2s", [1, D], BF16) if layer == 0 else None
        CWB = self.sb("f_cwb", [128, 2 * QN], F32)
        AEB = self.sb("f_aeb", [128, HC * QN], BF16)
        NSTG = 4
        HSB = self.sb("f_hs", [128, NSTG * 512], F32)
        GCB = self.sb("f_gc", [128, NSTG * 512], F32) if layer == 0 else None

        psA = [self.ps("f_psA%d" % k, [128, 512]) for k in range(4)]
        psB = [self.ps("f_psB%d" % k, [128, 512]) for k in range(4)]

        p.waits("gpsimd", mm=p.mark("mm"), sc=p.mark("sc"), ve=p.mark("ve"))

        p.sem("bld")
        for e in range(NE):
            p.dma("gpsimd", B1C[:, e * HC:(e + 1) * HC],
                  (i["b1r"] if layer == 0 else i["b1_1r"]).ap()[e, :, :], "bld")
            if layer == 0:
                p.dma("gpsimd", B3C[:, e * HC:(e + 1) * HC], i["b3r"].ap()[e, :, :], "bld")
        if layer == 0:
            p.dma("gpsimd", B1C[:, NE * HC:NE * HC + L0_SHC], i["sb1r"].ap()[:, :], "bld")
            p.dma("gpsimd", B3C[:, NE * HC:NE * HC + L0_SHC], i["sb3r"].ap()[:, :], "bld")
            p.dma("gpsimd", SB2S[:, :], i["sb2"].ap()[:, :], "bld")
        bias_wt = p.dma("gpsimd", B2S[:, :],
                        (i["b2"] if layer == 0 else i["b2_1"]).ap()[:, :], "bld")

        sched = [(e, hh) for e in range(NPE) for hh in range(HC)]
        NCH = len(sched)

        w13_marks = [0] * NCH
        w13_rel = [0] * NCH
        w2_marks = [0] * NPE
        phaseB_rel = [0] * NPE
        evac_marks = [0] * NPE

        def bcol(e, hh, which):
            buf = B1C if which == 0 else B3C
            return buf[:, e * HC + hh:e * HC + hh + 1]

        def w1_src(e, hh):
            if layer == 0:
                if e < NE:
                    return (i["w1r"].ap()[e, hh, :, :], i["w3r"].ap()[e, hh, :, :])
                hh2 = (e - NE) * HC + hh
                return (i["sw1r"].ap()[hh2, :, :], i["sw3r"].ap()[hh2, :, :])
            return (i["W1r"].ap()[e, hh, :, :], None)

        def w2_src(e, hh):
            if layer == 0:
                if e < NE:
                    return bass.AP(i["w2"], (e * L0_H + hh * 128) * D, [[D, 128], [1, D]])
                hh2 = (e - NE) * HC + hh
                return bass.AP(i["sw2"], (hh2 * 128) * D, [[D, 128], [1, D]])
            return i["W2r"].ap()[e, hh, :, :]

        for _slot in range(RING):
            p.sem("w13s%d" % _slot)

        def emit_w13_load(k):
            e, hh = sched[k]
            slot = k % RING
            if k >= RING:
                p.wait("gpsimd", "mm", w13_rel[k - RING])
            s1, s3 = w1_src(e, hh)
            sname = "w13s%d" % slot
            v = p.dma("gpsimd", W1C[:, slot * 1024:(slot + 1) * 1024], s1, sname)
            if layer == 0:
                v = p.dma("gpsimd", W3C[:, slot * 1024:(slot + 1) * 1024], s3, sname)
            w13_marks[k] = (sname, v)

        p.sem("w2s")

        def emit_w2_load(e):
            if e >= 1:
                p.wait("gpsimd", "mm", phaseB_rel[e - 1])
            v = 0
            for hh in range(HC):
                v = p.dma("gpsimd", W2C[:, hh * 1024:(hh + 1) * 1024], w2_src(e, hh), "w2s")
            w2_marks[e] = v

        for k in range(min(RING, NCH)):
            emit_w13_load(k)
        emit_w2_load(0)

        p.waits("tensor", bld=bias_wt, ve=gating_ve)
        p.waits("scalar", bld=bias_wt)

        a_ve = []
        grp = 0
        cwb_ready = [0, 0]
        last_evac = [0]

        def emit_cwb(e):
            if layer == 0 and e >= NE:
                return
            eslot = e % 2
            for sl in range(QSL):
                pp = psB[sl]

                def fb(t, e=e, sl=sl, pp=pp):
                    t.matmul(pp[:, :], self.onehot[0:NE, e * 128:(e + 1) * 128],
                             cwrow[:, sl * 512:(sl + 1) * 512],
                             start=True, stop=True).then_inc(p.h["mm"], 1)
                p.emit("tensor", fb)
                mv = p.bump("mm")
                p.wait("vector", "mm", mv)

                def fbc(v, eslot=eslot, sl=sl, pp=pp):
                    v.tensor_copy(CWB[:, eslot * QN + sl * 512:eslot * QN + (sl + 1) * 512],
                                  pp[:, :]).then_inc(p.h["ve"], 1)
                p.emit("vector", fbc)
                p.bump("ve")
            cwb_ready[e % 2] = p.mark("ve")

        emit_cwb(0)

        for k in range(NCH):
            e, hh = sched[k]
            slot = k % RING
            last_of_e = (hh == HC - 1)

            p.wait("tensor", *w13_marks[k])
            for ts in range(QSL):
                pa = psA[2 * (grp % 2)]
                pg = psA[2 * (grp % 2) + 1]
                stg = grp % NSTG
                gv = a_ve[grp - 2] if grp >= 2 else 0
                p.wait("tensor", "ve", max(gv, last_evac[0] if hh == 0 else 0))

                if layer == 0:
                    def fA(t, slot=slot, ts=ts, pa=pa, pg=pg):
                        ins = None
                        for c in range(NDC):
                            t.matmul(pa[:, :],
                                     W1C[:, slot * 1024 + c * 128:slot * 1024 + (c + 1) * 128],
                                     xt[:, c * QN + ts * 512:c * QN + ts * 512 + 512],
                                     start=(c == 0), stop=(c == NDC - 1))
                        for c in range(NDC):
                            ins = t.matmul(pg[:, :],
                                     W3C[:, slot * 1024 + c * 128:slot * 1024 + (c + 1) * 128],
                                     xt[:, c * QN + ts * 512:c * QN + ts * 512 + 512],
                                     start=(c == 0), stop=(c == NDC - 1))
                        ins.then_inc(p.h["mm"], 1)
                else:
                    def fA(t, slot=slot, ts=ts, pa=pa, pg=pg):
                        ins = None
                        for c in range(NDC):
                            ins = t.matmul(pa[:, :],
                                     W1C[:, slot * 1024 + c * 128:slot * 1024 + (c + 1) * 128],
                                     xt[:, c * QN + ts * 512:c * QN + ts * 512 + 512],
                                     start=(c == 0), stop=(c == NDC - 1))
                        ins.then_inc(p.h["mm"], 1)
                p.emit("tensor", fA)
                mA = p.bump("mm")

                if layer == 0:
                    if hh == 0 and ts == 0 and e < NE:
                        p.wait("vector", "ve", cwb_ready[e % 2])
                    p.wait("scalar", "mm", mA)
                    if grp >= NSTG:
                        p.wait("scalar", "ve", a_ve[grp - NSTG])

                    def fhs(s, pa=pa, stg=stg, b1c=bcol(e, hh, 0)):
                        s.activation(HSB[:, stg * 512:(stg + 1) * 512], pa[:, :],
                                     AF.Silu, bias=b1c).then_inc(p.h["sc"], 1)
                    p.emit("scalar", fhs)
                    sA = p.bump("sc")

                    p.wait("vector", "mm", mA)
                    if e < NE:
                        def fgc(v, pg=pg, stg=stg, b3c=bcol(e, hh, 1), eslot=e % 2, ts=ts):
                            v.scalar_tensor_tensor(
                                GCB[:, stg * 512:(stg + 1) * 512], pg[:, :], b3c,
                                CWB[:, eslot * QN + ts * 512:eslot * QN + ts * 512 + 512],
                                ALU.add, ALU.mult).then_inc(p.h["ve"], 1)
                    else:
                        def fgc(v, pg=pg, stg=stg, b3c=bcol(e, hh, 1)):
                            v.tensor_scalar(GCB[:, stg * 512:(stg + 1) * 512], pg[:, :],
                                            b3c, None, ALU.add).then_inc(p.h["ve"], 1)
                    p.emit("vector", fgc)
                    vgc = p.bump("ve")
                    p.wait("vector", "ve", vgc)
                    p.wait("vector", "sc", sA)

                    def fam(v, stg=stg, hh=hh, ts=ts):
                        v.tensor_mul(AEB[:, hh * QN + ts * 512:hh * QN + ts * 512 + 512],
                                     HSB[:, stg * 512:(stg + 1) * 512],
                                     GCB[:, stg * 512:(stg + 1) * 512]).then_inc(p.h["ve"], 1)
                    p.emit("vector", fam)
                    a_ve.append(p.bump("ve"))
                else:
                    if hh == 0 and ts == 0:
                        p.wait("vector", "ve", cwb_ready[e % 2])
                    p.wait("vector", "mm", mA)

                    def fre(v, pa=pa, stg=stg, b1c=bcol(e, hh, 0)):
                        v.tensor_scalar(HSB[:, stg * 512:(stg + 1) * 512], pa[:, :],
                                        b1c, 0.0, ALU.add, ALU.max).then_inc(p.h["ve"], 1)
                    p.emit("vector", fre)
                    vre = p.bump("ve")
                    p.wait("vector", "ve", vre)

                    def fam(v, stg=stg, hh=hh, ts=ts, eslot=e % 2):
                        v.tensor_mul(AEB[:, hh * QN + ts * 512:hh * QN + ts * 512 + 512],
                                     HSB[:, stg * 512:(stg + 1) * 512],
                                     CWB[:, eslot * QN + ts * 512:eslot * QN + ts * 512 + 512]).then_inc(p.h["ve"], 1)
                    p.emit("vector", fam)
                    a_ve.append(p.bump("ve"))
                grp += 1
            w13_rel[k] = p.mark("mm")

            if k + RING < NCH:
                emit_w13_load(k + RING)

            if last_of_e:
                w2buf = W2C
                p.wait("tensor", "w2s", w2_marks[e])
                p.wait("tensor", "ve", a_ve[-1])
                gi = 0
                evB = []
                for dh in range(2):
                    for ts in range(QSL):
                        pset = psB if gi % 2 == 0 else psA
                        if gi >= 2:
                            p.wait("tensor", "ve", evB[gi - 2])
                        gi += 1

                        def fB(t, e=e, dh=dh, ts=ts, w2buf=w2buf, pset=pset):
                            ins = None
                            for dc in range(4):
                                do = dh * 4 + dc
                                if e == 0:
                                    t.matmul(pset[dc][:, :], B2S[:, do * 128:(do + 1) * 128],
                                             cwfm[:, ts * 512:(ts + 1) * 512],
                                             start=True, stop=False)
                                    if layer == 0:
                                        t.matmul(pset[dc][:, :],
                                                 SB2S[0:1, do * 128:(do + 1) * 128],
                                                 self.ones_row[0:1, :],
                                                 start=False, stop=False)
                                for hh2 in range(HC):
                                    ins = t.matmul(pset[dc][:, :],
                                             w2buf[:, hh2 * 1024 + do * 128:hh2 * 1024 + (do + 1) * 128],
                                             AEB[:, hh2 * QN + ts * 512:hh2 * QN + ts * 512 + 512],
                                             start=(hh2 == 0 and e != 0),
                                             stop=(hh2 == HC - 1))
                            ins.then_inc(p.h["mm"], 1)
                        p.emit("tensor", fB)
                        mB = p.bump("mm")

                        p.wait("vector", "mm", mB)

                        def fev(v, e=e, dh=dh, ts=ts, pset=pset):
                            ins = None
                            for dc in range(4):
                                do = dh * 4 + dc
                                dsl = y_acc[:, do * QN + ts * 512:do * QN + ts * 512 + 512]
                                if e == 0:
                                    ins = v.tensor_copy(dsl, pset[dc][:, :])
                                else:
                                    ins = v.tensor_add(dsl, dsl, pset[dc][:, :])
                            ins.then_inc(p.h["ve"], 1)
                        p.emit("vector", fev)
                        evB.append(p.bump("ve"))
                evac_marks[e] = p.mark("ve")
                last_evac[0] = evac_marks[e]
                phaseB_rel[e] = p.mark("mm")
                if e + 1 < NPE:
                    emit_w2_load(e + 1)
                p.wait("tensor", "ve", evac_marks[e])
                if e + 1 < NPE:
                    emit_cwb(e + 1)

        if MK_DBG and not self.dbg_done[0]:
            self.dbg_done[0] = True
            p.wait("gpsimd", "ve", p.mark("ve"))
            p.dma("gpsimd", self.dbg_yacc.ap()[:, :], y_acc[:, :], "wt")
        xold = self.sb("f_xold", [128, 2 * QN], F32)
        xnew = self.sb("f_xnew", [128, 2 * QN], F32)
        yve = p.mark("ve")
        ld_m = {}
        res_marks = []
        st_prev = {}
        p.sem("xo0"); p.sem("xo1"); p.sem("sn0"); p.sem("sn1")

        def emit_load(c):
            if c >= 2:
                p.wait("sync", "ve", res_marks[c - 2])
            ld_m[c] = (("xo%d" % (c % 2)),
                       p.dma("sync", xold[:, (c % 2) * QN:(c % 2) * QN + QN],
                             src.ap()[c * 128:(c + 1) * 128, t0:t0 + QN], "xo%d" % (c % 2)))

        emit_load(0)
        emit_load(1)
        for c in range(NDC):
            p.wait("vector", *ld_m[c])
            p.wait("vector", "ve", yve)
            if c >= 2:
                p.wait("vector", *st_prev[c - 2])

            def fres(v, c=c):
                v.scalar_tensor_tensor(xnew[:, (c % 2) * QN:(c % 2) * QN + QN],
                                       y_acc[:, c * QN:(c + 1) * QN], 0.5,
                                       xold[:, (c % 2) * QN:(c % 2) * QN + QN],
                                       ALU.mult, ALU.add).then_inc(p.h["ve"], 1)
            p.emit("vector", fres)
            res_marks.append(p.bump("ve"))
            p.wait("sync", "ve", res_marks[-1])
            st_prev[c] = (("sn%d" % (c % 2)),
                          p.dma("sync", dst.ap()[c * 128:(c + 1) * 128, t0:t0 + QN],
                                xnew[:, (c % 2) * QN:(c % 2) * QN + QN], "sn%d" % (c % 2)))
            if c + 2 < NDC:
                emit_load(c + 2)


def build_nc(n_sublayers):
    b = Builder()
    p = b.p
    b.build_consts()

    chain = [("ffn", 0), ("attn", 0), ("ffn", 0), ("ffn", 1), ("attn", 1), ("ffn", 1)]
    ch_env = os.environ.get("MK_CHAIN", "")
    if ch_env:
        chain = [(c[:-1], int(c[-1])) for c in ch_env.split(",")]
    else:
        chain = chain[:n_sublayers]
    src = b.inputs["x0"]
    ping = 0
    for si, (kind, layer) in enumerate(chain):
        dst = b.out if si == len(chain) - 1 else (b.xA if ping == 0 else b.xB)
        if si < len(chain) - 1:
            ping ^= 1
        if kind == "ffn":
            b.emit_ffn(layer, src, dst)
        else:
            b.emit_attn(layer, src, dst)
        b.join_all()
        src = dst

    for snm in list(p.cnt):
        p.wait("sync", snm, p.mark(snm))
    with b.nc.Block() as block:
        p.finalize(block)
    b.pop_to(0)
    return b.nc


def _prep_shared(inp):
    c = np.ascontiguousarray
    f = np.float32
    d = {}
    d["gate_w"] = c(inp["L0_ffn_gate_w"].reshape(NDC, 128, L0_E).transpose(1, 0, 2)).astype(f)
    d["gate_b"] = c(inp["L0_ffn_gate_b"].reshape(1, L0_E)).astype(f)
    d["w1r"] = c(np.asarray(inp["L0_ffn_w1"]).reshape(L0_E, NDC, 128, L0_HC, 128)
                 .transpose(0, 3, 2, 1, 4).reshape(L0_E, L0_HC, 128, 1024)).astype(f)
    d["w3r"] = c(np.asarray(inp["L0_ffn_w3"]).reshape(L0_E, NDC, 128, L0_HC, 128)
                 .transpose(0, 3, 2, 1, 4).reshape(L0_E, L0_HC, 128, 1024)).astype(f)
    d["w2"] = c(np.asarray(inp["L0_ffn_w2"])).astype(f)
    d["b1r"] = c(np.asarray(inp["L0_ffn_b1"]).reshape(L0_E, L0_HC, 128).transpose(0, 2, 1)).astype(f)
    d["b3r"] = c(np.asarray(inp["L0_ffn_b3"]).reshape(L0_E, L0_HC, 128).transpose(0, 2, 1)).astype(f)
    d["b2"] = c(np.asarray(inp["L0_ffn_b2"])).astype(f)
    d["sw1r"] = c(np.asarray(inp["L0_ffn_sw1"]).reshape(NDC, 128, L0_SHC, 128)
                  .transpose(2, 1, 0, 3).reshape(L0_SHC, 128, 1024)).astype(f)
    d["sw3r"] = c(np.asarray(inp["L0_ffn_sw3"]).reshape(NDC, 128, L0_SHC, 128)
                  .transpose(2, 1, 0, 3).reshape(L0_SHC, 128, 1024)).astype(f)
    d["sw2"] = c(np.asarray(inp["L0_ffn_sw2"])).astype(f)
    d["sb1r"] = c(np.asarray(inp["L0_ffn_sb1"]).reshape(L0_SHC, 128).T).astype(f)
    d["sb3r"] = c(np.asarray(inp["L0_ffn_sb3"]).reshape(L0_SHC, 128).T).astype(f)
    d["sb2"] = c(np.asarray(inp["L0_ffn_sb2"]).reshape(1, D)).astype(f)
    d["WQr"] = c(np.asarray(inp["L0_attn_WQ"]).reshape(NDC, 128, D).transpose(1, 0, 2)).astype(f)
    d["WKr"] = c(np.asarray(inp["L0_attn_WK"]).reshape(NDC, 128, D).transpose(1, 0, 2)).astype(f)
    d["WVr"] = c(np.asarray(inp["L0_attn_WV"]).reshape(NDC, 128, D).transpose(1, 0, 2)).astype(f)
    d["wqkr"] = c(np.asarray(inp["L1_attn_wqk"]).reshape(NDC, 128, 128).transpose(1, 0, 2)).astype(f)
    d["bqk"] = c(np.asarray(inp["L1_attn_bqk"]).reshape(128, 1)).astype(f)
    gam = np.asarray(inp["L1_attn_gamma"])
    bet = np.asarray(inp["L1_attn_beta"])
    gb = np.stack([gam[0] / SEQ, bet[0] / SEQ, gam[1], bet[1]], axis=1)
    d["gbcols"] = c(gb).astype(f)
    d["wvr"] = c(np.asarray(inp["L1_attn_wv"]).reshape(NDC, 128, D).transpose(1, 0, 2)).astype(f)
    d["bvrow"] = c(np.asarray(inp["L1_attn_bv"]).reshape(1, D)).astype(f)
    d["relb"] = c(np.asarray(inp["L1_attn_rel_bias"]).reshape(1, NUM_BUCKETS)).astype(f)
    d["W1r"] = c(np.asarray(inp["L1_ffn_W1"]).reshape(L1_E, L1_HC, 128, NDC, 128)
                 .transpose(0, 1, 4, 3, 2).reshape(L1_E, L1_HC, 128, 1024)).astype(f)
    d["W2r"] = c(np.asarray(inp["L1_ffn_W2"]).reshape(L1_E, D, L1_HC, 128)
                 .transpose(0, 2, 3, 1).reshape(L1_E, L1_HC, 128, 1024)).astype(f)
    d["b1_1r"] = c(np.asarray(inp["L1_ffn_b1"]).reshape(L1_E, L1_HC, 128).transpose(0, 2, 1)).astype(f)
    d["b2_1"] = c(np.asarray(inp["L1_ffn_b2"])).astype(f)
    d["Wgr"] = c(np.asarray(inp["L1_ffn_Wg"]).T.reshape(NDC, 128, L1_E).transpose(1, 0, 2)).astype(f)
    d["bg_1"] = c(np.asarray(inp["L1_ffn_bg"]).reshape(1, L1_E)).astype(f)
    return d


def prepare_in_maps(inputs):
    x = np.asarray(inputs["L0_attn_inputs"])
    shared = _prep_shared(inputs)
    in_maps = []
    for c in range(NCORES):
        seq, half = c // 2, c % 2
        xs = np.ascontiguousarray(x[seq, half * NT:(half + 1) * NT, :].T).astype(np.float32)
        m = dict(shared)
        m["x0"] = xs
        m["hflag"] = np.array([[float(half)]], dtype=np.float32)
        in_maps.append(m)
    return in_maps


def kernel(**inputs):
    in_maps = prepare_in_maps(inputs)
    nc = build_nc(N_SUBLAYERS)
    res = run_bass_kernel_spmd(nc, in_maps, core_ids=list(range(NCORES)))
    outp = np.empty((4, SEQ, D), dtype=np.float32)
    for c in range(NCORES):
        seq, half = c // 2, c % 2
        outp[seq, half * NT:(half + 1) * NT, :] = res.results[c]["out"].T
    return outp


# version 02
# speedup vs baseline: 1.0149x; 1.0149x over previous
import math
import os
import numpy as np

import concourse.bass as bass
import concourse.mybir as mybir
from concourse.bass_utils import run_bass_kernel_spmd

F32 = mybir.dt.float32
BF16 = mybir.dt.bfloat16
AF = mybir.ActivationFunctionType
ALU = mybir.AluOpType
AXL = mybir.AxisListType

NCORES = 8
D = 1024
NDC = D // 128
NT = 4096
SEQ = 8192
BIG_NEG = -1.0e9

L0_E, L0_H, L0_HC = 16, 1408, 11
L0_SH, L0_SHC = 2816, 22
L1_E, L1_H, L1_HC = 8, 2048, 16
NUM_BUCKETS, MAX_DISTANCE = 32, 128

N_SUBLAYERS = int(os.environ.get("MK_N_SUBLAYERS", "6"))
ERF_FN_NAME = "Tanh" if os.environ.get("MK_ERF_TANH") == "1" else "Erf"
MK_QUARTERS = int(os.environ.get("MK_QUARTERS", "0"))
SIM_RACES = os.environ.get("MK_SIM", "0") == "1"
MK_DBG = os.environ.get("MK_DBG", "0") == "1"

QN = 1024
NQ = NT // QN
QSL = QN // 512


class Prog:
    ENGINES = ("tensor", "vector", "scalar", "gpsimd", "sync")

    def __init__(self, nc):
        self.nc = nc
        self.q = {e: [] for e in self.ENGINES}
        self.cnt = {}
        self.h = {}

    def sem(self, name):
        if name in self.h:
            return
        g = self.nc.semaphore(name)
        self.h[name] = g.__enter__()
        self.cnt[name] = 0

    def bump(self, name, n=1):
        self.cnt[name] += n
        return self.cnt[name]

    def mark(self, name):
        return self.cnt[name]

    def emit(self, eng, fn):
        self.q[eng].append(fn)

    def dma(self, qeng, out, in_, sem_name, **kw):
        h = self.h[sem_name]
        v = self.bump(sem_name, 16)

        def f(eng, out=out, in_=in_, h=h, kw=kw):
            eng.dma_start(out=out, in_=in_, **kw).then_inc(h, 16)
        self.emit(qeng, f)
        return v

    def wait(self, eng_name, sem_name, val):
        if val <= 0:
            return
        h = self.h[sem_name]

        def f(eng, h=h, val=val):
            eng.wait_ge(h, val)
        self.emit(eng_name, f)

    def waits(self, eng_name, **marks):
        for s, v in marks.items():
            self.wait(eng_name, s, v)

    def finalize(self, block):
        for e in self.ENGINES:
            fns = self.q[e]

            def body(eng, fns=fns):
                for f in fns:
                    f(eng)
            getattr(block, e)(body)


class Builder:
    def __init__(self):
        nc = bass.Bass("TRN2", target_bir_lowering=False,
                       detect_race_conditions=False)
        self.nc = nc
        self.ctx = []
        p = Prog(nc)
        self.p = p
        for s in ("ld", "st", "wt", "mm", "sc", "ve", "cc", "g2"):
            p.sem(s)

        def ein(name, shape):
            return nc.dram_tensor(name, list(shape), F32, kind="ExternalInput")
        self.inputs = {}
        i = self.inputs
        i["x0"] = ein("x0", [D, NT])
        i["hflag"] = ein("hflag", [1, 1])
        i["gate_w"] = ein("gate_w", [128, NDC, L0_E])
        i["gate_b"] = ein("gate_b", [1, L0_E])
        i["w1r"] = ein("w1r", [L0_E, L0_HC, 128, 1024])
        i["w3r"] = ein("w3r", [L0_E, L0_HC, 128, 1024])
        i["w2"] = ein("w2", [L0_E, L0_H, D])
        i["b1r"] = ein("b1r", [L0_E, 128, L0_HC])
        i["b3r"] = ein("b3r", [L0_E, 128, L0_HC])
        i["b2"] = ein("b2", [L0_E, D])
        i["sw1r"] = ein("sw1r", [L0_SHC, 128, 1024])
        i["sw3r"] = ein("sw3r", [L0_SHC, 128, 1024])
        i["sw2"] = ein("sw2", [L0_SH, D])
        i["sb1r"] = ein("sb1r", [128, L0_SHC])
        i["sb3r"] = ein("sb3r", [128, L0_SHC])
        i["sb2"] = ein("sb2", [1, D])
        i["WQr"] = ein("WQr", [128, NDC, D])
        i["WKr"] = ein("WKr", [128, NDC, D])
        i["WVr"] = ein("WVr", [128, NDC, D])
        i["wqkr"] = ein("wqkr", [128, NDC, 128])
        i["bqk"] = ein("bqk", [128, 1])
        i["gbcols"] = ein("gbcols", [128, 4])
        i["wvr"] = ein("wvr", [128, NDC, D])
        i["bvrow"] = ein("bvrow", [1, D])
        i["relb"] = ein("relb", [1, NUM_BUCKETS])
        i["W1r"] = ein("W1r", [L1_E, L1_HC, 128, 1024])
        i["W2r"] = ein("W2r", [L1_E, L1_HC, 128, 1024])
        i["b1_1r"] = ein("b1_1r", [L1_E, 128, L1_HC])
        i["b2_1"] = ein("b2_1", [L1_E, D])
        i["Wgr"] = ein("Wgr", [128, NDC, L1_E])
        i["bg_1"] = ein("bg_1", [1, L1_E])

        self.out = nc.dram_tensor("out", [D, NT], F32, kind="ExternalOutput")
        if MK_DBG:
            self.dbg_xt = nc.dram_tensor("dbg_xt", [128, NDC * QN], F32, kind="ExternalOutput")
            self.dbg_cw = nc.dram_tensor("dbg_cw", [16, QN], F32, kind="ExternalOutput")
            self.dbg_yacc = nc.dram_tensor("dbg_yacc", [128, NDC * QN], F32, kind="ExternalOutput")
            self.dbg_stat = nc.dram_tensor("dbg_stat", [1, 4 * QN], F32, kind="ExternalOutput")
            self.dbg_done = [False]
        self.xA = nc.dram_tensor("xA", [D, NT], F32, kind="Internal")
        self.xB = nc.dram_tensor("xB", [D, NT], F32, kind="Internal")

        self.eye_f = nc.inline_tensor(np.eye(128, dtype=np.float32), "eye_f")
        bmnp = np.zeros((128, 640), dtype=np.float32)
        for r in range(128):
            bmnp[r, r:r + 512] = 1.0
        self.bandmask_c = nc.inline_tensor(bmnp, "bandmask")
        oh = np.zeros((16, 16 * 128), dtype=np.float32)
        for e in range(16):
            oh[e, e * 128:(e + 1) * 128] = 1.0
        self.onehot_c = nc.inline_tensor(oh, "onehot16")
        oh2 = np.zeros((NUM_BUCKETS, 2 * NUM_BUCKETS), dtype=np.float32)
        for j, (_d, bj, bp) in enumerate([(1, 1, 0), (2, 2, 1), (3, 3, 2), (4, 4, 3), (5, 5, 4), (6, 6, 5), (7, 7, 6), (8, 8, 7), (9, 9, 8), (10, 10, 9), (11, 11, 10), (12, 12, 11), (13, 13, 12), (14, 14, 13), (15, 15, 14), (16, 16, 15), (19, 17, 16), (21, 18, 17), (24, 19, 18), (27, 20, 19), (31, 21, 20), (35, 22, 21), (40, 23, 22), (46, 24, 23), (52, 25, 24), (59, 26, 25), (67, 27, 26), (77, 28, 27), (87, 29, 28), (99, 30, 29), (113, 31, 30)]):
            oh2[bj, j] = 1.0
            oh2[bp, NUM_BUCKETS + j] = 1.0
        self.bktoh_c = nc.inline_tensor(oh2, "bktoh")

    _uid = 0

    def _un(self, name):
        Builder._uid += 1
        return "%s_%d" % (name, Builder._uid)

    def sb(self, name, shape, dtype):
        g = self.nc.sbuf_tensor(self._un(name), list(shape), dtype)
        t = g.__enter__()
        self.ctx.append(g)
        return t

    def ps(self, name, shape, dtype=F32):
        g = self.nc.psum_tensor(self._un(name), list(shape), dtype)
        t = g.__enter__()
        self.ctx.append(g)
        return t

    def scope(self):
        return len(self.ctx)

    def pop_to(self, n):
        while len(self.ctx) > n:
            self.ctx.pop().__exit__(None, None, None)

    def build_consts(self):
        p = self.p
        self.ones_col = self.sb("ones_col", [128, 1], BF16)
        self.onesf_col = self.sb("onesf_col", [128, 1], F32)
        self.onesf_row = self.sb("onesf_row", [1, 128], F32)
        self.ones_row = self.sb("ones_row", [1, 512], BF16)
        self.eps_c = self.sb("eps_c", [1, 1], F32)
        self.hf_c = self.sb("hf_c", [1, 1], F32)
        self.ident = self.sb("ident", [128, 128], F32)
        self.ident_bf = self.sb("ident_bf", [128, 128], BF16)
        self.onehot = self.sb("onehot", [16, 16 * 128], F32)
        self.bandm = self.sb("bandm", [128, 640], F32)
        p.dma("sync", self.ident[:, :], self.eye_f.ap()[:, :], "ld")
        p.dma("sync", self.onehot[:, :], self.onehot_c.ap()[:, :], "ld")
        p.dma("sync", self.bandm[:, :], self.bandmask_c.ap()[:, :], "ld")
        p.sem("ldg")
        p.dma("gpsimd", self.ident_bf[:, :], self.eye_f.ap()[:, :], "ldg")
        p.dma("sync", self.hf_c[:, :], self.inputs["hflag"].ap()[:, :], "ld")

        def g_init(g):
            h = p.h["g2"]
            g.memset(self.ones_col[:, :], 1.0).then_inc(h, 1)
            g.memset(self.onesf_col[:, :], 1.0).then_inc(h, 1)
            g.memset(self.onesf_row[:, :], 1.0).then_inc(h, 1)
            g.memset(self.ones_row[:, :], 1.0).then_inc(h, 1)
            g.memset(self.eps_c[:, :], 1e-5).then_inc(h, 1)
        p.emit("gpsimd", g_init)
        p.bump("g2", 5)
        self.init_marks = dict(g2=p.mark("g2"), ld=p.mark("ld"), ldg=p.mark("ldg"))

    def join_all(self):
        p = self.p
        marks = {s: p.mark(s) for s in list(p.cnt)}
        for e in Prog.ENGINES:
            for s, v in marks.items():
                p.wait(e, s, v)

    def emit_stats(self, src, t0, n, stat):
        p = self.p
        SS, SQ, SM, SA = 0, n, 2 * n, 3 * n
        xs = self.sb("st_x", [128, 2 * n], BF16)
        sq = self.sb("st_sq", [128, 2 * n], BF16)
        pss = [self.ps("st_ps%d" % k, [1, 512]) for k in range(4)]
        nsl = n // 512

        def z(v):
            v.memset(stat[0:1, 0:2 * n], 0.0).then_inc(p.h["ve"], 1)
        p.emit("vector", z)
        zv = p.bump("ve")

        ld_m = {}
        mm_done = {}
        sc_done = {}
        facc_marks = []
        pi = 0

        p.sem("stx0"); p.sem("stx1")
        p.waits("gpsimd", mm=p.mark("mm"), sc=p.mark("sc"), ve=p.mark("ve"))
        p.waits("tensor", ve=p.mark("ve"), sc=p.mark("sc"))

        def emit_load(c):
            if c >= 2:
                p.wait("gpsimd", "mm", mm_done[c - 2])
                p.wait("gpsimd", "sc", sc_done[c - 2])
            ld_m[c] = (("stx%d" % (c % 2)),
                       p.dma("gpsimd", xs[:, (c % 2) * n:(c % 2) * n + n],
                             src.ap()[c * 128:(c + 1) * 128, t0:t0 + n], "stx%d" % (c % 2)))

        emit_load(0)
        emit_load(1)
        p.wait("vector", "ve", zv)
        for c in range(NDC):
            xc = xs[:, (c % 2) * n:(c % 2) * n + n]
            sqc = sq[:, (c % 2) * n:(c % 2) * n + n]
            p.wait("scalar", *ld_m[c])

            def fsq(s, xc=xc, sqc=sqc):
                s.activation(sqc, xc, AF.Square).then_inc(p.h["sc"], 1)
            p.emit("scalar", fsq)
            sc_done[c] = p.bump("sc")

            p.wait("tensor", *ld_m[c])
            p.wait("tensor", "sc", sc_done[c])
            for which, buf in ((SS, xc), (SQ, sqc)):
                for sl in range(nsl):
                    pp = pss[pi % 4]
                    if pi >= 4:
                        p.wait("tensor", "ve", facc_marks[pi - 4])

                    def fmm(t, pp=pp, buf=buf, sl=sl):
                        t.matmul(pp[:, :], self.ones_col[:, 0:1],
                                 buf[:, sl * 512:(sl + 1) * 512],
                                 start=True, stop=True).then_inc(p.h["mm"], 1)
                    p.emit("tensor", fmm)
                    mv = p.bump("mm")
                    p.wait("vector", "mm", mv)

                    def facc(v, pp=pp, which=which, sl=sl):
                        v.tensor_add(stat[0:1, which + sl * 512:which + (sl + 1) * 512],
                                     stat[0:1, which + sl * 512:which + (sl + 1) * 512],
                                     pp[0:1, :]).then_inc(p.h["ve"], 1)
                    p.emit("vector", facc)
                    facc_marks.append(p.bump("ve"))
                    pi += 1
            mm_done[c] = p.mark("mm")
            if c + 2 < NDC:
                emit_load(c + 2)

        p.wait("scalar", "ve", p.mark("ve"))
        h = p.h["sc"]

        def f1(s):
            s.mul(stat[0:1, SM:SM + n], stat[0:1, SS:SS + n], 1.0 / 1024.0).then_inc(h, 1)
        p.emit("scalar", f1)
        v = p.bump("sc")
        p.wait("scalar", "sc", v)

        def f2(s):
            s.square(stat[0:1, SS:SS + n], stat[0:1, SM:SM + n]).then_inc(h, 1)
        p.emit("scalar", f2)
        v = p.bump("sc")
        p.wait("scalar", "sc", v)

        def f3(s):
            s.mul(stat[0:1, SQ:SQ + n], stat[0:1, SQ:SQ + n], 1.0 / 1024.0).then_inc(h, 1)
        p.emit("scalar", f3)
        v = p.bump("sc")
        p.wait("vector", "sc", v)

        def f4(v_):
            v_.tensor_sub(stat[0:1, SQ:SQ + n], stat[0:1, SQ:SQ + n],
                          stat[0:1, SS:SS + n]).then_inc(p.h["ve"], 1)
        p.emit("vector", f4)
        vv = p.bump("ve")
        p.wait("scalar", "ve", vv)

        def f5(s):
            s.activation(stat[0:1, SA:SA + n], stat[0:1, SQ:SQ + n], AF.Ln,
                         bias=self.eps_c[0:1, 0:1]).then_inc(h, 1)
        p.emit("scalar", f5)
        v = p.bump("sc")
        p.wait("scalar", "sc", v)

        def f6(s):
            s.activation(stat[0:1, SA:SA + n], stat[0:1, SA:SA + n], AF.Exp,
                         scale=-0.5).then_inc(h, 1)
        p.emit("scalar", f6)
        sc_mark = p.bump("sc")
        return SM, SA, sc_mark

    def emit_xtilde(self, src, xt, t0, n, stat, SM, SA, sc_mark,
                    xt_stride=None, xt_off=0):
        p = self.p
        mb = self.sb("xt_mb", [128, 2 * n], F32)
        xstg = self.sb("xt_stg", [128, 2 * n], F32)
        pb = [self.ps("xt_pb%d" % k, [128, 512]) for k in range(2)]
        nsl = n // 512
        if xt_stride is None:
            xt_stride = n
        p.wait("tensor", "sc", sc_mark)
        p.wait("tensor", "g2", self.init_marks["g2"])
        cp_marks = []
        k = 0
        for which, soff in ((0, SM), (1, SA)):
            for sl in range(nsl):
                pp = pb[k % 2]
                if k >= 2:
                    p.wait("tensor", "ve", cp_marks[k - 2])

                def fmm(t, pp=pp, soff=soff, sl=sl):
                    t.matmul(pp[:, :], self.onesf_row[0:1, :],
                             stat[0:1, soff + sl * 512:soff + (sl + 1) * 512],
                             start=True, stop=True).then_inc(p.h["mm"], 1)
                p.emit("tensor", fmm)
                mv = p.bump("mm")
                p.wait("vector", "mm", mv)

                def fcp(v, pp=pp, which=which, sl=sl):
                    v.tensor_copy(mb[:, which * n + sl * 512:which * n + (sl + 1) * 512],
                                  pp[:, :]).then_inc(p.h["ve"], 1)
                p.emit("vector", fcp)
                cp_marks.append(p.bump("ve"))
                k += 1

        ld_m = {}
        res_marks = []
        p.sem("xtg0"); p.sem("xtg1")
        p.waits("sync", mm=p.mark("mm"), sc=p.mark("sc"), ve=p.mark("ve"))

        def emit_load(c):
            if c >= 2:
                p.wait("sync", "ve", res_marks[c - 2])
            ld_m[c] = (("xtg%d" % (c % 2)),
                       p.dma("sync", xstg[:, (c % 2) * n:(c % 2) * n + n],
                             src.ap()[c * 128:(c + 1) * 128, t0:t0 + n], "xtg%d" % (c % 2)))

        emit_load(0)
        emit_load(1)
        p.wait("vector", "ve", cp_marks[-1])
        for c in range(NDC):
            xc = xstg[:, (c % 2) * n:(c % 2) * n + n]
            p.wait("vector", *ld_m[c])

            def fxt1(v, xc=xc):
                v.tensor_sub(xc, xc, mb[:, 0:n]).then_inc(p.h["ve"], 1)
            p.emit("vector", fxt1)
            v1 = p.bump("ve")
            p.wait("vector", "ve", v1)

            def fxt2(v, xc=xc, c=c):
                v.tensor_mul(xt[:, c * xt_stride + xt_off:c * xt_stride + xt_off + n],
                             xc, mb[:, n:2 * n]).then_inc(p.h["ve"], 1)
            p.emit("vector", fxt2)
            res_marks.append(p.bump("ve"))
            if c + 2 < NDC:
                emit_load(c + 2)
        return p.mark("ve")


    def emit_bcast(self, srcrow, dst, n, pbank, extra_tensor_wait=None):
        p = self.p
        nsl = (n + 511) // 512
        last = 0
        for sl in range(nsl):
            w = min(512, n - sl * 512)
            if extra_tensor_wait is not None:
                p.wait("tensor", extra_tensor_wait[0], extra_tensor_wait[1])
            if sl >= 1:
                p.wait("tensor", "ve", last)

            def fm(t, sl=sl, w=w):
                t.matmul(pbank[:, 0:w], self.onesf_row[0:1, :],
                         srcrow[0:1, sl * 512:sl * 512 + w],
                         start=True, stop=True).then_inc(p.h["mm"], 1)
            p.emit("tensor", fm)
            mv = p.bump("mm")
            p.wait("vector", "mm", mv)

            def fc(v, sl=sl, w=w):
                v.tensor_copy(dst[:, sl * 512:sl * 512 + w],
                              pbank[:, 0:w]).then_inc(p.h["ve"], 1)
            p.emit("vector", fc)
            last = p.bump("ve")
        return last

    def emit_attn(self, layer, src, dst):
        sc_all = self.scope()
        if layer == 0:
            self._attn0(src, dst)
        else:
            self._attn1(src, dst)
        self.pop_to(sc_all)

    def emit_xt_full(self, src):
        xt = self.sb("a_xt", [128, NDC * NT], BF16)
        xt_ve = 0
        for q in range(NQ):
            scq = self.scope()
            stat = self.sb("a_stat", [1, 4 * QN], F32)
            SM, SA, st_sc = self.emit_stats(src, q * QN, QN, stat)
            xt_ve = self.emit_xtilde(src, xt, q * QN, QN, stat, SM, SA, st_sc,
                                     xt_stride=NT, xt_off=q * QN)
            self.pop_to(scq)
        return xt, xt_ve

    def _attn0(self, src, dst):
        p = self.p
        i = self.inputs
        nc = self.nc
        HKN = 4608

        K_hbm = nc.dram_tensor(self._un("a0_K"), [NDC, 128, HKN], BF16, kind="Internal")
        V_hbm = nc.dram_tensor(self._un("a0_V"), [HKN, D], BF16, kind="Internal")
        kx_in = nc.dram_tensor(self._un("a0_kxi"), [NDC, 128, 512], BF16, kind="Internal")
        kx_out = nc.dram_tensor(self._un("a0_kxo"), [2, NDC, 128, 512], BF16, kind="Internal")
        vx_in = nc.dram_tensor(self._un("a0_vxi"), [512, D], BF16, kind="Internal")
        vx_out = nc.dram_tensor(self._un("a0_vxo"), [2, 512, D], BF16, kind="Internal")
        vs_in = nc.dram_tensor(self._un("a0_vsi"), [1, D], F32, kind="Internal")
        vs_out = nc.dram_tensor(self._un("a0_vso"), [1, D], F32, kind="Internal")

        for snm in (["awl", "a0x", "a0h0", "a0h1", "kw0", "kw1", "vw0", "vw1"]
                    + ["ks%d" % x for x in range(4)] + ["vs%d" % x for x in range(2)]
                    + ["xoa%d" % x for x in range(4)] + ["sna%d" % x for x in range(4)]):
            p.sem(snm)

        Vb = self.sb("a0_vbc", [128, D], F32)
        hfc = self.sb("a0_hf", [128, 2], F32)
        Q_fm = self.sb("a0_q", [128, NDC * NT], BF16)
        scp = self.scope()
        xt, xt_ve = self.emit_xt_full(src)
        scpW = self.scope()
        WQs = self.sb("a0_wq", [128, NDC * D], BF16)
        WKs = self.sb("a0_wk", [128, NDC * D], BF16)
        WVs = self.sb("a0_wv", [128, NDC * D], BF16)
        kst = self.sb("a0_kst", [128, 4 * 512], BF16)
        vst = self.sb("a0_vst", [128, 2 * D], BF16)
        vsum = self.sb("a0_vsum", [1, D], F32)
        psp = [self.ps("a0_pp%d" % k2, [128, 512]) for k2 in range(4)]
        psv = [self.ps("a0_pv%d" % k2, [1, 512]) for k2 in range(2)]

        p.waits("gpsimd", mm=p.mark("mm"), sc=p.mark("sc"), ve=p.mark("ve"))
        p.dma("gpsimd", WQs[:, :], i["WQr"].ap()[:, :, :], "awl")
        p.dma("gpsimd", WKs[:, :], i["WKr"].ap()[:, :, :], "awl")
        lv = p.dma("gpsimd", WVs[:, :], i["WVr"].ap()[:, :, :], "awl")
        p.waits("tensor", awl=lv, ve=xt_ve, g2=self.init_marks["g2"],
                ld=self.init_marks["ld"], ldg=self.init_marks["ldg"])

        hh1 = self.emit_bcast(self.hf_c[0:1, 0:1], hfc[:, 0:1], 1, psp[0])
        p.wait("vector", "ve", hh1)

        def fihf(v):
            v.tensor_scalar(hfc[:, 1:2], hfc[:, 0:1], -1.0, 1.0,
                            ALU.mult, ALU.add).then_inc(p.h["ve"], 1)
        p.emit("vector", fihf)
        hf_ve = p.bump("ve")

        q_sc = {}
        gi = 0
        for dq in range(NDC):
            for ts in range(NT // 512):
                pp = psp[gi % 2]
                if gi >= 2:
                    p.wait("tensor", "sc", q_sc[gi - 2])
                elif gi == 0:
                    p.wait("tensor", "ve", hh1)

                def fq(t, dq=dq, ts=ts, pp=pp):
                    ins = None
                    for c in range(NDC):
                        ins = t.matmul(pp[:, :],
                                       WQs[:, c * D + dq * 128:c * D + (dq + 1) * 128],
                                       xt[:, c * NT + ts * 512:c * NT + ts * 512 + 512],
                                       start=(c == 0), stop=(c == NDC - 1))
                    ins.then_inc(p.h["mm"], 1)
                p.emit("tensor", fq)
                mv = p.bump("mm")
                p.wait("scalar", "mm", mv)

                def fqe(s2, dq=dq, ts=ts, pp=pp):
                    s2.activation(Q_fm[:, dq * NT + ts * 512:dq * NT + ts * 512 + 512],
                                  pp[:, :], AF.Copy, scale=1.0 / 32.0).then_inc(p.h["sc"], 1)
                p.emit("scalar", fqe)
                q_sc[gi] = p.bump("sc")
                gi += 1

        k_ve = {}
        k_st = {}
        gi = 0
        for dq in range(NDC):
            for ts in range(NT // 512):
                pp = psp[2 + gi % 2]
                slot = gi % 4
                if gi >= 2:
                    p.wait("tensor", "ve", k_ve[gi - 2])

                def fk(t, dq=dq, ts=ts, pp=pp):
                    ins = None
                    for c in range(NDC):
                        ins = t.matmul(pp[:, :],
                                       WKs[:, c * D + dq * 128:c * D + (dq + 1) * 128],
                                       xt[:, c * NT + ts * 512:c * NT + ts * 512 + 512],
                                       start=(c == 0), stop=(c == NDC - 1))
                    ins.then_inc(p.h["mm"], 1)
                p.emit("tensor", fk)
                mv = p.bump("mm")
                p.wait("vector", "mm", mv)
                if gi >= 4:
                    p.wait("vector", "ks%d" % slot, k_st[gi - 4])

                def fke(v, slot=slot, pp=pp):
                    v.tensor_copy(kst[:, slot * 512:(slot + 1) * 512],
                                  pp[:, :]).then_inc(p.h["ve"], 1)
                p.emit("vector", fke)
                k_ve[gi] = p.bump("ve")
                p.wait("sync", "ve", k_ve[gi])
                k_st[gi] = p.dma("sync",
                                 bass.AP(K_hbm, (dq * 128) * HKN + 256 + ts * 512,
                                         [[HKN, 128], [1, 512]]),
                                 kst[:, slot * 512:(slot + 1) * 512], "ks%d" % slot)
                gi += 1

        p.wait("tensor", "ve", k_ve[gi - 1])
        p.wait("tensor", "sc", q_sc[len(q_sc) - 1])
        v_ve = {}
        v_st = {}
        vs_mm_marks = {}
        NTT = NT // 128
        for tt in range(NTT):
            pp2 = [psp[tt % 2], psp[2 + tt % 2]]
            if tt >= 2:
                p.wait("tensor", "ve", v_ve[tt - 2])

            def fv(t, tt=tt, pp2=pp2):
                ins = None
                for sl in range(2):
                    for c in range(NDC):
                        ins = t.matmul(pp2[sl][:, :],
                                       xt[:, c * NT + tt * 128:c * NT + (tt + 1) * 128],
                                       WVs[:, c * D + sl * 512:c * D + sl * 512 + 512],
                                       start=(c == 0), stop=(c == NDC - 1))
                ins.then_inc(p.h["mm"], 1)
            p.emit("tensor", fv)
            mv = p.bump("mm")
            p.wait("vector", "mm", mv)
            if tt >= 2:
                p.wait("vector", "vs%d" % (tt % 2), v_st[tt - 2])
                p.wait("vector", "mm", vs_mm_marks[tt - 2])

            def fve(v, tt=tt, pp2=pp2):
                v.tensor_copy(vst[:, (tt % 2) * D:(tt % 2) * D + 512], pp2[0][:, :])
                v.tensor_copy(vst[:, (tt % 2) * D + 512:(tt % 2) * D + 1024],
                              pp2[1][:, :]).then_inc(p.h["ve"], 1)
            p.emit("vector", fve)
            v_ve[tt] = p.bump("ve")
            p.wait("sync", "ve", v_ve[tt])
            v_st[tt] = p.dma("sync",
                             bass.AP(V_hbm, (256 + tt * 128) * D, [[D, 128], [1, D]]),
                             vst[:, (tt % 2) * D:(tt % 2) * D + D], "vs%d" % (tt % 2))
            p.wait("tensor", "ve", v_ve[tt])

            def fvs(t, tt=tt):
                ins = None
                for sl in range(2):
                    ins = t.matmul(psv[sl][:, :], self.ones_col[:, 0:1],
                                   vst[:, (tt % 2) * D + sl * 512:(tt % 2) * D + (sl + 1) * 512],
                                   start=(tt == 0), stop=(tt == NTT - 1))
                ins.then_inc(p.h["mm"], 1)
            p.emit("tensor", fvs)
            vs_mm_marks[tt] = p.bump("mm")
        p.wait("vector", "mm", vs_mm_marks[NTT - 1])

        def fvsum(v):
            v.tensor_copy(vsum[0:1, 0:512], psv[0][0:1, :])
            v.tensor_copy(vsum[0:1, 512:1024], psv[1][0:1, :]).then_inc(p.h["ve"], 1)
        p.emit("vector", fvsum)
        vsum_ve = p.bump("ve")

        for x in range(4):
            p.wait("gpsimd", "ks%d" % x, p.mark("ks%d" % x))
        for x in range(2):
            p.wait("gpsimd", "vs%d" % x, p.mark("vs%d" % x))
        p.wait("gpsimd", "ve", vsum_ve)
        for c in range(NDC):
            p.dma("gpsimd", kx_in.ap()[c, :, 0:256],
                  bass.AP(K_hbm, (c * 128) * HKN + 256, [[HKN, 128], [1, 256]]), "a0x")
            p.dma("gpsimd", kx_in.ap()[c, :, 256:512],
                  bass.AP(K_hbm, (c * 128) * HKN + 4096, [[HKN, 128], [1, 256]]), "a0x")
        p.dma("gpsimd", vx_in.ap()[0:256, :],
              bass.AP(V_hbm, 256 * D, [[D, 256], [1, D]]), "a0x")
        p.dma("gpsimd", vx_in.ap()[256:512, :],
              bass.AP(V_hbm, 4096 * D, [[D, 256], [1, D]]), "a0x")
        xm = p.dma("gpsimd", vs_in.ap()[:, :], vsum[0:1, :], "a0x")
        p.wait("gpsimd", "a0x", xm)
        rg = [[2 * g, 2 * g + 1] for g in range(4)]

        def fcc(g):
            h = p.h["cc"]
            g.collective_compute("AllGather", ALU.bypass, replica_groups=rg,
                                 ins=[kx_in.ap()], outs=[kx_out.ap()]).then_inc(h, 1)
            g.collective_compute("AllGather", ALU.bypass, replica_groups=rg,
                                 ins=[vx_in.ap()], outs=[vx_out.ap()]).then_inc(h, 1)
            g.collective_compute("AllReduce", ALU.add, replica_groups=rg,
                                 ins=[vs_in.ap()], outs=[vs_out.ap()]).then_inc(h, 1)
        p.emit("gpsimd", fcc)
        cc_done = p.bump("cc", 3)

        self.pop_to(scpW)
        hst = self.sb("a0_hst", [128, NDC * 256], BF16)
        vhst = self.sb("a0_vhst", [128, 2 * D], BF16)
        vsg = self.sb("a0_vsg", [1, D], F32)
        psh = self.ps("a0_psh", [128, 512])
        p.wait("sync", "cc", cc_done)
        for side in range(2):
            smark = p.mark("a0h1")
            hl = 0
            for c in range(NDC):
                hl = p.dma("sync", hst[:, c * 256:(c + 1) * 256],
                           kx_out.ap()[side, c, :, (1 - side) * 256:(1 - side) * 256 + 256], "a0h0")
            p.wait("vector", "a0h0", hl)
            p.wait("vector", "ve", hf_ve)
            p.wait("vector", "a0h1", smark)
            hv = 0
            for c in range(NDC):
                def fh(v, c=c, side=side):
                    v.tensor_scalar(hst[:, c * 256:(c + 1) * 256],
                                    hst[:, c * 256:(c + 1) * 256],
                                    hfc[:, side:side + 1], None,
                                    ALU.mult).then_inc(p.h["ve"], 1)
                p.emit("vector", fh)
                hv = p.bump("ve")
            p.wait("sync", "ve", hv)
            base = 0 if side == 0 else 4352
            for c in range(NDC):
                p.dma("sync", bass.AP(K_hbm, (c * 128) * HKN + base, [[HKN, 128], [1, 256]]),
                      hst[:, c * 256:(c + 1) * 256], "a0h1")
            smark = p.mark("a0h1")
            lds = 0
            for r2 in range(2):
                lds = p.dma("sync", vhst[:, r2 * D:(r2 + 1) * D],
                            vx_out.ap()[side, (1 - side) * 256 + r2 * 128:(1 - side) * 256 + (r2 + 1) * 128, :], "a0h0")
            p.wait("vector", "a0h0", lds)
            p.wait("vector", "a0h1", smark)
            vvm = 0
            for r2 in range(2):
                def fvh(v, r2=r2, side=side):
                    v.tensor_scalar(vhst[:, r2 * D:(r2 + 1) * D],
                                    vhst[:, r2 * D:(r2 + 1) * D],
                                    hfc[:, side:side + 1], None,
                                    ALU.mult).then_inc(p.h["ve"], 1)
                p.emit("vector", fvh)
                vvm = p.bump("ve")
            p.wait("sync", "ve", vvm)
            for r2 in range(2):
                p.dma("sync", bass.AP(V_hbm, (base + r2 * 128) * D, [[D, 128], [1, D]]),
                      vhst[:, r2 * D:(r2 + 1) * D], "a0h1")
        vg = p.dma("sync", vsg[:, :], vs_out.ap()[:, :], "a0h0")
        self.emit_bcast(vsg[0:1, :], Vb[:, :], D, psh,
                        extra_tensor_wait=("a0h0", vg))
        vb_ve = p.mark("ve")

        self.pop_to(scp)

        halo_done = p.mark("a0h1")
        KW = self.sb("a0_KW", [128, 2 * NDC * 640], BF16)
        VW = self.sb("a0_VW", [128, 2 * 5 * D], BF16)
        PW = self.sb("a0_PW", [128, 640], F32)
        WT = self.sb("a0_WT", [128, 640], BF16)
        WTT = self.sb("a0_WTT", [128, 5 * 128], BF16)
        TMP = self.sb("a0_TMP", [128, D], F32)
        OUT = self.sb("a0_OUT", [128, D], BF16)
        asc = self.sb("a0_asc", [128, 6], F32)
        xon = self.sb("a0_xo", [128, 4 * 128], F32)
        xnn = self.sb("a0_xn", [128, 4 * 128], F32)
        ps_s = self.ps("a0_ps_s", [128, 640])
        ps_o = [self.ps("a0_ps_o%d" % k2, [128, D]) for k2 in range(2)]
        ps_t = [self.ps("a0_ps_t%d" % k2, [128, 128], BF16) for k2 in range(2)]

        NQT = NT // 128
        kw_m = {}
        vw_m = {}
        out_mm = {}
        sc_rel = {}
        ev_rel = {}
        xo_m = {}
        st_m = {}
        xa_rel = {}
        pst_rel = [0, 0]

        def emit_win_load(oi):
            qt = QT_ORDER[oi]
            slot = oi % 2
            if oi >= 2:
                p.wait("gpsimd", "mm", out_mm[QT_ORDER[oi - 2]])
            if qt in HALO_QTS:
                p.wait("gpsimd", "a0h1", halo_done)
            else:
                for x2 in range(4):
                    p.wait("gpsimd", "ks%d" % x2, kv_marks[0][x2])
                for x2 in range(2):
                    p.wait("gpsimd", "vs%d" % x2, kv_marks[1][x2])
            w0 = qt * 128
            v = 0
            for c in range(NDC):
                v = p.dma("gpsimd",
                          KW[:, slot * NDC * 640 + c * 640:slot * NDC * 640 + (c + 1) * 640],
                          bass.AP(K_hbm, (c * 128) * HKN + w0, [[HKN, 128], [1, 640]]),
                          "kw%d" % slot)
            kw_m[qt] = v
            v = 0
            for kt in range(5):
                v = p.dma("gpsimd",
                          VW[:, slot * 5 * D + kt * D:slot * 5 * D + (kt + 1) * D],
                          bass.AP(V_hbm, (w0 + kt * 128) * D, [[D, 128], [1, D]]),
                          "vw%d" % slot)
            vw_m[qt] = v

        kv_marks = ([p.mark("ks%d" % x) for x in range(4)],
                    [p.mark("vs%d" % x) for x in range(2)])
        HALO_QTS = {0, 1, 30, 31}
        QT_ORDER = [q2 for q2 in range(2, 30)] + [0, 1, 30, 31]
        p.wait("tensor", "ve", vb_ve)

        def vop(fn):
            def g2(v, fn=fn):
                fn(v).then_inc(p.h["ve"], 1)
            p.emit("vector", g2)
            vv = p.bump("ve")
            p.wait("vector", "ve", vv)
            return vv

        emit_win_load(0)
        emit_win_load(1)
        prev_qt = [None]
        for oi in range(NQT):
            qt = QT_ORDER[oi]
            slot = oi % 2
            p.waits("tensor", **{("kw%d" % slot): kw_m[qt]})
            if prev_qt[0] is not None:
                p.wait("tensor", "sc", sc_rel[prev_qt[0]])

            def fsc(t, slot=slot, qt=qt):
                ins = None
                for c in range(NDC):
                    base = slot * NDC * 640 + c * 640
                    t.matmul(ps_s[:, 0:512],
                             Q_fm[:, c * NT + qt * 128:c * NT + (qt + 1) * 128],
                             KW[:, base:base + 512],
                             start=(c == 0), stop=(c == NDC - 1))
                for c in range(NDC):
                    base = slot * NDC * 640 + c * 640
                    ins = t.matmul(ps_s[:, 512:640],
                                   Q_fm[:, c * NT + qt * 128:c * NT + (qt + 1) * 128],
                                   KW[:, base + 512:base + 640],
                                   start=(c == 0), stop=(c == NDC - 1))
                ins.then_inc(p.h["mm"], 1)
            p.emit("tensor", fsc)
            sc_mm = p.bump("mm")

            p.wait("vector", "mm", sc_mm)
            vop(lambda v: v.tensor_reduce(asc[:, 0:1], ps_s[:, :], AXL.X, ALU.max,
                                          negate=True))
            m1n_ve = vop(lambda v: v.tensor_scalar_min(asc[:, 0:1], asc[:, 0:1], 0.0))
            p.wait("scalar", "ve", m1n_ve)

            def fexp(s2):
                s2.activation(asc[:, 1:2], asc[:, 0:1], AF.Exp)
                s2.activation(PW[:, :], ps_s[:, :], AF.Exp,
                              bias=asc[:, 0:1]).then_inc(p.h["sc"], 1)
            p.emit("scalar", fexp)
            sc_rel[qt] = p.bump("sc")
            p.wait("vector", "sc", sc_rel[qt])
            p.wait("vector", "mm", getattr(self, "_a0_last_wtr", 0))
            wt_ve = vop(lambda v: v.scalar_tensor_tensor(WT[:, :], PW[:, :], asc[:, 1:2],
                                                 self.bandm[:, :], ALU.subtract, ALU.mult,
                                                 accum_out=asc[:, 2:3]))
            vop(lambda v: v.scalar_tensor_tensor(asc[:, 3:4], asc[:, 1:2], float(SEQ),
                                                 asc[:, 2:3], ALU.mult, ALU.add))
            vop(lambda v: v.reciprocal(asc[:, 4:5], asc[:, 3:4]))

            p.wait("tensor", "ve", wt_ve)
            for kt in range(5):
                p.wait("tensor", "ve", pst_rel[kt % 2])

                def ftr(t, kt=kt):
                    t.transpose(ps_t[kt % 2][:, :], WT[:, kt * 128:(kt + 1) * 128],
                                self.ident_bf[:, :]).then_inc(p.h["mm"], 1)
                p.emit("tensor", ftr)
                mtv = p.bump("mm")
                p.wait("vector", "mm", mtv)

                def fwc(v, kt=kt):
                    v.tensor_copy(WTT[:, kt * 128:(kt + 1) * 128],
                                  ps_t[kt % 2][:, :]).then_inc(p.h["ve"], 1)
                p.emit("vector", fwc)
                pst_rel[kt % 2] = p.bump("ve")
                if kt == 4:
                    self._a0_last_wtr = mtv

            p.waits("tensor", **{("vw%d" % slot): vw_m[qt]})
            p.wait("tensor", "ve", pst_rel[0])
            p.wait("tensor", "ve", pst_rel[1])
            if oi >= 2:
                p.wait("tensor", "ve", ev_rel[QT_ORDER[oi - 2]])
            po = ps_o[oi % 2]

            def fo(t, slot=slot, po=po):
                ins = None
                for kt in range(5):
                    for sl in range(2):
                        ins = t.matmul(po[:, sl * 512:(sl + 1) * 512],
                                       WTT[:, kt * 128:(kt + 1) * 128],
                                       VW[:, slot * 5 * D + kt * D + sl * 512:slot * 5 * D + kt * D + (sl + 1) * 512],
                                       start=(kt == 0), stop=(kt == 4))
                ins.then_inc(p.h["mm"], 1)
            p.emit("tensor", fo)
            out_mm[qt] = p.bump("mm")
            if oi + 2 < NQT:
                emit_win_load(oi + 2)

            p.wait("vector", "mm", out_mm[qt])
            p.wait("vector", "ve", vb_ve)
            p.wait("vector", "mm", getattr(self, "_a0_last_ot", 0))
            prev_qt[0] = qt
            vop(lambda v, po=po: v.scalar_tensor_tensor(TMP[:, :], Vb[:, :], asc[:, 1:2],
                                                        po[:, :], ALU.mult, ALU.add))
            ev_rel[qt] = vop(lambda v: v.tensor_scalar(OUT[:, :], TMP[:, :], asc[:, 4:5],
                                                       None, ALU.mult))

            for dc in range(NDC):
                idx = oi * NDC + dc
                sl4 = idx % 4
                if idx >= 4:
                    p.wait("sync", "ve", xa_rel[idx - 4])
                xo_m[idx] = p.dma("sync", xon[:, sl4 * 128:(sl4 + 1) * 128],
                                  src.ap()[dc * 128:(dc + 1) * 128, qt * 128:(qt + 1) * 128],
                                  "xoa%d" % sl4)
                p.wait("tensor", "ve", pst_rel[dc % 2])
                p.wait("tensor", "ve", ev_rel[qt])

                def fot(t, dc=dc):
                    t.transpose(ps_t[dc % 2][:, :], OUT[:, dc * 128:(dc + 1) * 128],
                                self.ident_bf[:, :]).then_inc(p.h["mm"], 1)
                p.emit("tensor", fot)
                mo = p.bump("mm")
                p.wait("vector", "mm", mo)
                p.wait("vector", "xoa%d" % sl4, xo_m[idx])
                if idx >= 4:
                    p.wait("vector", "sna%d" % sl4, st_m[idx - 4])

                def fxa(v, dc=dc, sl4=sl4):
                    v.tensor_add(xnn[:, sl4 * 128:(sl4 + 1) * 128],
                                 ps_t[dc % 2][:, :],
                                 xon[:, sl4 * 128:(sl4 + 1) * 128]).then_inc(p.h["ve"], 1)
                p.emit("vector", fxa)
                xa_rel[idx] = p.bump("ve")
                pst_rel[dc % 2] = xa_rel[idx]
                p.wait("sync", "ve", xa_rel[idx])
                st_m[idx] = p.dma("sync",
                                  dst.ap()[dc * 128:(dc + 1) * 128, qt * 128:(qt + 1) * 128],
                                  xnn[:, sl4 * 128:(sl4 + 1) * 128], "sna%d" % sl4)
                if dc == NDC - 1:
                    self._a0_last_ot = mo
                if dc == NDC - 1:
                    self._a1_last_ot = mo

    _BKT_TH = [1, 2, 3, 4, 5, 6, 7, 8, 9, 10, 11, 12, 13, 14, 15, 16, 19, 21, 24, 27, 31, 35, 40, 46, 52, 59, 67, 77, 87, 99, 113]
    _BKT_BJ = [1, 2, 3, 4, 5, 6, 7, 8, 9, 10, 11, 12, 13, 14, 15, 16, 17, 18, 19, 20, 21, 22, 23, 24, 25, 26, 27, 28, 29, 30, 31]
    _BKT_BP = [0, 1, 2, 3, 4, 5, 6, 7, 8, 9, 10, 11, 12, 13, 14, 15, 16, 17, 18, 19, 20, 21, 22, 23, 24, 25, 26, 27, 28, 29, 30]

    def _attn1(self, src, dst):
        p = self.p
        i = self.inputs
        nc = self.nc
        MU = math.sqrt(0.5)
        SIG = math.sqrt(0.25 * math.pi)
        ERF_SC = 1.0 / (SIG * math.sqrt(2.0))
        ERF_B = -MU / (SIG * math.sqrt(2.0))

        kx_in = nc.dram_tensor(self._un("a1_kxi"), [128, NT], BF16, kind="Internal")
        kx_out = nc.dram_tensor(self._un("a1_kxo"), [2, 128, NT], BF16, kind="Internal")
        vx_in = nc.dram_tensor(self._un("a1_vxi"), [NT, D], BF16, kind="Internal")
        vx_out = nc.dram_tensor(self._un("a1_vxo"), [2, NT, D], BF16, kind="Internal")

        for snm in (["a1wl", "a1x", "a1kv", "a1h"]
                    + ["xoa%d" % x for x in range(4)] + ["sna%d" % x for x in range(4)]
                    + ["vs%d" % x for x in range(2)] + ["ks%d" % x for x in range(2)]):
            p.sem(snm)

        Q_fm = self.sb("a1_q", [128, NT], BF16)
        K_all = self.sb("a1_k", [128, SEQ], BF16)
        BANK = self.sb("a1_bank", [128, 6 * 128], F32)
        erfb = self.sb("a1_erfb", [128, 1], F32)
        hfc = self.sb("a1_hf", [128, 2], F32)

        scp = self.scope()
        xt, xt_ve = self.emit_xt_full(src)
        wqks = self.sb("a1_wqk", [128, NDC * 128], BF16)
        wvs = self.sb("a1_wv", [128, NDC * D], BF16)
        gbc = self.sb("a1_gbc", [128, 4], F32)
        bqkc = self.sb("a1_bqk", [128, 1], F32)
        bvr = self.sb("a1_bvr", [1, D], BF16)
        qkst = self.sb("a1_qkst", [128, 2 * 512], F32)
        kstg = self.sb("a1_kstg", [128, 2 * 512], BF16)
        vst = self.sb("a1_vst", [128, 2 * D], BF16)
        psp = [self.ps("a1_pp%d" % k2, [128, 512]) for k2 in range(4)]

        p.waits("gpsimd", mm=p.mark("mm"), sc=p.mark("sc"), ve=p.mark("ve"))
        p.dma("gpsimd", wqks[:, :], i["wqkr"].ap()[:, :, :], "a1wl")
        p.dma("gpsimd", wvs[:, :], i["wvr"].ap()[:, :, :], "a1wl")
        p.dma("gpsimd", gbc[:, :], i["gbcols"].ap()[:, :], "a1wl")
        p.dma("gpsimd", bqkc[:, :], i["bqk"].ap()[:, :], "a1wl")
        lw = p.dma("gpsimd", bvr[:, :], i["bvrow"].ap()[:, :], "a1wl")
        p.waits("tensor", a1wl=lw, ve=xt_ve, g2=self.init_marks["g2"],
                ld=self.init_marks["ld"], ldg=self.init_marks["ldg"])
        p.waits("scalar", a1wl=lw)
        p.waits("vector", a1wl=lw)

        hh1 = self.emit_bcast(self.hf_c[0:1, 0:1], hfc[:, 0:1], 1, psp[0])
        p.wait("vector", "ve", hh1)

        def fihf(v):
            v.tensor_scalar(hfc[:, 1:2], hfc[:, 0:1], -1.0, 1.0,
                            ALU.mult, ALU.add).then_inc(p.h["ve"], 1)
        p.emit("vector", fihf)
        p.bump("ve")

        def fmub(g):
            g.memset(erfb[:, :], ERF_B).then_inc(p.h["g2"], 1)
        p.emit("gpsimd", fmub)
        mub_g2 = p.bump("g2")

        qk_sc = {}
        qk_ve = {}
        k_st = {}
        for ts in range(NT // 512):
            pp = psp[ts % 2]
            slot = ts % 2
            if ts >= 2:
                p.wait("tensor", "ve", qk_ve[ts - 2])
            elif ts == 0:
                p.wait("tensor", "ve", hh1)

            def fqk(t, ts=ts, pp=pp):
                ins = None
                for c in range(NDC):
                    ins = t.matmul(pp[:, :],
                                   wqks[:, c * 128:(c + 1) * 128],
                                   xt[:, c * NT + ts * 512:c * NT + ts * 512 + 512],
                                   start=(c == 0), stop=(c == NDC - 1))
                ins.then_inc(p.h["mm"], 1)
            p.emit("tensor", fqk)
            mv = p.bump("mm")
            p.wait("scalar", "mm", mv)
            if ts >= 2:
                p.wait("scalar", "ve", qk_ve[ts - 2])

            def fqke(s2, slot=slot, pp=pp):
                s2.activation(qkst[:, slot * 512:(slot + 1) * 512], pp[:, :],
                              AF.Silu, bias=bqkc[:, 0:1]).then_inc(p.h["sc"], 1)
            p.emit("scalar", fqke)
            qk_sc[ts] = p.bump("sc")
            p.wait("vector", "sc", qk_sc[ts])
            if ts >= 2:
                p.wait("vector", "ks%d" % slot, k_st[ts - 2])

            def fqv(v, ts=ts, slot=slot):
                v.tensor_scalar(Q_fm[:, ts * 512:(ts + 1) * 512],
                                qkst[:, slot * 512:(slot + 1) * 512],
                                gbc[:, 0:1], gbc[:, 1:2], ALU.mult, ALU.add)
                v.tensor_scalar(kstg[:, slot * 512:(slot + 1) * 512],
                                qkst[:, slot * 512:(slot + 1) * 512],
                                gbc[:, 2:3], gbc[:, 3:4], ALU.mult,
                                ALU.add).then_inc(p.h["ve"], 1)
            p.emit("vector", fqv)
            qk_ve[ts] = p.bump("ve")
            p.wait("sync", "ve", qk_ve[ts])
            k_st[ts] = p.dma("sync", kx_in.ap()[:, ts * 512:(ts + 1) * 512],
                             kstg[:, slot * 512:(slot + 1) * 512], "ks%d" % slot)

        v_ve = {}
        v_st = {}
        NTT = NT // 128
        for tt in range(NTT):
            pp2 = [psp[2], psp[3]] if tt % 2 == 0 else [psp[0], psp[1]]
            if tt >= 2:
                p.wait("tensor", "sc", v_ve[tt - 2])
            elif tt == 1:
                p.wait("tensor", "ve", qk_ve[NT // 512 - 1])

            def fv(t, tt=tt, pp2=pp2):
                ins = None
                for sl in range(2):
                    for c in range(NDC):
                        t.matmul(pp2[sl][:, :],
                                 xt[:, c * NT + tt * 128:c * NT + (tt + 1) * 128],
                                 wvs[:, c * D + sl * 512:c * D + sl * 512 + 512],
                                 start=(c == 0), stop=False)
                    ins = t.matmul(pp2[sl][:, :], self.ones_row[0:1, 0:128],
                                   bvr[0:1, sl * 512:(sl + 1) * 512],
                                   start=False, stop=True)
                ins.then_inc(p.h["mm"], 1)
            p.emit("tensor", fv)
            mv = p.bump("mm")
            p.wait("scalar", "mm", mv)
            if tt >= 2:
                p.wait("scalar", "vs%d" % (tt % 2), v_st[tt - 2])

            def fve(s2, tt=tt, pp2=pp2):
                s2.activation(vst[:, (tt % 2) * D:(tt % 2) * D + 512],
                              pp2[0][:, :], AF.Silu)
                s2.activation(vst[:, (tt % 2) * D + 512:(tt % 2) * D + 1024],
                              pp2[1][:, :], AF.Silu).then_inc(p.h["sc"], 1)
            p.emit("scalar", fve)
            v_ve[tt] = p.bump("sc")
            p.wait("sync", "sc", v_ve[tt])
            v_st[tt] = p.dma("sync",
                             bass.AP(vx_in, (tt * 128) * D, [[D, 128], [1, D]]),
                             vst[:, (tt % 2) * D:(tt % 2) * D + D], "vs%d" % (tt % 2))


        for x in range(2):
            p.wait("gpsimd", "ks%d" % x, p.mark("ks%d" % x))
            p.wait("gpsimd", "vs%d" % x, p.mark("vs%d" % x))
        rg = [[2 * g, 2 * g + 1] for g in range(4)]

        def fcc(g):
            h = p.h["cc"]
            g.collective_compute("AllGather", ALU.bypass, replica_groups=rg,
                                 ins=[kx_in.ap()], outs=[kx_out.ap()]).then_inc(h, 1)
            g.collective_compute("AllGather", ALU.bypass, replica_groups=rg,
                                 ins=[vx_in.ap()], outs=[vx_out.ap()]).then_inc(h, 1)
        p.emit("gpsimd", fcc)
        cc_done = p.bump("cc", 2)

        self.pop_to(scp)
        p.wait("sync", "cc", cc_done)
        ka = p.dma("sync", K_all[:, 0:NT], kx_out.ap()[0, :, :], "a1h")
        ka = p.dma("sync", K_all[:, NT:SEQ], kx_out.ap()[1, :, :], "a1h")
        V_all = self.sb("a1_vall", [128, (SEQ // 128) * D], BF16)
        va = p.dma("sync", V_all[:, :],
                   bass.AP(vx_out, 0, [[D, 128], [128 * D, SEQ // 128], [1, D]]), "a1h")

        scbk = self.scope()
        pspb = [self.ps("a1_pb%d" % k2, [128, 512]) for k2 in range(2)]
        relsc = self.sb("a1_rel", [1, 3 * NUM_BUCKETS], F32)
        drow = self.sb("a1_drow", [1, NUM_BUCKETS + 2], F32)
        dcols = self.sb("a1_dcols", [128, NUM_BUCKETS + 2], F32)
        dtile = self.sb("a1_dt", [128, 2 * 128], F32)
        base6 = self.sb("a1_b6", [128, 6 * 128], F32)
        ohj = self.sb("a1_ohj", [NUM_BUCKETS, 2 * NUM_BUCKETS], F32)

        ldr = p.dma("sync", relsc[0:1, 0:NUM_BUCKETS], i["relb"].ap()[:, :], "a1h")
        ldo = p.dma("sync", ohj[:, :], self.bktoh_c.ap()[:, :], "a1h")
        relc = self.sb("a1_relc", [NUM_BUCKETS, 1], F32)
        ldc = p.dma("sync", relc[:, :], bass.AP(i["relb"], 0, [[1, NUM_BUCKETS], [1, 1]]), "a1h")

        NJ2 = len(self._BKT_TH)
        p.wait("tensor", "a1h", ldc)
        pb = pspb[0]

        def fga(t):
            t.matmul(pb[0:1, 0:2 * NUM_BUCKETS], relc[:, 0:1], ohj[:, :],
                     start=True, stop=True).then_inc(p.h["mm"], 1)
        p.emit("tensor", fga)
        mga = p.bump("mm")
        p.wait("vector", "mm", mga)
        p.wait("vector", "a1h", ldc)

        def fgb(v):
            v.tensor_copy(relsc[0:1, NUM_BUCKETS:3 * NUM_BUCKETS],
                          pb[0:1, 0:2 * NUM_BUCKETS]).then_inc(p.h["ve"], 1)
        p.emit("vector", fgb)
        vv = p.bump("ve")
        p.wait("vector", "ve", vv)
        SQ128 = float(math.sqrt(128.0))

        def fdr1(v):
            v.tensor_sub(drow[0:1, 0:NJ2], relsc[0:1, NUM_BUCKETS:NUM_BUCKETS + NJ2],
                         relsc[0:1, 2 * NUM_BUCKETS:2 * NUM_BUCKETS + NJ2]).then_inc(p.h["ve"], 1)
        p.emit("vector", fdr1)
        vv = p.bump("ve")
        p.wait("vector", "ve", vv)

        def fdr2(v):
            v.tensor_scalar(drow[0:1, 0:NJ2], drow[0:1, 0:NJ2], SQ128, None,
                            ALU.mult).then_inc(p.h["ve"], 1)
        p.emit("vector", fdr2)
        vv = p.bump("ve")
        p.wait("vector", "ve", vv)

        def fb0(v):
            v.tensor_scalar(drow[0:1, NJ2:NJ2 + 1], relsc[0:1, 0:1], SQ128, None, ALU.mult)
            v.tensor_scalar(drow[0:1, NJ2 + 1:NJ2 + 2],
                            relsc[0:1, NUM_BUCKETS - 1:NUM_BUCKETS], SQ128, None,
                            ALU.mult).then_inc(p.h["ve"], 1)
        p.emit("vector", fb0)
        dr_ve = p.bump("ve")
        NJx = NJ2 + 2
        bc_ve = self.emit_bcast(drow[0:1, 0:NJx], dcols[:, 0:NJx],
                                NJx, pspb[1], extra_tensor_wait=("ve", dr_ve))

        def fio(g):
            h = p.h["g2"]
            g.iota(dtile[:, 0:128], [[1, 128]], base=0, channel_multiplier=-1,
                   allow_small_or_imprecise_dtypes=True).then_inc(h, 1)
            g.iota(dtile[:, 128:256], [[1, 128]], base=128, channel_multiplier=-1,
                   allow_small_or_imprecise_dtypes=True).then_inc(h, 1)
        p.emit("gpsimd", fio)
        io_g2 = p.bump("g2", 2)

        p.wait("vector", "g2", io_g2)
        p.wait("vector", "ve", bc_ve)
        for w2 in range(2):
            def finit(v, w2=w2):
                v.tensor_scalar(base6[:, w2 * 128:(w2 + 1) * 128],
                                dtile[:, w2 * 128:(w2 + 1) * 128], -1e30,
                                dcols[:, NJ2:NJ2 + 1], ALU.is_ge,
                                ALU.mult).then_inc(p.h["ve"], 1)
            p.emit("vector", finit)
            vv = p.bump("ve")
            p.wait("vector", "ve", vv)
            for j in range(NJ2):
                def facc1(v, w2=w2, j=j):
                    v.tensor_scalar(base6[:, 512 + 128 * w2:512 + 128 * (w2 + 1)],
                                    dtile[:, w2 * 128:(w2 + 1) * 128],
                                    float(self._BKT_TH[j]), dcols[:, j:j + 1],
                                    ALU.is_ge, ALU.mult).then_inc(p.h["ve"], 1)
                p.emit("vector", facc1)
                vv = p.bump("ve")
                p.wait("vector", "ve", vv)

                def facc2(v, w2=w2):
                    v.tensor_add(base6[:, w2 * 128:(w2 + 1) * 128],
                                 base6[:, w2 * 128:(w2 + 1) * 128],
                                 base6[:, 512 + 128 * w2:512 + 128 * (w2 + 1)]).then_inc(p.h["ve"], 1)
                p.emit("vector", facc2)
                vv = p.bump("ve")
                p.wait("vector", "ve", vv)
        def fca1(v):
            v.tensor_scalar(base6[:, 512:640], dtile[:, 0:128], 0.0, BIG_NEG,
                            ALU.is_lt, ALU.mult).then_inc(p.h["ve"], 1)
        p.emit("vector", fca1)
        vv = p.bump("ve")
        p.wait("vector", "ve", vv)

        def fca2(v):
            v.tensor_add(base6[:, 0:128], base6[:, 0:128],
                         base6[:, 512:640]).then_inc(p.h["ve"], 1)
        p.emit("vector", fca2)
        vv = p.bump("ve")
        p.wait("vector", "ve", vv)

        def fct(v):
            v.tensor_scalar(base6[:, 256:384], self.ident[:, :], 0.0,
                            dcols[:, NJ2 + 1:NJ2 + 2], ALU.mult, ALU.add).then_inc(p.h["ve"], 1)
        p.emit("vector", fct)
        vv = p.bump("ve")

        def fbn(g):
            g.memset(base6[:, 384:512], BIG_NEG).then_inc(p.h["g2"], 1)
        p.emit("gpsimd", fbn)
        bn_g2 = p.bump("g2")
        p.wait("vector", "g2", bn_g2)
        p.wait("vector", "ve", vv)

        pairs = [(3, 0), (3, 1), (3, 2), (0, 2), (1, 2), (2, 2)]
        DIAG0, C1280, CONST0, BIGN0 = 0, 128, 256, 384
        offs = [DIAG0, C1280, CONST0, BIGN0]
        for si, (a0, b0) in enumerate(pairs):
            def fsel1(v, si=si, a0=a0):
                v.tensor_scalar(BANK[:, si * 128:(si + 1) * 128],
                                base6[:, offs[a0]:offs[a0] + 128],
                                hfc[:, 1:2], None, ALU.mult).then_inc(p.h["ve"], 1)
            p.emit("vector", fsel1)
            vv = p.bump("ve")
            p.wait("vector", "ve", vv)

            def fsel2(v, b0=b0):
                v.tensor_scalar(base6[:, 640:768],
                                base6[:, offs[b0]:offs[b0] + 128],
                                hfc[:, 0:1], None, ALU.mult).then_inc(p.h["ve"], 1)
            p.emit("vector", fsel2)
            vv = p.bump("ve")
            p.wait("vector", "ve", vv)

            def fsel3(v, si=si):
                v.tensor_add(BANK[:, si * 128:(si + 1) * 128],
                             BANK[:, si * 128:(si + 1) * 128],
                             base6[:, 640:768]).then_inc(p.h["ve"], 1)
            p.emit("vector", fsel3)
            vv = p.bump("ve")
            p.wait("vector", "ve", vv)
        bank_ve = p.mark("ve")

        self.pop_to(scbk)

        SIMF = self.sb("a1_simf", [128, 8 * 128], F32)
        PSI = self.sb("a1_psi", [128, 8 * 128], F32)
        WTB = self.sb("a1_wtb", [128, 8 * 128], BF16)
        OUT = self.sb("a1_out", [128, D], BF16)
        xon = self.sb("a1_xo", [128, 4 * 128], F32)
        xnn = self.sb("a1_xn", [128, 4 * 128], F32)
        ps_sim = [self.ps("a1_ps_s%d" % k2, [128, 128]) for k2 in range(2)]
        ps_o = [self.ps("a1_ps_o%d" % k2, [128, D]) for k2 in range(2)]
        ps_t = [self.ps("a1_ps_t%d" % k2, [128, 128], BF16) for k2 in range(2)]

        p.waits("tensor", a1h=p.mark("a1h"), ve=bank_ve, g2=p.mark("g2"))
        p.waits("scalar", g2=mub_g2)

        NQT = NT // 128
        LA = 6
        xo_m = {}
        st_m = {}
        xa_rel = {}
        pst_rel = [0, 0]
        ev_rel = {}

        def vop(fn):
            def g2f(v, fn=fn):
                fn(v).then_inc(p.h["ve"], 1)
            p.emit("vector", g2f)
            vv2 = p.bump("ve")
            return vv2

        sim_last = [0, 0]
        for qt in range(NQT):
            extent = 33 + qt
            po = ps_o[qt % 2]
            sim_rel = {}
            wtb_ve = {}
            psi_sc = {}

            def emit_sim(kt):
                r = extent - 1 - kt
                si = 0 if r == 0 else 1 if r == 1 else 2 if r <= 31 else \
                     3 if r == 32 else 4 if r == 33 else 5
                pss = ps_sim[kt % 2]
                p.wait("tensor", "ve", sim_rel[kt - 2] if kt >= 2 else sim_last[kt % 2])

                def fsim(t, kt=kt, pss=pss, qt=qt):
                    t.matmul(pss[:, :], K_all[:, kt * 128:(kt + 1) * 128],
                             Q_fm[:, qt * 128:(qt + 1) * 128],
                             start=True, stop=True).then_inc(p.h["mm"], 1)
                p.emit("tensor", fsim)
                mv2 = p.bump("mm")
                p.wait("vector", "mm", mv2)
                st4 = kt % 8
                if kt >= 8:
                    p.wait("vector", "sc", psi_sc[kt - 8])

                def fadd(v, st4=st4, si=si, pss=pss):
                    v.tensor_add(SIMF[:, st4 * 128:(st4 + 1) * 128], pss[:, :],
                                 BANK[:, si * 128:(si + 1) * 128]).then_inc(p.h["ve"], 1)
                p.emit("vector", fadd)
                sim_rel[kt] = p.bump("ve")
                sim_last[kt % 2] = sim_rel[kt]
                p.wait("scalar", "ve", sim_rel[kt])
                if kt >= 8:
                    p.wait("scalar", "ve", wtb_ve[kt - 8])

                def ferf(s2, st4=st4):
                    s2.activation(PSI[:, st4 * 128:(st4 + 1) * 128],
                                  SIMF[:, st4 * 128:(st4 + 1) * 128],
                                  getattr(AF, ERF_FN_NAME),
                                  bias=erfb[:, 0:1], scale=ERF_SC).then_inc(p.h["sc"], 1)
                p.emit("scalar", ferf)
                psi_sc[kt] = p.bump("sc")
                p.wait("vector", "sc", psi_sc[kt])
                if kt >= 8:
                    p.wait("vector", "mm", out_done[kt - 8])

                def fwtb(v, st4=st4):
                    v.tensor_scalar(WTB[:, st4 * 128:(st4 + 1) * 128],
                                    PSI[:, st4 * 128:(st4 + 1) * 128],
                                    0.5, 0.5, ALU.mult, ALU.add).then_inc(p.h["ve"], 1)
                p.emit("vector", fwtb)
                wtb_ve[kt] = p.bump("ve")

            out_done = {}

            def emit_out(kt):
                st4 = kt % 8
                p.wait("tensor", "ve", wtb_ve[kt])
                if kt == 0 and qt >= 2:
                    p.wait("tensor", "ve", ev_rel[qt - 2])

                def fo(t, kt=kt, st4=st4, po=po, extent=extent):
                    ins = None
                    for sl in range(2):
                        ins = t.matmul(po[:, sl * 512:(sl + 1) * 512],
                                       WTB[:, st4 * 128:(st4 + 1) * 128],
                                       V_all[:, kt * D + sl * 512:kt * D + (sl + 1) * 512],
                                       start=(kt == 0), stop=(kt == extent - 1))
                    ins.then_inc(p.h["mm"], 1)
                p.emit("tensor", fo)
                out_done[kt] = p.bump("mm")

            for j in range(extent + LA):
                if j < extent:
                    emit_sim(j)
                if j >= LA:
                    emit_out(j - LA)

            p.wait("vector", "mm", out_done[extent - 1])
            p.wait("vector", "mm", getattr(self, "_a1_last_ot", 0))
            ev_rel[qt] = vop(lambda v, po=po: v.tensor_copy(OUT[:, :], po[:, :]))
            p.wait("vector", "ve", ev_rel[qt])

            for dc in range(NDC):
                idx = qt * NDC + dc
                sl4 = idx % 4
                if idx >= 4:
                    p.wait("sync", "ve", xa_rel[idx - 4])
                xo_m[idx] = p.dma("sync", xon[:, sl4 * 128:(sl4 + 1) * 128],
                                  src.ap()[dc * 128:(dc + 1) * 128, qt * 128:(qt + 1) * 128],
                                  "xoa%d" % sl4)
                p.wait("tensor", "ve", pst_rel[dc % 2])
                p.wait("tensor", "ve", ev_rel[qt])

                def fot(t, dc=dc):
                    t.transpose(ps_t[dc % 2][:, :], OUT[:, dc * 128:(dc + 1) * 128],
                                self.ident_bf[:, :]).then_inc(p.h["mm"], 1)
                p.emit("tensor", fot)
                mo = p.bump("mm")
                p.wait("vector", "mm", mo)
                p.wait("vector", "xoa%d" % sl4, xo_m[idx])
                if idx >= 4:
                    p.wait("vector", "sna%d" % sl4, st_m[idx - 4])

                def fxa(v, dc=dc, sl4=sl4):
                    v.tensor_add(xnn[:, sl4 * 128:(sl4 + 1) * 128],
                                 ps_t[dc % 2][:, :],
                                 xon[:, sl4 * 128:(sl4 + 1) * 128]).then_inc(p.h["ve"], 1)
                p.emit("vector", fxa)
                xa_rel[idx] = p.bump("ve")
                pst_rel[dc % 2] = xa_rel[idx]
                p.wait("sync", "ve", xa_rel[idx])
                st_m[idx] = p.dma("sync",
                                  dst.ap()[dc * 128:(dc + 1) * 128, qt * 128:(qt + 1) * 128],
                                  xnn[:, sl4 * 128:(sl4 + 1) * 128], "sna%d" % sl4)
                if dc == NDC - 1:
                    self._a1_last_ot = mo

    def emit_ffn(self, layer, src, dst):
        for q in range(MK_QUARTERS if MK_QUARTERS else NQ):
            sc0 = self.scope()
            stat = self.sb("st_stat", [1, 4 * QN], F32)
            xt = self.sb("f_xt", [128, NDC * QN], BF16)
            sc1 = self.scope()
            SM, SA, st_sc = self.emit_stats(src, q * QN, QN, stat)
            self.pop_to(sc1)
            xt_ve = self.emit_xtilde(src, xt, q * QN, QN, stat, SM, SA, st_sc)
            self.pop_to(sc1)
            if MK_DBG and not self.dbg_done[0]:
                self.p.wait("gpsimd", "ve", xt_ve)
                self.p.dma("gpsimd", self.dbg_xt.ap()[:, :], xt[:, :], "wt")
                self.p.dma("gpsimd", self.dbg_stat.ap()[:, :], stat[0:1, :], "wt")
            self._ffn_quarter(layer, src, dst, q, xt, xt_ve)
            self.pop_to(sc0)

    def _ffn_quarter(self, layer, src, dst, q, xt, xt_ve):
        p = self.p
        i = self.inputs
        t0 = q * QN
        if layer == 0:
            NE, HC, NPE = L0_E, L0_HC, L0_E + 2
        else:
            NE, HC, NPE = L1_E, L1_HC, L1_E

        cwrow = self.sb("f_cwrow", [NE, QN], F32)
        cwfm = self.sb("f_cwfm", [NE, QN], BF16)
        gate_scope = self.scope()
        gwsb = self.sb("f_gw", [128, NDC, 16], BF16)
        gbsb = self.sb("f_gb", [1, NE], BF16)
        gsc = self.sb("f_gsc", [128, 8], F32)
        gst = self.sb("f_gst", [128, NE], F32)
        gtmp = self.sb("f_gtmp", [128, NE], F32)
        gtmp2 = self.sb("f_gtmp2", [128, NE], F32)
        gp = self.ps("f_gp", [128, NE])
        gtp = self.ps("f_gtp", [NE, 128])

        p.waits("gpsimd", mm=p.mark("mm"), sc=p.mark("sc"), ve=p.mark("ve"))
        p.sem("gld")
        gw_in = (i["gate_w"] if layer == 0 else i["Wgr"])
        lw = 0
        for c in range(NDC):
            lw = p.dma("gpsimd", gwsb[:, c, 0:NE], gw_in.ap()[:, c, :], "gld")
        lb = p.dma("gpsimd", gbsb[:, :],
                   (i["gate_b"] if layer == 0 else i["bg_1"]).ap()[:, :], "gld")
        p.waits("tensor", gld=lb, ve=xt_ve, g2=self.init_marks["g2"],
                ld=self.init_marks["ld"], ldg=self.init_marks["ldg"])

        NTT = QN // 128
        tile_done = {"ve": 0, "sc": 0}
        for tt in range(NTT):
            p.wait("tensor", "ve", tile_done["ve"])
            p.wait("tensor", "sc", tile_done["sc"])

            def fmm(t, tt=tt):
                for c in range(NDC):
                    t.matmul(gp[:, :], xt[:, c * QN + tt * 128:c * QN + (tt + 1) * 128],
                             gwsb[:, c, 0:NE], start=(c == 0), stop=False)
                t.matmul(gp[:, :], self.ones_row[0:1, 0:128], gbsb[0:1, :],
                         start=False, stop=True).then_inc(p.h["mm"], 1)
            p.emit("tensor", fmm)
            mv = p.bump("mm")

            p.wait("vector", "mm", mv)

            def fm1(v):
                v.tensor_reduce(gsc[:, 0:1], gp[:, :], AXL.X, ALU.max,
                                negate=True).then_inc(p.h["ve"], 1)
            p.emit("vector", fm1)
            v1 = p.bump("ve")

            if layer == 0:
                p.wait("scalar", "ve", v1)

                def fexp(s):
                    s.activation(gst[:, :], gp[:, :], AF.Exp, bias=gsc[:, 0:1],
                                 accum_out=gsc[:, 1:2]).then_inc(p.h["sc"], 1)
                p.emit("scalar", fexp)
                s1 = p.bump("sc")
                tile_done["sc"] = s1
                p.wait("vector", "sc", s1)

                def vop(fn):
                    def g(v, fn=fn):
                        fn(v).then_inc(p.h["ve"], 1)
                    p.emit("vector", g)
                    vv = p.bump("ve")
                    p.wait("vector", "ve", vv)
                    return vv

                vop(lambda v: v.reciprocal(gsc[:, 2:3], gsc[:, 1:2]))
                vop(lambda v: v.tensor_scalar(gst[:, :], gst[:, :], gsc[:, 2:3], None, ALU.mult))
                vop(lambda v: v.reduce_max(gsc[:, 3:4], gst[:, :], axis=AXL.X))
                vop(lambda v: v.tensor_scalar(gtmp[:, :], gst[:, :], gsc[:, 3:4], BIG_NEG,
                                              ALU.is_ge, ALU.mult))
                vop(lambda v: v.tensor_add(gtmp[:, :], gtmp[:, :], gst[:, :]))
                vop(lambda v: v.reduce_max(gsc[:, 4:5], gtmp[:, :], axis=AXL.X))
                vop(lambda v: v.tensor_scalar(gtmp2[:, :], gst[:, :], gsc[:, 4:5], None, ALU.is_ge))
                vc = vop(lambda v: v.tensor_mul(gtmp[:, :], gtmp2[:, :], gst[:, :]))
            else:
                p.wait("vector", "ve", v1)

                def vop(fn):
                    def g(v, fn=fn):
                        fn(v).then_inc(p.h["ve"], 1)
                    p.emit("vector", g)
                    vv = p.bump("ve")
                    p.wait("vector", "ve", vv)
                    return vv

                vop(lambda v: v.tensor_scalar(gtmp[:, :], gp[:, :], gsc[:, 0:1], None, ALU.add))
                vop(lambda v: v.tensor_scalar(gst[:, :], gtmp[:, :], 0.0, BIG_NEG, ALU.is_ge, ALU.mult))
                vop(lambda v: v.tensor_add(gst[:, :], gst[:, :], gtmp[:, :]))
                vs = vop(lambda v: v.tensor_reduce(gsc[:, 1:2], gst[:, :], AXL.X, ALU.max))
                p.wait("scalar", "ve", vs)

                def fden(s):
                    s.activation(gsc[:, 5:6], gsc[:, 1:2], AF.Exp).then_inc(p.h["sc"], 1)
                p.emit("scalar", fden)
                p.bump("sc")

                def fexp(s):
                    s.activation(gst[:, :], gtmp[:, :], AF.Exp).then_inc(p.h["sc"], 1)
                p.emit("scalar", fexp)
                se = p.bump("sc")
                tile_done["sc"] = se
                p.wait("vector", "sc", se)

                vop(lambda v: v.tensor_scalar(gsc[:, 6:7], gsc[:, 5:6], 1.0, None, ALU.add))
                vop(lambda v: v.reciprocal(gsc[:, 7:8], gsc[:, 6:7]))
                vop(lambda v: v.tensor_scalar(gtmp2[:, :], gtmp[:, :], gsc[:, 1:2], None, ALU.is_ge))
                vop(lambda v: v.tensor_mul(gtmp2[:, :], gtmp2[:, :], gst[:, :]))
                vc = vop(lambda v: v.tensor_scalar(gtmp2[:, :], gtmp2[:, :], gsc[:, 7:8], None, ALU.mult))

            p.wait("tensor", "ve", vc)
            trsrc = gtmp if layer == 0 else gtmp2

            def ftr(t, trsrc=trsrc):
                t.transpose(gtp[:, :], trsrc[:, :], self.ident[:, :]).then_inc(p.h["mm"], 1)
            p.emit("tensor", ftr)
            mt = p.bump("mm")
            p.wait("vector", "mm", mt)

            def fcpr(v, tt=tt):
                v.tensor_copy(cwrow[:, tt * 128:(tt + 1) * 128], gtp[0:NE, :])
                v.tensor_copy(cwfm[:, tt * 128:(tt + 1) * 128],
                              gtp[0:NE, :]).then_inc(p.h["ve"], 1)
            p.emit("vector", fcpr)
            tile_done["ve"] = p.bump("ve")
        gating_ve = p.mark("ve")
        if MK_DBG and not self.dbg_done[0]:
            p.wait("gpsimd", "ve", gating_ve)
            p.dma("gpsimd", self.dbg_cw.ap()[0:NE, :], cwrow[:, :], "wt")
        self.pop_to(gate_scope)

        y_acc = self.sb("f_yacc", [128, NDC * QN], F32)
        RING = 4
        W1C = self.sb("f_w1c", [128, RING * 1024], BF16)
        W3C = self.sb("f_w3c", [128, RING * 1024], BF16) if layer == 0 else None
        W2C = self.sb("f_w2c0", [128, HC * 1024], BF16)
        B1C = self.sb("f_b1c", [128, NPE * HC], F32)
        B3C = self.sb("f_b3c", [128, NPE * HC], F32) if layer == 0 else None
        B2S = self.sb("f_b2s", [NE, D], BF16)
        SB2S = self.sb("f_sb2s", [1, D], BF16) if layer == 0 else None
        CWB = self.sb("f_cwb", [128, 2 * QN], F32)
        AEB = self.sb("f_aeb", [128, HC * QN], BF16)
        NSTG = 4
        HSB = self.sb("f_hs", [128, NSTG * 512], F32)
        GCB = self.sb("f_gc", [128, NSTG * 512], F32) if layer == 0 else None

        psA = [self.ps("f_psA%d" % k, [128, 512]) for k in range(4)]
        psB = [self.ps("f_psB%d" % k, [128, 512]) for k in range(4)]

        p.waits("gpsimd", mm=p.mark("mm"), sc=p.mark("sc"), ve=p.mark("ve"))

        p.sem("bld")
        for e in range(NE):
            p.dma("gpsimd", B1C[:, e * HC:(e + 1) * HC],
                  (i["b1r"] if layer == 0 else i["b1_1r"]).ap()[e, :, :], "bld")
            if layer == 0:
                p.dma("gpsimd", B3C[:, e * HC:(e + 1) * HC], i["b3r"].ap()[e, :, :], "bld")
        if layer == 0:
            p.dma("gpsimd", B1C[:, NE * HC:NE * HC + L0_SHC], i["sb1r"].ap()[:, :], "bld")
            p.dma("gpsimd", B3C[:, NE * HC:NE * HC + L0_SHC], i["sb3r"].ap()[:, :], "bld")
            p.dma("gpsimd", SB2S[:, :], i["sb2"].ap()[:, :], "bld")
        bias_wt = p.dma("gpsimd", B2S[:, :],
                        (i["b2"] if layer == 0 else i["b2_1"]).ap()[:, :], "bld")

        sched = [(e, hh) for e in range(NPE) for hh in range(HC)]
        NCH = len(sched)

        w13_marks = [0] * NCH
        w13_rel = [0] * NCH
        w2_marks = [0] * NPE
        phaseB_rel = [0] * NPE
        evac_marks = [0] * NPE

        def bcol(e, hh, which):
            buf = B1C if which == 0 else B3C
            return buf[:, e * HC + hh:e * HC + hh + 1]

        def w1_src(e, hh):
            if layer == 0:
                if e < NE:
                    return (i["w1r"].ap()[e, hh, :, :], i["w3r"].ap()[e, hh, :, :])
                hh2 = (e - NE) * HC + hh
                return (i["sw1r"].ap()[hh2, :, :], i["sw3r"].ap()[hh2, :, :])
            return (i["W1r"].ap()[e, hh, :, :], None)

        def w2_src(e, hh):
            if layer == 0:
                if e < NE:
                    return bass.AP(i["w2"], (e * L0_H + hh * 128) * D, [[D, 128], [1, D]])
                hh2 = (e - NE) * HC + hh
                return bass.AP(i["sw2"], (hh2 * 128) * D, [[D, 128], [1, D]])
            return i["W2r"].ap()[e, hh, :, :]

        for _slot in range(RING):
            p.sem("w13s%d" % _slot)

        def emit_w13_load(k):
            e, hh = sched[k]
            slot = k % RING
            if k >= RING:
                p.wait("gpsimd", "mm", w13_rel[k - RING])
            s1, s3 = w1_src(e, hh)
            sname = "w13s%d" % slot
            v = p.dma("gpsimd", W1C[:, slot * 1024:(slot + 1) * 1024], s1, sname)
            if layer == 0:
                v = p.dma("gpsimd", W3C[:, slot * 1024:(slot + 1) * 1024], s3, sname)
            w13_marks[k] = (sname, v)

        p.sem("w2s")

        def emit_w2_load(e):
            if e >= 1:
                p.wait("gpsimd", "mm", phaseB_rel[e - 1])
            buf = W2C
            v = 0
            for hh in range(HC):
                v = p.dma("gpsimd", buf[:, hh * 1024:(hh + 1) * 1024], w2_src(e, hh), "w2s")
            w2_marks[e] = v

        for k in range(min(RING, NCH)):
            emit_w13_load(k)
        emit_w2_load(0)

        p.waits("tensor", bld=bias_wt, ve=gating_ve)
        p.waits("scalar", bld=bias_wt)

        a_ve = []
        grp = 0
        cwb_ready = [0, 0]
        last_evac = [0]

        def emit_cwb(e):
            if layer == 0 and e >= NE:
                return
            eslot = e % 2
            for sl in range(QSL):
                pp = psB[sl]

                def fb(t, e=e, sl=sl, pp=pp):
                    t.matmul(pp[:, :], self.onehot[0:NE, e * 128:(e + 1) * 128],
                             cwrow[:, sl * 512:(sl + 1) * 512],
                             start=True, stop=True).then_inc(p.h["mm"], 1)
                p.emit("tensor", fb)
                mv = p.bump("mm")
                p.wait("vector", "mm", mv)

                def fbc(v, eslot=eslot, sl=sl, pp=pp):
                    v.tensor_copy(CWB[:, eslot * QN + sl * 512:eslot * QN + (sl + 1) * 512],
                                  pp[:, :]).then_inc(p.h["ve"], 1)
                p.emit("vector", fbc)
                p.bump("ve")
            cwb_ready[e % 2] = p.mark("ve")

        emit_cwb(0)

        for k in range(NCH):
            e, hh = sched[k]
            slot = k % RING
            last_of_e = (hh == HC - 1)

            p.wait("tensor", *w13_marks[k])
            for ts in range(QSL):
                pa = psA[2 * (grp % 2)]
                pg = psA[2 * (grp % 2) + 1]
                stg = grp % NSTG
                gv = a_ve[grp - 2] if grp >= 2 else 0
                p.wait("tensor", "ve", max(gv, last_evac[0] if hh == 0 else 0))

                if layer == 0:
                    def fA(t, slot=slot, ts=ts, pa=pa, pg=pg):
                        ins = None
                        for c in range(NDC):
                            t.matmul(pa[:, :],
                                     W1C[:, slot * 1024 + c * 128:slot * 1024 + (c + 1) * 128],
                                     xt[:, c * QN + ts * 512:c * QN + ts * 512 + 512],
                                     start=(c == 0), stop=(c == NDC - 1))
                        for c in range(NDC):
                            ins = t.matmul(pg[:, :],
                                     W3C[:, slot * 1024 + c * 128:slot * 1024 + (c + 1) * 128],
                                     xt[:, c * QN + ts * 512:c * QN + ts * 512 + 512],
                                     start=(c == 0), stop=(c == NDC - 1))
                        ins.then_inc(p.h["mm"], 1)
                else:
                    def fA(t, slot=slot, ts=ts, pa=pa, pg=pg):
                        ins = None
                        for c in range(NDC):
                            ins = t.matmul(pa[:, :],
                                     W1C[:, slot * 1024 + c * 128:slot * 1024 + (c + 1) * 128],
                                     xt[:, c * QN + ts * 512:c * QN + ts * 512 + 512],
                                     start=(c == 0), stop=(c == NDC - 1))
                        ins.then_inc(p.h["mm"], 1)
                p.emit("tensor", fA)
                mA = p.bump("mm")

                if layer == 0:
                    if hh == 0 and ts == 0 and e < NE:
                        p.wait("vector", "ve", cwb_ready[e % 2])
                    p.wait("scalar", "mm", mA)
                    if grp >= NSTG:
                        p.wait("scalar", "ve", a_ve[grp - NSTG])

                    def fhs(s, pa=pa, stg=stg, b1c=bcol(e, hh, 0)):
                        s.activation(HSB[:, stg * 512:(stg + 1) * 512], pa[:, :],
                                     AF.Silu, bias=b1c).then_inc(p.h["sc"], 1)
                    p.emit("scalar", fhs)
                    sA = p.bump("sc")

                    p.wait("vector", "mm", mA)
                    if e < NE:
                        def fgc(v, pg=pg, stg=stg, b3c=bcol(e, hh, 1), eslot=e % 2, ts=ts):
                            v.scalar_tensor_tensor(
                                GCB[:, stg * 512:(stg + 1) * 512], pg[:, :], b3c,
                                CWB[:, eslot * QN + ts * 512:eslot * QN + ts * 512 + 512],
                                ALU.add, ALU.mult).then_inc(p.h["ve"], 1)
                    else:
                        def fgc(v, pg=pg, stg=stg, b3c=bcol(e, hh, 1)):
                            v.tensor_scalar(GCB[:, stg * 512:(stg + 1) * 512], pg[:, :],
                                            b3c, None, ALU.add).then_inc(p.h["ve"], 1)
                    p.emit("vector", fgc)
                    vgc = p.bump("ve")
                    p.wait("vector", "ve", vgc)
                    p.wait("vector", "sc", sA)

                    def fam(v, stg=stg, hh=hh, ts=ts):
                        v.tensor_mul(AEB[:, hh * QN + ts * 512:hh * QN + ts * 512 + 512],
                                     HSB[:, stg * 512:(stg + 1) * 512],
                                     GCB[:, stg * 512:(stg + 1) * 512]).then_inc(p.h["ve"], 1)
                    p.emit("vector", fam)
                    a_ve.append(p.bump("ve"))
                else:
                    if hh == 0 and ts == 0:
                        p.wait("vector", "ve", cwb_ready[e % 2])
                    p.wait("vector", "mm", mA)

                    def fre(v, pa=pa, stg=stg, b1c=bcol(e, hh, 0)):
                        v.tensor_scalar(HSB[:, stg * 512:(stg + 1) * 512], pa[:, :],
                                        b1c, 0.0, ALU.add, ALU.max).then_inc(p.h["ve"], 1)
                    p.emit("vector", fre)
                    vre = p.bump("ve")
                    p.wait("vector", "ve", vre)

                    def fam(v, stg=stg, hh=hh, ts=ts, eslot=e % 2):
                        v.tensor_mul(AEB[:, hh * QN + ts * 512:hh * QN + ts * 512 + 512],
                                     HSB[:, stg * 512:(stg + 1) * 512],
                                     CWB[:, eslot * QN + ts * 512:eslot * QN + ts * 512 + 512]).then_inc(p.h["ve"], 1)
                    p.emit("vector", fam)
                    a_ve.append(p.bump("ve"))
                grp += 1
            w13_rel[k] = p.mark("mm")

            if k + RING < NCH:
                emit_w13_load(k + RING)

            if last_of_e:
                w2buf = W2C
                p.wait("tensor", "w2s", w2_marks[e])
                p.wait("tensor", "ve", a_ve[-1])
                gi = 0
                evB = []
                for dh in range(2):
                    for ts in range(QSL):
                        pset = psB if gi % 2 == 0 else psA
                        if gi >= 2:
                            p.wait("tensor", "ve", evB[gi - 2])
                        gi += 1

                        def fB(t, e=e, dh=dh, ts=ts, w2buf=w2buf, pset=pset):
                            ins = None
                            for dc in range(4):
                                do = dh * 4 + dc
                                if e == 0:
                                    t.matmul(pset[dc][:, :], B2S[:, do * 128:(do + 1) * 128],
                                             cwfm[:, ts * 512:(ts + 1) * 512],
                                             start=True, stop=False)
                                    if layer == 0:
                                        t.matmul(pset[dc][:, :],
                                                 SB2S[0:1, do * 128:(do + 1) * 128],
                                                 self.ones_row[0:1, :],
                                                 start=False, stop=False)
                                for hh2 in range(HC):
                                    ins = t.matmul(pset[dc][:, :],
                                             w2buf[:, hh2 * 1024 + do * 128:hh2 * 1024 + (do + 1) * 128],
                                             AEB[:, hh2 * QN + ts * 512:hh2 * QN + ts * 512 + 512],
                                             start=(hh2 == 0 and e != 0),
                                             stop=(hh2 == HC - 1))
                            ins.then_inc(p.h["mm"], 1)
                        p.emit("tensor", fB)
                        mB = p.bump("mm")

                        p.wait("vector", "mm", mB)

                        def fev(v, e=e, dh=dh, ts=ts, pset=pset):
                            ins = None
                            for dc in range(4):
                                do = dh * 4 + dc
                                dsl = y_acc[:, do * QN + ts * 512:do * QN + ts * 512 + 512]
                                if e == 0:
                                    ins = v.tensor_copy(dsl, pset[dc][:, :])
                                else:
                                    ins = v.tensor_add(dsl, dsl, pset[dc][:, :])
                            ins.then_inc(p.h["ve"], 1)
                        p.emit("vector", fev)
                        evB.append(p.bump("ve"))
                evac_marks[e] = p.mark("ve")
                last_evac[0] = evac_marks[e]
                phaseB_rel[e] = p.mark("mm")
                if e + 1 < NPE:
                    emit_w2_load(e + 1)
                p.wait("tensor", "ve", evac_marks[e])
                if e + 1 < NPE:
                    emit_cwb(e + 1)

        if MK_DBG and not self.dbg_done[0]:
            self.dbg_done[0] = True
            p.wait("gpsimd", "ve", p.mark("ve"))
            p.dma("gpsimd", self.dbg_yacc.ap()[:, :], y_acc[:, :], "wt")
        xold = self.sb("f_xold", [128, 2 * QN], F32)
        xnew = self.sb("f_xnew", [128, 2 * QN], F32)
        yve = p.mark("ve")
        ld_m = {}
        res_marks = []
        st_prev = {}
        p.sem("xo0"); p.sem("xo1"); p.sem("sn0"); p.sem("sn1")

        def emit_load(c):
            if c >= 2:
                p.wait("sync", "ve", res_marks[c - 2])
            ld_m[c] = (("xo%d" % (c % 2)),
                       p.dma("sync", xold[:, (c % 2) * QN:(c % 2) * QN + QN],
                             src.ap()[c * 128:(c + 1) * 128, t0:t0 + QN], "xo%d" % (c % 2)))

        emit_load(0)
        emit_load(1)
        for c in range(NDC):
            p.wait("vector", *ld_m[c])
            p.wait("vector", "ve", yve)
            if c >= 2:
                p.wait("vector", *st_prev[c - 2])

            def fres(v, c=c):
                v.scalar_tensor_tensor(xnew[:, (c % 2) * QN:(c % 2) * QN + QN],
                                       y_acc[:, c * QN:(c + 1) * QN], 0.5,
                                       xold[:, (c % 2) * QN:(c % 2) * QN + QN],
                                       ALU.mult, ALU.add).then_inc(p.h["ve"], 1)
            p.emit("vector", fres)
            res_marks.append(p.bump("ve"))
            p.wait("sync", "ve", res_marks[-1])
            st_prev[c] = (("sn%d" % (c % 2)),
                          p.dma("sync", dst.ap()[c * 128:(c + 1) * 128, t0:t0 + QN],
                                xnew[:, (c % 2) * QN:(c % 2) * QN + QN], "sn%d" % (c % 2)))
            if c + 2 < NDC:
                emit_load(c + 2)


def build_nc(n_sublayers):
    b = Builder()
    p = b.p
    b.build_consts()

    chain = [("ffn", 0), ("attn", 0), ("ffn", 0), ("ffn", 1), ("attn", 1), ("ffn", 1)]
    ch_env = os.environ.get("MK_CHAIN", "")
    if ch_env:
        chain = [(c[:-1], int(c[-1])) for c in ch_env.split(",")]
    else:
        chain = chain[:n_sublayers]
    src = b.inputs["x0"]
    ping = 0
    for si, (kind, layer) in enumerate(chain):
        dst = b.out if si == len(chain) - 1 else (b.xA if ping == 0 else b.xB)
        if si < len(chain) - 1:
            ping ^= 1
        if kind == "ffn":
            b.emit_ffn(layer, src, dst)
        else:
            b.emit_attn(layer, src, dst)
        b.join_all()
        src = dst

    for snm in list(p.cnt):
        p.wait("sync", snm, p.mark(snm))
    with b.nc.Block() as block:
        p.finalize(block)
    b.pop_to(0)
    return b.nc


def _prep_shared(inp):
    c = np.ascontiguousarray
    f = np.float32
    d = {}
    d["gate_w"] = c(inp["L0_ffn_gate_w"].reshape(NDC, 128, L0_E).transpose(1, 0, 2)).astype(f)
    d["gate_b"] = c(inp["L0_ffn_gate_b"].reshape(1, L0_E)).astype(f)
    d["w1r"] = c(np.asarray(inp["L0_ffn_w1"]).reshape(L0_E, NDC, 128, L0_HC, 128)
                 .transpose(0, 3, 2, 1, 4).reshape(L0_E, L0_HC, 128, 1024)).astype(f)
    d["w3r"] = c(np.asarray(inp["L0_ffn_w3"]).reshape(L0_E, NDC, 128, L0_HC, 128)
                 .transpose(0, 3, 2, 1, 4).reshape(L0_E, L0_HC, 128, 1024)).astype(f)
    d["w2"] = c(np.asarray(inp["L0_ffn_w2"])).astype(f)
    d["b1r"] = c(np.asarray(inp["L0_ffn_b1"]).reshape(L0_E, L0_HC, 128).transpose(0, 2, 1)).astype(f)
    d["b3r"] = c(np.asarray(inp["L0_ffn_b3"]).reshape(L0_E, L0_HC, 128).transpose(0, 2, 1)).astype(f)
    d["b2"] = c(np.asarray(inp["L0_ffn_b2"])).astype(f)
    d["sw1r"] = c(np.asarray(inp["L0_ffn_sw1"]).reshape(NDC, 128, L0_SHC, 128)
                  .transpose(2, 1, 0, 3).reshape(L0_SHC, 128, 1024)).astype(f)
    d["sw3r"] = c(np.asarray(inp["L0_ffn_sw3"]).reshape(NDC, 128, L0_SHC, 128)
                  .transpose(2, 1, 0, 3).reshape(L0_SHC, 128, 1024)).astype(f)
    d["sw2"] = c(np.asarray(inp["L0_ffn_sw2"])).astype(f)
    d["sb1r"] = c(np.asarray(inp["L0_ffn_sb1"]).reshape(L0_SHC, 128).T).astype(f)
    d["sb3r"] = c(np.asarray(inp["L0_ffn_sb3"]).reshape(L0_SHC, 128).T).astype(f)
    d["sb2"] = c(np.asarray(inp["L0_ffn_sb2"]).reshape(1, D)).astype(f)
    d["WQr"] = c(np.asarray(inp["L0_attn_WQ"]).reshape(NDC, 128, D).transpose(1, 0, 2)).astype(f)
    d["WKr"] = c(np.asarray(inp["L0_attn_WK"]).reshape(NDC, 128, D).transpose(1, 0, 2)).astype(f)
    d["WVr"] = c(np.asarray(inp["L0_attn_WV"]).reshape(NDC, 128, D).transpose(1, 0, 2)).astype(f)
    d["wqkr"] = c(np.asarray(inp["L1_attn_wqk"]).reshape(NDC, 128, 128).transpose(1, 0, 2)).astype(f)
    d["bqk"] = c(np.asarray(inp["L1_attn_bqk"]).reshape(128, 1)).astype(f)
    gam = np.asarray(inp["L1_attn_gamma"])
    bet = np.asarray(inp["L1_attn_beta"])
    gb = np.stack([gam[0] / SEQ, bet[0] / SEQ, gam[1], bet[1]], axis=1)
    d["gbcols"] = c(gb).astype(f)
    d["wvr"] = c(np.asarray(inp["L1_attn_wv"]).reshape(NDC, 128, D).transpose(1, 0, 2)).astype(f)
    d["bvrow"] = c(np.asarray(inp["L1_attn_bv"]).reshape(1, D)).astype(f)
    d["relb"] = c(np.asarray(inp["L1_attn_rel_bias"]).reshape(1, NUM_BUCKETS)).astype(f)
    d["W1r"] = c(np.asarray(inp["L1_ffn_W1"]).reshape(L1_E, L1_HC, 128, NDC, 128)
                 .transpose(0, 1, 4, 3, 2).reshape(L1_E, L1_HC, 128, 1024)).astype(f)
    d["W2r"] = c(np.asarray(inp["L1_ffn_W2"]).reshape(L1_E, D, L1_HC, 128)
                 .transpose(0, 2, 3, 1).reshape(L1_E, L1_HC, 128, 1024)).astype(f)
    d["b1_1r"] = c(np.asarray(inp["L1_ffn_b1"]).reshape(L1_E, L1_HC, 128).transpose(0, 2, 1)).astype(f)
    d["b2_1"] = c(np.asarray(inp["L1_ffn_b2"])).astype(f)
    d["Wgr"] = c(np.asarray(inp["L1_ffn_Wg"]).T.reshape(NDC, 128, L1_E).transpose(1, 0, 2)).astype(f)
    d["bg_1"] = c(np.asarray(inp["L1_ffn_bg"]).reshape(1, L1_E)).astype(f)
    return d


def prepare_in_maps(inputs):
    x = np.asarray(inputs["L0_attn_inputs"])
    shared = _prep_shared(inputs)
    in_maps = []
    for c in range(NCORES):
        seq, half = c // 2, c % 2
        xs = np.ascontiguousarray(x[seq, half * NT:(half + 1) * NT, :].T).astype(np.float32)
        m = dict(shared)
        m["x0"] = xs
        m["hflag"] = np.array([[float(half)]], dtype=np.float32)
        in_maps.append(m)
    return in_maps


def kernel(**inputs):
    in_maps = prepare_in_maps(inputs)
    nc = build_nc(N_SUBLAYERS)
    res = run_bass_kernel_spmd(nc, in_maps, core_ids=list(range(NCORES)))
    outp = np.empty((4, SEQ, D), dtype=np.float32)
    for c in range(NCORES):
        seq, half = c // 2, c % 2
        outp[seq, half * NT:(half + 1) * NT, :] = res.results[c]["out"].T
    return outp


# version 03 (final)
# speedup vs baseline: 1.0972x; 1.0812x over previous
import math
import os
import numpy as np

import concourse.bass as bass
import concourse.mybir as mybir
from concourse.bass_utils import run_bass_kernel_spmd

F32 = mybir.dt.float32
BF16 = mybir.dt.bfloat16
AF = mybir.ActivationFunctionType
ALU = mybir.AluOpType
AXL = mybir.AxisListType

NCORES = 8
D = 1024
NDC = D // 128
NT = 4096
SEQ = 8192
BIG_NEG = -1.0e9

L0_E, L0_H, L0_HC = 16, 1408, 11
L0_SH, L0_SHC = 2816, 22
L1_E, L1_H, L1_HC = 8, 2048, 16
NUM_BUCKETS, MAX_DISTANCE = 32, 128

N_SUBLAYERS = int(os.environ.get("MK_N_SUBLAYERS", "6"))
ERF_FN_NAME = "Tanh" if os.environ.get("MK_ERF_TANH") == "1" else "Erf"
MK_QUARTERS = int(os.environ.get("MK_QUARTERS", "0"))
SIM_RACES = os.environ.get("MK_SIM", "0") == "1"
MK_DBG = os.environ.get("MK_DBG", "0") == "1"

QN = 1024
NQ = NT // QN
QSL = QN // 512


class Prog:
    ENGINES = ("tensor", "vector", "scalar", "gpsimd", "sync")

    def __init__(self, nc):
        self.nc = nc
        self.q = {e: [] for e in self.ENGINES}
        self.cnt = {}
        self.h = {}

    def sem(self, name):
        if name in self.h:
            return
        g = self.nc.semaphore(name)
        self.h[name] = g.__enter__()
        self.cnt[name] = 0

    def bump(self, name, n=1):
        self.cnt[name] += n
        return self.cnt[name]

    def mark(self, name):
        return self.cnt[name]

    def emit(self, eng, fn):
        self.q[eng].append(fn)

    def dma(self, qeng, out, in_, sem_name, **kw):
        h = self.h[sem_name]
        v = self.bump(sem_name, 16)

        def f(eng, out=out, in_=in_, h=h, kw=kw):
            eng.dma_start(out=out, in_=in_, **kw).then_inc(h, 16)
        self.emit(qeng, f)
        return v

    def wait(self, eng_name, sem_name, val):
        if val <= 0:
            return
        h = self.h[sem_name]

        def f(eng, h=h, val=val):
            eng.wait_ge(h, val)
        self.emit(eng_name, f)

    def waits(self, eng_name, **marks):
        for s, v in marks.items():
            self.wait(eng_name, s, v)

    def finalize(self, block):
        for e in self.ENGINES:
            fns = self.q[e]

            def body(eng, fns=fns):
                for f in fns:
                    f(eng)
            getattr(block, e)(body)


class Builder:
    def __init__(self):
        nc = bass.Bass("TRN2", target_bir_lowering=False,
                       detect_race_conditions=False)
        self.nc = nc
        self.ctx = []
        p = Prog(nc)
        self.p = p
        for s in ("ld", "st", "wt", "mm", "sc", "ve", "cc", "g2"):
            p.sem(s)

        def ein(name, shape):
            return nc.dram_tensor(name, list(shape), F32, kind="ExternalInput")
        self.inputs = {}
        i = self.inputs
        i["x0"] = ein("x0", [D, NT])
        i["hflag"] = ein("hflag", [1, 1])
        i["gate_w"] = ein("gate_w", [128, NDC, L0_E])
        i["gate_b"] = ein("gate_b", [1, L0_E])
        i["w1r"] = ein("w1r", [L0_E, L0_HC, 128, 1024])
        i["w3r"] = ein("w3r", [L0_E, L0_HC, 128, 1024])
        i["w2"] = ein("w2", [L0_E, L0_H, D])
        i["b1r"] = ein("b1r", [L0_E, 128, L0_HC])
        i["b3r"] = ein("b3r", [L0_E, 128, L0_HC])
        i["b2"] = ein("b2", [L0_E, D])
        i["sw1r"] = ein("sw1r", [L0_SHC, 128, 1024])
        i["sw3r"] = ein("sw3r", [L0_SHC, 128, 1024])
        i["sw2"] = ein("sw2", [L0_SH, D])
        i["sb1r"] = ein("sb1r", [128, L0_SHC])
        i["sb3r"] = ein("sb3r", [128, L0_SHC])
        i["sb2"] = ein("sb2", [1, D])
        i["WQr"] = ein("WQr", [128, NDC, D])
        i["WKr"] = ein("WKr", [128, NDC, D])
        i["WVr"] = ein("WVr", [128, NDC, D])
        i["wqkr"] = ein("wqkr", [128, NDC, 128])
        i["bqk"] = ein("bqk", [128, 1])
        i["gbcols"] = ein("gbcols", [128, 4])
        i["wvr"] = ein("wvr", [128, NDC, D])
        i["bvrow"] = ein("bvrow", [1, D])
        i["relb"] = ein("relb", [1, NUM_BUCKETS])
        i["W1r"] = ein("W1r", [L1_E, L1_HC, 128, 1024])
        i["W2r"] = ein("W2r", [L1_E, L1_HC, 128, 1024])
        i["b1_1r"] = ein("b1_1r", [L1_E, 128, L1_HC])
        i["b2_1"] = ein("b2_1", [L1_E, D])
        i["Wgr"] = ein("Wgr", [128, NDC, L1_E])
        i["bg_1"] = ein("bg_1", [1, L1_E])

        self.out = nc.dram_tensor("out", [D, NT], F32, kind="ExternalOutput")
        if MK_DBG:
            self.dbg_xt = nc.dram_tensor("dbg_xt", [128, NDC * QN], F32, kind="ExternalOutput")
            self.dbg_cw = nc.dram_tensor("dbg_cw", [16, QN], F32, kind="ExternalOutput")
            self.dbg_yacc = nc.dram_tensor("dbg_yacc", [128, NDC * QN], F32, kind="ExternalOutput")
            self.dbg_stat = nc.dram_tensor("dbg_stat", [1, 4 * QN], F32, kind="ExternalOutput")
            self.dbg_done = [False]
        self.xA = nc.dram_tensor("xA", [D, NT], F32, kind="Internal")
        self.xB = nc.dram_tensor("xB", [D, NT], F32, kind="Internal")

        self.eye_f = nc.inline_tensor(np.eye(128, dtype=np.float32), "eye_f")
        bmnp = np.zeros((128, 640), dtype=np.float32)
        for r in range(128):
            bmnp[r, r:r + 512] = 1.0
        self.bandmask_c = nc.inline_tensor(bmnp, "bandmask")
        oh = np.zeros((16, 16 * 128), dtype=np.float32)
        for e in range(16):
            oh[e, e * 128:(e + 1) * 128] = 1.0
        self.onehot_c = nc.inline_tensor(oh, "onehot16")
        oh2 = np.zeros((NUM_BUCKETS, 2 * NUM_BUCKETS), dtype=np.float32)
        for j, (_d, bj, bp) in enumerate([(1, 1, 0), (2, 2, 1), (3, 3, 2), (4, 4, 3), (5, 5, 4), (6, 6, 5), (7, 7, 6), (8, 8, 7), (9, 9, 8), (10, 10, 9), (11, 11, 10), (12, 12, 11), (13, 13, 12), (14, 14, 13), (15, 15, 14), (16, 16, 15), (19, 17, 16), (21, 18, 17), (24, 19, 18), (27, 20, 19), (31, 21, 20), (35, 22, 21), (40, 23, 22), (46, 24, 23), (52, 25, 24), (59, 26, 25), (67, 27, 26), (77, 28, 27), (87, 29, 28), (99, 30, 29), (113, 31, 30)]):
            oh2[bj, j] = 1.0
            oh2[bp, NUM_BUCKETS + j] = 1.0
        self.bktoh_c = nc.inline_tensor(oh2, "bktoh")

    _uid = 0

    def _un(self, name):
        Builder._uid += 1
        return "%s_%d" % (name, Builder._uid)

    def sb(self, name, shape, dtype):
        g = self.nc.sbuf_tensor(self._un(name), list(shape), dtype)
        t = g.__enter__()
        self.ctx.append(g)
        return t

    def ps(self, name, shape, dtype=F32):
        g = self.nc.psum_tensor(self._un(name), list(shape), dtype)
        t = g.__enter__()
        self.ctx.append(g)
        return t

    def scope(self):
        return len(self.ctx)

    def pop_to(self, n):
        while len(self.ctx) > n:
            self.ctx.pop().__exit__(None, None, None)

    def build_consts(self):
        p = self.p
        self.ones_col = self.sb("ones_col", [128, 1], BF16)
        self.onesf_col = self.sb("onesf_col", [128, 1], F32)
        self.onesf_row = self.sb("onesf_row", [1, 128], F32)
        self.ones_row = self.sb("ones_row", [1, 512], BF16)
        self.eps_c = self.sb("eps_c", [1, 1], F32)
        self.hf_c = self.sb("hf_c", [1, 1], F32)
        self.ident = self.sb("ident", [128, 128], F32)
        self.ident_bf = self.sb("ident_bf", [128, 128], BF16)
        self.onehot = self.sb("onehot", [16, 16 * 128], F32)
        self.bandm = self.sb("bandm", [128, 640], F32)
        p.dma("sync", self.ident[:, :], self.eye_f.ap()[:, :], "ld")
        p.dma("sync", self.onehot[:, :], self.onehot_c.ap()[:, :], "ld")
        p.dma("sync", self.bandm[:, :], self.bandmask_c.ap()[:, :], "ld")
        p.sem("ldg")
        p.dma("gpsimd", self.ident_bf[:, :], self.eye_f.ap()[:, :], "ldg")
        p.dma("sync", self.hf_c[:, :], self.inputs["hflag"].ap()[:, :], "ld")

        def g_init(g):
            h = p.h["g2"]
            g.memset(self.ones_col[:, :], 1.0).then_inc(h, 1)
            g.memset(self.onesf_col[:, :], 1.0).then_inc(h, 1)
            g.memset(self.onesf_row[:, :], 1.0).then_inc(h, 1)
            g.memset(self.ones_row[:, :], 1.0).then_inc(h, 1)
            g.memset(self.eps_c[:, :], 1e-5).then_inc(h, 1)
        p.emit("gpsimd", g_init)
        p.bump("g2", 5)
        self.init_marks = dict(g2=p.mark("g2"), ld=p.mark("ld"), ldg=p.mark("ldg"))

    def join_all(self):
        p = self.p
        marks = {s: p.mark(s) for s in list(p.cnt)}
        for e in Prog.ENGINES:
            for s, v in marks.items():
                p.wait(e, s, v)

    def emit_stats(self, src, t0, n, stat):
        p = self.p
        SS, SQ, SM, SA = 0, n, 2 * n, 3 * n
        xs = self.sb("st_x", [128, 2 * n], BF16)
        sq = self.sb("st_sq", [128, 2 * n], BF16)
        pss = [self.ps("st_ps%d" % k, [1, 512]) for k in range(4)]
        nsl = n // 512

        def z(v):
            v.memset(stat[0:1, 0:2 * n], 0.0).then_inc(p.h["ve"], 1)
        p.emit("vector", z)
        zv = p.bump("ve")

        ld_m = {}
        mm_done = {}
        sc_done = {}
        facc_marks = []
        pi = 0

        p.sem("stx0"); p.sem("stx1")
        p.waits("gpsimd", mm=p.mark("mm"), sc=p.mark("sc"), ve=p.mark("ve"))
        p.waits("tensor", ve=p.mark("ve"), sc=p.mark("sc"))

        def emit_load(c):
            if c >= 2:
                p.wait("gpsimd", "mm", mm_done[c - 2])
                p.wait("gpsimd", "sc", sc_done[c - 2])
            ld_m[c] = (("stx%d" % (c % 2)),
                       p.dma("gpsimd", xs[:, (c % 2) * n:(c % 2) * n + n],
                             src.ap()[c * 128:(c + 1) * 128, t0:t0 + n], "stx%d" % (c % 2)))

        emit_load(0)
        emit_load(1)
        p.wait("vector", "ve", zv)
        for c in range(NDC):
            xc = xs[:, (c % 2) * n:(c % 2) * n + n]
            sqc = sq[:, (c % 2) * n:(c % 2) * n + n]
            p.wait("scalar", *ld_m[c])

            def fsq(s, xc=xc, sqc=sqc):
                s.activation(sqc, xc, AF.Square).then_inc(p.h["sc"], 1)
            p.emit("scalar", fsq)
            sc_done[c] = p.bump("sc")

            p.wait("tensor", *ld_m[c])
            p.wait("tensor", "sc", sc_done[c])
            for which, buf in ((SS, xc), (SQ, sqc)):
                for sl in range(nsl):
                    pp = pss[pi % 4]
                    if pi >= 4:
                        p.wait("tensor", "ve", facc_marks[pi - 4])

                    def fmm(t, pp=pp, buf=buf, sl=sl):
                        t.matmul(pp[:, :], self.ones_col[:, 0:1],
                                 buf[:, sl * 512:(sl + 1) * 512],
                                 start=True, stop=True).then_inc(p.h["mm"], 1)
                    p.emit("tensor", fmm)
                    mv = p.bump("mm")
                    p.wait("vector", "mm", mv)

                    def facc(v, pp=pp, which=which, sl=sl):
                        v.tensor_add(stat[0:1, which + sl * 512:which + (sl + 1) * 512],
                                     stat[0:1, which + sl * 512:which + (sl + 1) * 512],
                                     pp[0:1, :]).then_inc(p.h["ve"], 1)
                    p.emit("vector", facc)
                    facc_marks.append(p.bump("ve"))
                    pi += 1
            mm_done[c] = p.mark("mm")
            if c + 2 < NDC:
                emit_load(c + 2)

        p.wait("scalar", "ve", p.mark("ve"))
        h = p.h["sc"]

        def f1(s):
            s.mul(stat[0:1, SM:SM + n], stat[0:1, SS:SS + n], 1.0 / 1024.0).then_inc(h, 1)
        p.emit("scalar", f1)
        v = p.bump("sc")
        p.wait("scalar", "sc", v)

        def f2(s):
            s.square(stat[0:1, SS:SS + n], stat[0:1, SM:SM + n]).then_inc(h, 1)
        p.emit("scalar", f2)
        v = p.bump("sc")
        p.wait("scalar", "sc", v)

        def f3(s):
            s.mul(stat[0:1, SQ:SQ + n], stat[0:1, SQ:SQ + n], 1.0 / 1024.0).then_inc(h, 1)
        p.emit("scalar", f3)
        v = p.bump("sc")
        p.wait("vector", "sc", v)

        def f4(v_):
            v_.tensor_sub(stat[0:1, SQ:SQ + n], stat[0:1, SQ:SQ + n],
                          stat[0:1, SS:SS + n]).then_inc(p.h["ve"], 1)
        p.emit("vector", f4)
        vv = p.bump("ve")
        p.wait("scalar", "ve", vv)

        def f5(s):
            s.activation(stat[0:1, SA:SA + n], stat[0:1, SQ:SQ + n], AF.Ln,
                         bias=self.eps_c[0:1, 0:1]).then_inc(h, 1)
        p.emit("scalar", f5)
        v = p.bump("sc")
        p.wait("scalar", "sc", v)

        def f6(s):
            s.activation(stat[0:1, SA:SA + n], stat[0:1, SA:SA + n], AF.Exp,
                         scale=-0.5).then_inc(h, 1)
        p.emit("scalar", f6)
        sc_mark = p.bump("sc")
        return SM, SA, sc_mark

    def emit_xtilde(self, src, xt, t0, n, stat, SM, SA, sc_mark,
                    xt_stride=None, xt_off=0):
        p = self.p
        mb = self.sb("xt_mb", [128, 2 * n], F32)
        xstg = self.sb("xt_stg", [128, 2 * n], F32)
        pb = [self.ps("xt_pb%d" % k, [128, 512]) for k in range(2)]
        nsl = n // 512
        if xt_stride is None:
            xt_stride = n
        p.wait("tensor", "sc", sc_mark)
        p.wait("tensor", "g2", self.init_marks["g2"])
        cp_marks = []
        k = 0
        for which, soff in ((0, SM), (1, SA)):
            for sl in range(nsl):
                pp = pb[k % 2]
                if k >= 2:
                    p.wait("tensor", "ve", cp_marks[k - 2])

                def fmm(t, pp=pp, soff=soff, sl=sl):
                    t.matmul(pp[:, :], self.onesf_row[0:1, :],
                             stat[0:1, soff + sl * 512:soff + (sl + 1) * 512],
                             start=True, stop=True).then_inc(p.h["mm"], 1)
                p.emit("tensor", fmm)
                mv = p.bump("mm")
                p.wait("vector", "mm", mv)

                def fcp(v, pp=pp, which=which, sl=sl):
                    v.tensor_copy(mb[:, which * n + sl * 512:which * n + (sl + 1) * 512],
                                  pp[:, :]).then_inc(p.h["ve"], 1)
                p.emit("vector", fcp)
                cp_marks.append(p.bump("ve"))
                k += 1

        ld_m = {}
        res_marks = []
        p.sem("xtg0"); p.sem("xtg1")
        p.waits("sync", mm=p.mark("mm"), sc=p.mark("sc"), ve=p.mark("ve"))

        def emit_load(c):
            if c >= 2:
                p.wait("sync", "ve", res_marks[c - 2])
            ld_m[c] = (("xtg%d" % (c % 2)),
                       p.dma("sync", xstg[:, (c % 2) * n:(c % 2) * n + n],
                             src.ap()[c * 128:(c + 1) * 128, t0:t0 + n], "xtg%d" % (c % 2)))

        emit_load(0)
        emit_load(1)
        p.wait("vector", "ve", cp_marks[-1])
        for c in range(NDC):
            xc = xstg[:, (c % 2) * n:(c % 2) * n + n]
            p.wait("vector", *ld_m[c])

            def fxt1(v, xc=xc):
                v.tensor_sub(xc, xc, mb[:, 0:n]).then_inc(p.h["ve"], 1)
            p.emit("vector", fxt1)
            v1 = p.bump("ve")
            p.wait("vector", "ve", v1)

            def fxt2(v, xc=xc, c=c):
                v.tensor_mul(xt[:, c * xt_stride + xt_off:c * xt_stride + xt_off + n],
                             xc, mb[:, n:2 * n]).then_inc(p.h["ve"], 1)
            p.emit("vector", fxt2)
            res_marks.append(p.bump("ve"))
            if c + 2 < NDC:
                emit_load(c + 2)
        return p.mark("ve")


    def emit_bcast(self, srcrow, dst, n, pbank, extra_tensor_wait=None):
        p = self.p
        nsl = (n + 511) // 512
        last = 0
        for sl in range(nsl):
            w = min(512, n - sl * 512)
            if extra_tensor_wait is not None:
                p.wait("tensor", extra_tensor_wait[0], extra_tensor_wait[1])
            if sl >= 1:
                p.wait("tensor", "ve", last)

            def fm(t, sl=sl, w=w):
                t.matmul(pbank[:, 0:w], self.onesf_row[0:1, :],
                         srcrow[0:1, sl * 512:sl * 512 + w],
                         start=True, stop=True).then_inc(p.h["mm"], 1)
            p.emit("tensor", fm)
            mv = p.bump("mm")
            p.wait("vector", "mm", mv)

            def fc(v, sl=sl, w=w):
                v.tensor_copy(dst[:, sl * 512:sl * 512 + w],
                              pbank[:, 0:w]).then_inc(p.h["ve"], 1)
            p.emit("vector", fc)
            last = p.bump("ve")
        return last

    def emit_attn(self, layer, src, dst):
        sc_all = self.scope()
        if layer == 0:
            self._attn0(src, dst)
        else:
            self._attn1(src, dst)
        self.pop_to(sc_all)

    def emit_xt_full(self, src):
        xt = self.sb("a_xt", [128, NDC * NT], BF16)
        xt_ve = 0
        for q in range(NQ):
            scq = self.scope()
            stat = self.sb("a_stat", [1, 4 * QN], F32)
            SM, SA, st_sc = self.emit_stats(src, q * QN, QN, stat)
            xt_ve = self.emit_xtilde(src, xt, q * QN, QN, stat, SM, SA, st_sc,
                                     xt_stride=NT, xt_off=q * QN)
            self.pop_to(scq)
        return xt, xt_ve

    def _attn0(self, src, dst):
        p = self.p
        i = self.inputs
        nc = self.nc
        HKN = 4608

        K_hbm = nc.dram_tensor(self._un("a0_K"), [NDC, 128, HKN], BF16, kind="Internal")
        V_hbm = nc.dram_tensor(self._un("a0_V"), [HKN, D], BF16, kind="Internal")
        kx_in = nc.dram_tensor(self._un("a0_kxi"), [NDC, 128, 512], BF16, kind="Internal")
        kx_out = nc.dram_tensor(self._un("a0_kxo"), [2, NDC, 128, 512], BF16, kind="Internal")
        vx_in = nc.dram_tensor(self._un("a0_vxi"), [512, D], BF16, kind="Internal")
        vx_out = nc.dram_tensor(self._un("a0_vxo"), [2, 512, D], BF16, kind="Internal")
        vs_in = nc.dram_tensor(self._un("a0_vsi"), [1, D], F32, kind="Internal")
        vs_out = nc.dram_tensor(self._un("a0_vso"), [1, D], F32, kind="Internal")

        for snm in (["awl", "a0x", "a0h0", "a0h1", "kw0", "kw1", "vw0", "vw1"]
                    + ["ks%d" % x for x in range(4)] + ["vs%d" % x for x in range(2)]
                    + ["xoa%d" % x for x in range(4)] + ["sna%d" % x for x in range(4)]):
            p.sem(snm)

        Vb = self.sb("a0_vbc", [128, D], F32)
        vsg = self.sb("a0_vsg2", [1, D], F32)
        hfc = self.sb("a0_hf", [128, 2], F32)
        Q_fm = self.sb("a0_q", [128, NDC * NT], BF16)
        scp = self.scope()
        xt, xt_ve = self.emit_xt_full(src)
        scpW = self.scope()
        WQs = self.sb("a0_wq", [128, NDC * D], BF16)
        WKs = self.sb("a0_wk", [128, NDC * D], BF16)
        WVs = self.sb("a0_wv", [128, NDC * D], BF16)
        kst = self.sb("a0_kst", [128, 2 * 512], BF16)
        vst = self.sb("a0_vst", [128, 2 * D], BF16)
        vsum = self.sb("a0_vsum", [1, D], F32)
        psp = [self.ps("a0_pp%d" % k2, [128, 512]) for k2 in range(4)]
        psv = [self.ps("a0_pv%d" % k2, [1, 512]) for k2 in range(2)]

        p.waits("gpsimd", mm=p.mark("mm"), sc=p.mark("sc"), ve=p.mark("ve"))
        p.dma("gpsimd", WQs[:, :], i["WQr"].ap()[:, :, :], "awl")
        p.dma("gpsimd", WKs[:, :], i["WKr"].ap()[:, :, :], "awl")
        lv = p.dma("gpsimd", WVs[:, :], i["WVr"].ap()[:, :, :], "awl")
        p.waits("tensor", awl=lv, ve=xt_ve, g2=self.init_marks["g2"],
                ld=self.init_marks["ld"], ldg=self.init_marks["ldg"])

        hh1 = self.emit_bcast(self.hf_c[0:1, 0:1], hfc[:, 0:1], 1, psp[0])
        p.wait("vector", "ve", hh1)

        def fihf(v):
            v.tensor_scalar(hfc[:, 1:2], hfc[:, 0:1], -1.0, 1.0,
                            ALU.mult, ALU.add).then_inc(p.h["ve"], 1)
        p.emit("vector", fihf)
        hf_ve = p.bump("ve")

        q_sc = {}
        gi = 0
        for dq in range(NDC):
            for ts in range(NT // 512):
                pp = psp[gi % 2]
                if gi >= 2:
                    p.wait("tensor", "sc", q_sc[gi - 2])
                elif gi == 0:
                    p.wait("tensor", "ve", hh1)

                def fq(t, dq=dq, ts=ts, pp=pp):
                    ins = None
                    for c in range(NDC):
                        ins = t.matmul(pp[:, :],
                                       WQs[:, c * D + dq * 128:c * D + (dq + 1) * 128],
                                       xt[:, c * NT + ts * 512:c * NT + ts * 512 + 512],
                                       start=(c == 0), stop=(c == NDC - 1))
                    ins.then_inc(p.h["mm"], 1)
                p.emit("tensor", fq)
                mv = p.bump("mm")
                p.wait("scalar", "mm", mv)

                def fqe(s2, dq=dq, ts=ts, pp=pp):
                    s2.activation(Q_fm[:, dq * NT + ts * 512:dq * NT + ts * 512 + 512],
                                  pp[:, :], AF.Copy, scale=1.0 / 32.0).then_inc(p.h["sc"], 1)
                p.emit("scalar", fqe)
                q_sc[gi] = p.bump("sc")
                gi += 1

        k_ve = {}
        k_st = {}
        gi = 0
        for dq in range(NDC):
            for ts in range(NT // 512):
                pp = psp[2 + gi % 2]
                slot = gi % 2
                if gi >= 2:
                    p.wait("tensor", "ve", k_ve[gi - 2])

                def fk(t, dq=dq, ts=ts, pp=pp):
                    ins = None
                    for c in range(NDC):
                        ins = t.matmul(pp[:, :],
                                       WKs[:, c * D + dq * 128:c * D + (dq + 1) * 128],
                                       xt[:, c * NT + ts * 512:c * NT + ts * 512 + 512],
                                       start=(c == 0), stop=(c == NDC - 1))
                    ins.then_inc(p.h["mm"], 1)
                p.emit("tensor", fk)
                mv = p.bump("mm")
                p.wait("vector", "mm", mv)
                if gi >= 2:
                    p.wait("vector", "ks%d" % slot, k_st[gi - 2])

                def fke(v, slot=slot, pp=pp):
                    v.tensor_copy(kst[:, slot * 512:(slot + 1) * 512],
                                  pp[:, :]).then_inc(p.h["ve"], 1)
                p.emit("vector", fke)
                k_ve[gi] = p.bump("ve")
                p.wait("sync", "ve", k_ve[gi])
                k_st[gi] = p.dma("sync",
                                 bass.AP(K_hbm, (dq * 128) * HKN + 256 + ts * 512,
                                         [[HKN, 128], [1, 512]]),
                                 kst[:, slot * 512:(slot + 1) * 512], "ks%d" % slot)
                gi += 1

        p.wait("tensor", "ve", k_ve[gi - 1])
        p.wait("tensor", "sc", q_sc[len(q_sc) - 1])
        v_ve = {}
        v_st = {}
        vs_mm_marks = {}
        NTT = NT // 128
        for tt in range(NTT):
            pp2 = [psp[tt % 2], psp[2 + tt % 2]]
            if tt >= 2:
                p.wait("tensor", "ve", v_ve[tt - 2])

            def fv(t, tt=tt, pp2=pp2):
                ins = None
                for sl in range(2):
                    for c in range(NDC):
                        ins = t.matmul(pp2[sl][:, :],
                                       xt[:, c * NT + tt * 128:c * NT + (tt + 1) * 128],
                                       WVs[:, c * D + sl * 512:c * D + sl * 512 + 512],
                                       start=(c == 0), stop=(c == NDC - 1))
                ins.then_inc(p.h["mm"], 1)
            p.emit("tensor", fv)
            mv = p.bump("mm")
            p.wait("vector", "mm", mv)
            if tt >= 2:
                p.wait("vector", "vs%d" % (tt % 2), v_st[tt - 2])
                p.wait("vector", "mm", vs_mm_marks[tt - 2])

            def fve(v, tt=tt, pp2=pp2):
                v.tensor_copy(vst[:, (tt % 2) * D:(tt % 2) * D + 512], pp2[0][:, :])
                v.tensor_copy(vst[:, (tt % 2) * D + 512:(tt % 2) * D + 1024],
                              pp2[1][:, :]).then_inc(p.h["ve"], 1)
            p.emit("vector", fve)
            v_ve[tt] = p.bump("ve")
            p.wait("sync", "ve", v_ve[tt])
            v_st[tt] = p.dma("sync",
                             bass.AP(V_hbm, (256 + tt * 128) * D, [[D, 128], [1, D]]),
                             vst[:, (tt % 2) * D:(tt % 2) * D + D], "vs%d" % (tt % 2))
            p.wait("tensor", "ve", v_ve[tt])

            def fvs(t, tt=tt):
                ins = None
                for sl in range(2):
                    ins = t.matmul(psv[sl][:, :], self.ones_col[:, 0:1],
                                   vst[:, (tt % 2) * D + sl * 512:(tt % 2) * D + (sl + 1) * 512],
                                   start=(tt == 0), stop=(tt == NTT - 1))
                ins.then_inc(p.h["mm"], 1)
            p.emit("tensor", fvs)
            vs_mm_marks[tt] = p.bump("mm")
        p.wait("vector", "mm", vs_mm_marks[NTT - 1])

        def fvsum(v):
            v.tensor_copy(vsum[0:1, 0:512], psv[0][0:1, :])
            v.tensor_copy(vsum[0:1, 512:1024], psv[1][0:1, :]).then_inc(p.h["ve"], 1)
        p.emit("vector", fvsum)
        vsum_ve = p.bump("ve")

        for x in range(4):
            p.wait("gpsimd", "ks%d" % x, p.mark("ks%d" % x))
        for x in range(2):
            p.wait("gpsimd", "vs%d" % x, p.mark("vs%d" % x))
        p.wait("gpsimd", "ve", vsum_ve)
        for c in range(NDC):
            p.dma("gpsimd", kx_in.ap()[c, :, 0:256],
                  bass.AP(K_hbm, (c * 128) * HKN + 256, [[HKN, 128], [1, 256]]), "a0x")
            p.dma("gpsimd", kx_in.ap()[c, :, 256:512],
                  bass.AP(K_hbm, (c * 128) * HKN + 4096, [[HKN, 128], [1, 256]]), "a0x")
        p.dma("gpsimd", vx_in.ap()[0:256, :],
              bass.AP(V_hbm, 256 * D, [[D, 256], [1, D]]), "a0x")
        p.dma("gpsimd", vx_in.ap()[256:512, :],
              bass.AP(V_hbm, 4096 * D, [[D, 256], [1, D]]), "a0x")
        xm = p.dma("gpsimd", vs_in.ap()[:, :], vsum[0:1, :], "a0x")
        p.wait("gpsimd", "a0x", xm)
        rg = [[2 * g, 2 * g + 1] for g in range(4)]

        def fcc(g):
            h = p.h["cc"]
            g.collective_compute("AllGather", ALU.bypass, replica_groups=rg,
                                 ins=[kx_in.ap()], outs=[kx_out.ap()]).then_inc(h, 1)
            g.collective_compute("AllGather", ALU.bypass, replica_groups=rg,
                                 ins=[vx_in.ap()], outs=[vx_out.ap()]).then_inc(h, 1)
            g.collective_compute("AllReduce", ALU.add, replica_groups=rg,
                                 ins=[vs_in.ap()], outs=[vs_out.ap()]).then_inc(h, 1)
        p.emit("gpsimd", fcc)
        cc_done = p.bump("cc", 3)

        self.pop_to(scpW)
        hst = self.sb("a0_hst", [128, NDC * 256], BF16)
        vhst = self.sb("a0_vhst", [128, 2 * D], BF16)
        p.wait("sync", "cc", cc_done)
        for side in range(2):
            smark = p.mark("a0h1")
            hl = 0
            for c in range(NDC):
                hl = p.dma("sync", hst[:, c * 256:(c + 1) * 256],
                           kx_out.ap()[side, c, :, (1 - side) * 256:(1 - side) * 256 + 256], "a0h0")
            p.wait("vector", "a0h0", hl)
            p.wait("vector", "ve", hf_ve)
            p.wait("vector", "a0h1", smark)
            hv = 0
            for c in range(NDC):
                def fh(v, c=c, side=side):
                    v.tensor_scalar(hst[:, c * 256:(c + 1) * 256],
                                    hst[:, c * 256:(c + 1) * 256],
                                    hfc[:, side:side + 1], None,
                                    ALU.mult).then_inc(p.h["ve"], 1)
                p.emit("vector", fh)
                hv = p.bump("ve")
            p.wait("sync", "ve", hv)
            base = 0 if side == 0 else 4352
            for c in range(NDC):
                p.dma("sync", bass.AP(K_hbm, (c * 128) * HKN + base, [[HKN, 128], [1, 256]]),
                      hst[:, c * 256:(c + 1) * 256], "a0h1")
            smark = p.mark("a0h1")
            lds = 0
            for r2 in range(2):
                lds = p.dma("sync", vhst[:, r2 * D:(r2 + 1) * D],
                            vx_out.ap()[side, (1 - side) * 256 + r2 * 128:(1 - side) * 256 + (r2 + 1) * 128, :], "a0h0")
            p.wait("vector", "a0h0", lds)
            p.wait("vector", "a0h1", smark)
            vvm = 0
            for r2 in range(2):
                def fvh(v, r2=r2, side=side):
                    v.tensor_scalar(vhst[:, r2 * D:(r2 + 1) * D],
                                    vhst[:, r2 * D:(r2 + 1) * D],
                                    hfc[:, side:side + 1], None,
                                    ALU.mult).then_inc(p.h["ve"], 1)
                p.emit("vector", fvh)
                vvm = p.bump("ve")
            p.wait("sync", "ve", vvm)
            for r2 in range(2):
                p.dma("sync", bass.AP(V_hbm, (base + r2 * 128) * D, [[D, 128], [1, D]]),
                      vhst[:, r2 * D:(r2 + 1) * D], "a0h1")
        vg = p.dma("sync", vsg[:, :], vs_out.ap()[:, :], "a0h0")

        self.pop_to(scp)

        halo_done = p.mark("a0h1")
        KW = self.sb("a0_KW", [128, 2 * NDC * 640], BF16)
        VW = self.sb("a0_VW", [128, 2 * 5 * D], BF16)
        PW = self.sb("a0_PW", [128, 640], F32)
        WT = self.sb("a0_WT", [128, 640], BF16)
        WTT = self.sb("a0_WTT", [128, 5 * 128], BF16)
        TMP = self.sb("a0_TMP", [128, D], F32)
        OUT = self.sb("a0_OUT", [128, D], BF16)
        asc = self.sb("a0_asc", [128, 6], F32)
        xon = self.sb("a0_xo", [128, 4 * 128], F32)
        xnn = self.sb("a0_xn", [128, 4 * 128], F32)
        ps_s = self.ps("a0_ps_s", [128, 640])
        ps_o = [self.ps("a0_ps_o%d" % k2, [128, D]) for k2 in range(1)]
        ps_t = [self.ps("a0_ps_t%d" % k2, [128, 128], BF16) for k2 in range(2)]
        psh = self.ps("a0_psh2", [128, 512])

        NQT = NT // 128
        kw_m = {}
        vw_m = {}
        out_mm = {}
        sc_rel = {}
        ev_rel = {}
        xo_m = {}
        st_m = {}
        xa_rel = {}
        pst_rel = [0, 0]

        def emit_win_load(oi):
            qt = QT_ORDER[oi]
            slot = oi % 2
            if oi >= 2:
                p.wait("gpsimd", "mm", out_mm[QT_ORDER[oi - 2]])
            if qt in HALO_QTS:
                p.wait("gpsimd", "a0h1", halo_done)
            else:
                for x2 in range(4):
                    p.wait("gpsimd", "ks%d" % x2, kv_marks[0][x2])
                for x2 in range(2):
                    p.wait("gpsimd", "vs%d" % x2, kv_marks[1][x2])
            w0 = qt * 128
            v = 0
            for c in range(NDC):
                v = p.dma("gpsimd",
                          KW[:, slot * NDC * 640 + c * 640:slot * NDC * 640 + (c + 1) * 640],
                          bass.AP(K_hbm, (c * 128) * HKN + w0, [[HKN, 128], [1, 640]]),
                          "kw%d" % slot)
            kw_m[qt] = v
            v = 0
            for kt in range(5):
                v = p.dma("gpsimd",
                          VW[:, slot * 5 * D + kt * D:slot * 5 * D + (kt + 1) * D],
                          bass.AP(V_hbm, (w0 + kt * 128) * D, [[D, 128], [1, D]]),
                          "vw%d" % slot)
            vw_m[qt] = v

        kv_marks = ([p.mark("ks%d" % x) for x in range(4)],
                    [p.mark("vs%d" % x) for x in range(2)])
        HALO_QTS = {0, 1, 30, 31}
        QT_ORDER = [q2 for q2 in range(2, 30)] + [0, 1, 30, 31]
        vb_state = [0]

        def vop(fn):
            def g2(v, fn=fn):
                fn(v).then_inc(p.h["ve"], 1)
            p.emit("vector", g2)
            vv = p.bump("ve")
            p.wait("vector", "ve", vv)
            return vv

        emit_win_load(0)
        emit_win_load(1)
        prev_qt = [None]
        for oi in range(NQT):
            qt = QT_ORDER[oi]
            slot = oi % 2
            p.waits("tensor", **{("kw%d" % slot): kw_m[qt]})
            if prev_qt[0] is not None:
                p.wait("tensor", "sc", sc_rel[prev_qt[0]])

            def fsc(t, slot=slot, qt=qt):
                ins = None
                for c in range(NDC):
                    base = slot * NDC * 640 + c * 640
                    t.matmul(ps_s[:, 0:512],
                             Q_fm[:, c * NT + qt * 128:c * NT + (qt + 1) * 128],
                             KW[:, base:base + 512],
                             start=(c == 0), stop=(c == NDC - 1))
                for c in range(NDC):
                    base = slot * NDC * 640 + c * 640
                    ins = t.matmul(ps_s[:, 512:640],
                                   Q_fm[:, c * NT + qt * 128:c * NT + (qt + 1) * 128],
                                   KW[:, base + 512:base + 640],
                                   start=(c == 0), stop=(c == NDC - 1))
                ins.then_inc(p.h["mm"], 1)
            p.emit("tensor", fsc)
            sc_mm = p.bump("mm")

            p.wait("vector", "mm", sc_mm)
            vop(lambda v: v.tensor_reduce(asc[:, 0:1], ps_s[:, :], AXL.X, ALU.max,
                                          negate=True))
            m1n_ve = vop(lambda v: v.tensor_scalar_min(asc[:, 0:1], asc[:, 0:1], 0.0))
            p.wait("scalar", "ve", m1n_ve)

            def fexp(s2):
                s2.activation(asc[:, 1:2], asc[:, 0:1], AF.Exp)
                s2.activation(PW[:, :], ps_s[:, :], AF.Exp,
                              bias=asc[:, 0:1]).then_inc(p.h["sc"], 1)
            p.emit("scalar", fexp)
            sc_rel[qt] = p.bump("sc")
            p.wait("vector", "sc", sc_rel[qt])
            p.wait("vector", "mm", getattr(self, "_a0_last_wtr", 0))
            wt_ve = vop(lambda v: v.scalar_tensor_tensor(WT[:, :], PW[:, :], asc[:, 1:2],
                                                 self.bandm[:, :], ALU.subtract, ALU.mult,
                                                 accum_out=asc[:, 2:3]))
            vop(lambda v: v.scalar_tensor_tensor(asc[:, 3:4], asc[:, 1:2], float(SEQ),
                                                 asc[:, 2:3], ALU.mult, ALU.add))
            vop(lambda v: v.reciprocal(asc[:, 4:5], asc[:, 3:4]))

            p.wait("tensor", "ve", wt_ve)
            for kt in range(5):
                p.wait("tensor", "ve", pst_rel[kt % 2])

                def ftr(t, kt=kt):
                    t.transpose(ps_t[kt % 2][:, :], WT[:, kt * 128:(kt + 1) * 128],
                                self.ident_bf[:, :]).then_inc(p.h["mm"], 1)
                p.emit("tensor", ftr)
                mtv = p.bump("mm")
                p.wait("vector", "mm", mtv)

                def fwc(v, kt=kt):
                    v.tensor_copy(WTT[:, kt * 128:(kt + 1) * 128],
                                  ps_t[kt % 2][:, :]).then_inc(p.h["ve"], 1)
                p.emit("vector", fwc)
                pst_rel[kt % 2] = p.bump("ve")
                if kt == 4:
                    self._a0_last_wtr = mtv

            p.waits("tensor", **{("vw%d" % slot): vw_m[qt]})
            p.wait("tensor", "ve", pst_rel[0])
            p.wait("tensor", "ve", pst_rel[1])
            if oi >= 1:
                p.wait("tensor", "ve", ev_rel[QT_ORDER[oi - 1]])
            po = ps_o[0]

            def fo(t, slot=slot, po=po):
                ins = None
                for kt in range(5):
                    for sl in range(2):
                        ins = t.matmul(po[:, sl * 512:(sl + 1) * 512],
                                       WTT[:, kt * 128:(kt + 1) * 128],
                                       VW[:, slot * 5 * D + kt * D + sl * 512:slot * 5 * D + kt * D + (sl + 1) * 512],
                                       start=(kt == 0), stop=(kt == 4))
                ins.then_inc(p.h["mm"], 1)
            p.emit("tensor", fo)
            out_mm[qt] = p.bump("mm")
            if oi + 2 < NQT:
                emit_win_load(oi + 2)

            if oi == 0:
                vb_state[0] = self.emit_bcast(vsg[0:1, :], Vb[:, :], D, psh,
                                              extra_tensor_wait=("a0h0", vg))
            p.wait("vector", "mm", out_mm[qt])
            p.wait("vector", "ve", vb_state[0])
            p.wait("vector", "mm", getattr(self, "_a0_last_ot", 0))
            prev_qt[0] = qt
            vop(lambda v, po=po: v.scalar_tensor_tensor(TMP[:, :], Vb[:, :], asc[:, 1:2],
                                                        po[:, :], ALU.mult, ALU.add))
            ev_rel[qt] = vop(lambda v: v.tensor_scalar(OUT[:, :], TMP[:, :], asc[:, 4:5],
                                                       None, ALU.mult))

            for dc in range(NDC):
                idx = oi * NDC + dc
                sl4 = idx % 4
                if idx >= 4:
                    p.wait("sync", "ve", xa_rel[idx - 4])
                xo_m[idx] = p.dma("sync", xon[:, sl4 * 128:(sl4 + 1) * 128],
                                  src.ap()[dc * 128:(dc + 1) * 128, qt * 128:(qt + 1) * 128],
                                  "xoa%d" % sl4)
                p.wait("tensor", "ve", pst_rel[dc % 2])
                p.wait("tensor", "ve", ev_rel[qt])

                def fot(t, dc=dc):
                    t.transpose(ps_t[dc % 2][:, :], OUT[:, dc * 128:(dc + 1) * 128],
                                self.ident_bf[:, :]).then_inc(p.h["mm"], 1)
                p.emit("tensor", fot)
                mo = p.bump("mm")
                p.wait("vector", "mm", mo)
                p.wait("vector", "xoa%d" % sl4, xo_m[idx])
                if idx >= 4:
                    p.wait("vector", "sna%d" % sl4, st_m[idx - 4])

                def fxa(v, dc=dc, sl4=sl4):
                    v.tensor_add(xnn[:, sl4 * 128:(sl4 + 1) * 128],
                                 ps_t[dc % 2][:, :],
                                 xon[:, sl4 * 128:(sl4 + 1) * 128]).then_inc(p.h["ve"], 1)
                p.emit("vector", fxa)
                xa_rel[idx] = p.bump("ve")
                pst_rel[dc % 2] = xa_rel[idx]
                p.wait("sync", "ve", xa_rel[idx])
                st_m[idx] = p.dma("sync",
                                  dst.ap()[dc * 128:(dc + 1) * 128, qt * 128:(qt + 1) * 128],
                                  xnn[:, sl4 * 128:(sl4 + 1) * 128], "sna%d" % sl4)
                if dc == NDC - 1:
                    self._a0_last_ot = mo
                if dc == NDC - 1:
                    self._a1_last_ot = mo

    _BKT_TH = [1, 2, 3, 4, 5, 6, 7, 8, 9, 10, 11, 12, 13, 14, 15, 16, 19, 21, 24, 27, 31, 35, 40, 46, 52, 59, 67, 77, 87, 99, 113]
    _BKT_BJ = [1, 2, 3, 4, 5, 6, 7, 8, 9, 10, 11, 12, 13, 14, 15, 16, 17, 18, 19, 20, 21, 22, 23, 24, 25, 26, 27, 28, 29, 30, 31]
    _BKT_BP = [0, 1, 2, 3, 4, 5, 6, 7, 8, 9, 10, 11, 12, 13, 14, 15, 16, 17, 18, 19, 20, 21, 22, 23, 24, 25, 26, 27, 28, 29, 30]

    def _attn1(self, src, dst):
        p = self.p
        i = self.inputs
        nc = self.nc
        MU = math.sqrt(0.5)
        SIG = math.sqrt(0.25 * math.pi)
        ERF_SC = 1.0 / (SIG * math.sqrt(2.0))
        ERF_B = -MU / (SIG * math.sqrt(2.0))

        kx_in = nc.dram_tensor(self._un("a1_kxi"), [128, NT], BF16, kind="Internal")
        kx_out = nc.dram_tensor(self._un("a1_kxo"), [2, 128, NT], BF16, kind="Internal")
        vx_in = nc.dram_tensor(self._un("a1_vxi"), [NT, D], BF16, kind="Internal")
        vx_out = nc.dram_tensor(self._un("a1_vxo"), [2, NT, D], BF16, kind="Internal")

        for snm in (["a1wl", "a1x", "a1kv", "a1h"]
                    + ["xoa%d" % x for x in range(4)] + ["sna%d" % x for x in range(4)]
                    + ["vs%d" % x for x in range(2)] + ["ks%d" % x for x in range(2)]):
            p.sem(snm)

        Q_fm = self.sb("a1_q", [128, NT], BF16)
        K_all = self.sb("a1_k", [128, SEQ], BF16)
        BANK = self.sb("a1_bank", [128, 6 * 128], F32)
        erfb = self.sb("a1_erfb", [128, 1], F32)
        hfc = self.sb("a1_hf", [128, 2], F32)

        scp = self.scope()
        xt, xt_ve = self.emit_xt_full(src)
        wqks = self.sb("a1_wqk", [128, NDC * 128], BF16)
        wvs = self.sb("a1_wv", [128, NDC * D], BF16)
        gbc = self.sb("a1_gbc", [128, 4], F32)
        bqkc = self.sb("a1_bqk", [128, 1], F32)
        bvr = self.sb("a1_bvr", [1, D], BF16)
        qkst = self.sb("a1_qkst", [128, 2 * 512], F32)
        kstg = self.sb("a1_kstg", [128, 2 * 512], BF16)
        vst = self.sb("a1_vst", [128, 2 * D], BF16)
        psp = [self.ps("a1_pp%d" % k2, [128, 512]) for k2 in range(4)]

        p.waits("gpsimd", mm=p.mark("mm"), sc=p.mark("sc"), ve=p.mark("ve"))
        p.dma("gpsimd", wqks[:, :], i["wqkr"].ap()[:, :, :], "a1wl")
        p.dma("gpsimd", wvs[:, :], i["wvr"].ap()[:, :, :], "a1wl")
        p.dma("gpsimd", gbc[:, :], i["gbcols"].ap()[:, :], "a1wl")
        p.dma("gpsimd", bqkc[:, :], i["bqk"].ap()[:, :], "a1wl")
        lw = p.dma("gpsimd", bvr[:, :], i["bvrow"].ap()[:, :], "a1wl")
        p.waits("tensor", a1wl=lw, ve=xt_ve, g2=self.init_marks["g2"],
                ld=self.init_marks["ld"], ldg=self.init_marks["ldg"])
        p.waits("scalar", a1wl=lw)
        p.waits("vector", a1wl=lw)

        hh1 = self.emit_bcast(self.hf_c[0:1, 0:1], hfc[:, 0:1], 1, psp[0])
        p.wait("vector", "ve", hh1)

        def fihf(v):
            v.tensor_scalar(hfc[:, 1:2], hfc[:, 0:1], -1.0, 1.0,
                            ALU.mult, ALU.add).then_inc(p.h["ve"], 1)
        p.emit("vector", fihf)
        p.bump("ve")

        def fmub(g):
            g.memset(erfb[:, :], ERF_B).then_inc(p.h["g2"], 1)
        p.emit("gpsimd", fmub)
        mub_g2 = p.bump("g2")

        qk_sc = {}
        qk_ve = {}
        k_st = {}
        for ts in range(NT // 512):
            pp = psp[ts % 2]
            slot = ts % 2
            if ts >= 2:
                p.wait("tensor", "ve", qk_ve[ts - 2])
            elif ts == 0:
                p.wait("tensor", "ve", hh1)

            def fqk(t, ts=ts, pp=pp):
                ins = None
                for c in range(NDC):
                    ins = t.matmul(pp[:, :],
                                   wqks[:, c * 128:(c + 1) * 128],
                                   xt[:, c * NT + ts * 512:c * NT + ts * 512 + 512],
                                   start=(c == 0), stop=(c == NDC - 1))
                ins.then_inc(p.h["mm"], 1)
            p.emit("tensor", fqk)
            mv = p.bump("mm")
            p.wait("scalar", "mm", mv)
            if ts >= 2:
                p.wait("scalar", "ve", qk_ve[ts - 2])

            def fqke(s2, slot=slot, pp=pp):
                s2.activation(qkst[:, slot * 512:(slot + 1) * 512], pp[:, :],
                              AF.Silu, bias=bqkc[:, 0:1]).then_inc(p.h["sc"], 1)
            p.emit("scalar", fqke)
            qk_sc[ts] = p.bump("sc")
            p.wait("vector", "sc", qk_sc[ts])
            if ts >= 2:
                p.wait("vector", "ks%d" % slot, k_st[ts - 2])

            def fqv(v, ts=ts, slot=slot):
                v.tensor_scalar(Q_fm[:, ts * 512:(ts + 1) * 512],
                                qkst[:, slot * 512:(slot + 1) * 512],
                                gbc[:, 0:1], gbc[:, 1:2], ALU.mult, ALU.add)
                v.tensor_scalar(kstg[:, slot * 512:(slot + 1) * 512],
                                qkst[:, slot * 512:(slot + 1) * 512],
                                gbc[:, 2:3], gbc[:, 3:4], ALU.mult,
                                ALU.add).then_inc(p.h["ve"], 1)
            p.emit("vector", fqv)
            qk_ve[ts] = p.bump("ve")
            p.wait("sync", "ve", qk_ve[ts])
            k_st[ts] = p.dma("sync", kx_in.ap()[:, ts * 512:(ts + 1) * 512],
                             kstg[:, slot * 512:(slot + 1) * 512], "ks%d" % slot)

        v_ve = {}
        v_st = {}
        NTT = NT // 128
        for tt in range(NTT):
            pp2 = [psp[2], psp[3]] if tt % 2 == 0 else [psp[0], psp[1]]
            if tt >= 2:
                p.wait("tensor", "sc", v_ve[tt - 2])
            elif tt == 1:
                p.wait("tensor", "ve", qk_ve[NT // 512 - 1])

            def fv(t, tt=tt, pp2=pp2):
                ins = None
                for sl in range(2):
                    for c in range(NDC):
                        t.matmul(pp2[sl][:, :],
                                 xt[:, c * NT + tt * 128:c * NT + (tt + 1) * 128],
                                 wvs[:, c * D + sl * 512:c * D + sl * 512 + 512],
                                 start=(c == 0), stop=False)
                    ins = t.matmul(pp2[sl][:, :], self.ones_row[0:1, 0:128],
                                   bvr[0:1, sl * 512:(sl + 1) * 512],
                                   start=False, stop=True)
                ins.then_inc(p.h["mm"], 1)
            p.emit("tensor", fv)
            mv = p.bump("mm")
            p.wait("scalar", "mm", mv)
            if tt >= 2:
                p.wait("scalar", "vs%d" % (tt % 2), v_st[tt - 2])

            def fve(s2, tt=tt, pp2=pp2):
                s2.activation(vst[:, (tt % 2) * D:(tt % 2) * D + 512],
                              pp2[0][:, :], AF.Silu)
                s2.activation(vst[:, (tt % 2) * D + 512:(tt % 2) * D + 1024],
                              pp2[1][:, :], AF.Silu).then_inc(p.h["sc"], 1)
            p.emit("scalar", fve)
            v_ve[tt] = p.bump("sc")
            p.wait("sync", "sc", v_ve[tt])
            v_st[tt] = p.dma("sync",
                             bass.AP(vx_in, (tt * 128) * D, [[D, 128], [1, D]]),
                             vst[:, (tt % 2) * D:(tt % 2) * D + D], "vs%d" % (tt % 2))


        for x in range(2):
            p.wait("gpsimd", "ks%d" % x, p.mark("ks%d" % x))
            p.wait("gpsimd", "vs%d" % x, p.mark("vs%d" % x))
        rg = [[2 * g, 2 * g + 1] for g in range(4)]

        def fcc(g):
            h = p.h["cc"]
            g.collective_compute("AllGather", ALU.bypass, replica_groups=rg,
                                 ins=[kx_in.ap()], outs=[kx_out.ap()]).then_inc(h, 1)
            g.collective_compute("AllGather", ALU.bypass, replica_groups=rg,
                                 ins=[vx_in.ap()], outs=[vx_out.ap()]).then_inc(h, 1)
        p.emit("gpsimd", fcc)
        cc_done = p.bump("cc", 2)

        self.pop_to(scp)
        p.wait("sync", "cc", cc_done)
        ka = p.dma("sync", K_all[:, 0:NT], kx_out.ap()[0, :, :], "a1h")
        ka = p.dma("sync", K_all[:, NT:SEQ], kx_out.ap()[1, :, :], "a1h")
        V_all = self.sb("a1_vall", [128, (SEQ // 128) * D], BF16)
        va = p.dma("sync", V_all[:, :],
                   bass.AP(vx_out, 0, [[D, 128], [128 * D, SEQ // 128], [1, D]]), "a1h")

        scbk = self.scope()
        pspb = [self.ps("a1_pb%d" % k2, [128, 512]) for k2 in range(2)]
        relsc = self.sb("a1_rel", [1, 3 * NUM_BUCKETS], F32)
        drow = self.sb("a1_drow", [1, NUM_BUCKETS + 2], F32)
        dcols = self.sb("a1_dcols", [128, NUM_BUCKETS + 2], F32)
        dtile = self.sb("a1_dt", [128, 2 * 128], F32)
        base6 = self.sb("a1_b6", [128, 6 * 128], F32)
        ohj = self.sb("a1_ohj", [NUM_BUCKETS, 2 * NUM_BUCKETS], F32)

        ldr = p.dma("sync", relsc[0:1, 0:NUM_BUCKETS], i["relb"].ap()[:, :], "a1h")
        ldo = p.dma("sync", ohj[:, :], self.bktoh_c.ap()[:, :], "a1h")
        relc = self.sb("a1_relc", [NUM_BUCKETS, 1], F32)
        ldc = p.dma("sync", relc[:, :], bass.AP(i["relb"], 0, [[1, NUM_BUCKETS], [1, 1]]), "a1h")

        NJ2 = len(self._BKT_TH)
        p.wait("tensor", "a1h", ldc)
        pb = pspb[0]

        def fga(t):
            t.matmul(pb[0:1, 0:2 * NUM_BUCKETS], relc[:, 0:1], ohj[:, :],
                     start=True, stop=True).then_inc(p.h["mm"], 1)
        p.emit("tensor", fga)
        mga = p.bump("mm")
        p.wait("vector", "mm", mga)
        p.wait("vector", "a1h", ldc)

        def fgb(v):
            v.tensor_copy(relsc[0:1, NUM_BUCKETS:3 * NUM_BUCKETS],
                          pb[0:1, 0:2 * NUM_BUCKETS]).then_inc(p.h["ve"], 1)
        p.emit("vector", fgb)
        vv = p.bump("ve")
        p.wait("vector", "ve", vv)
        SQ128 = float(math.sqrt(128.0))

        def fdr1(v):
            v.tensor_sub(drow[0:1, 0:NJ2], relsc[0:1, NUM_BUCKETS:NUM_BUCKETS + NJ2],
                         relsc[0:1, 2 * NUM_BUCKETS:2 * NUM_BUCKETS + NJ2]).then_inc(p.h["ve"], 1)
        p.emit("vector", fdr1)
        vv = p.bump("ve")
        p.wait("vector", "ve", vv)

        def fdr2(v):
            v.tensor_scalar(drow[0:1, 0:NJ2], drow[0:1, 0:NJ2], SQ128, None,
                            ALU.mult).then_inc(p.h["ve"], 1)
        p.emit("vector", fdr2)
        vv = p.bump("ve")
        p.wait("vector", "ve", vv)

        def fb0(v):
            v.tensor_scalar(drow[0:1, NJ2:NJ2 + 1], relsc[0:1, 0:1], SQ128, None, ALU.mult)
            v.tensor_scalar(drow[0:1, NJ2 + 1:NJ2 + 2],
                            relsc[0:1, NUM_BUCKETS - 1:NUM_BUCKETS], SQ128, None,
                            ALU.mult).then_inc(p.h["ve"], 1)
        p.emit("vector", fb0)
        dr_ve = p.bump("ve")
        NJx = NJ2 + 2
        bc_ve = self.emit_bcast(drow[0:1, 0:NJx], dcols[:, 0:NJx],
                                NJx, pspb[1], extra_tensor_wait=("ve", dr_ve))

        def fio(g):
            h = p.h["g2"]
            g.iota(dtile[:, 0:128], [[1, 128]], base=0, channel_multiplier=-1,
                   allow_small_or_imprecise_dtypes=True).then_inc(h, 1)
            g.iota(dtile[:, 128:256], [[1, 128]], base=128, channel_multiplier=-1,
                   allow_small_or_imprecise_dtypes=True).then_inc(h, 1)
        p.emit("gpsimd", fio)
        io_g2 = p.bump("g2", 2)

        p.wait("vector", "g2", io_g2)
        p.wait("vector", "ve", bc_ve)
        for w2 in range(2):
            def finit(v, w2=w2):
                v.tensor_scalar(base6[:, w2 * 128:(w2 + 1) * 128],
                                dtile[:, w2 * 128:(w2 + 1) * 128], -1e30,
                                dcols[:, NJ2:NJ2 + 1], ALU.is_ge,
                                ALU.mult).then_inc(p.h["ve"], 1)
            p.emit("vector", finit)
            vv = p.bump("ve")
            p.wait("vector", "ve", vv)
            for j in range(NJ2):
                def facc1(v, w2=w2, j=j):
                    v.tensor_scalar(base6[:, 512 + 128 * w2:512 + 128 * (w2 + 1)],
                                    dtile[:, w2 * 128:(w2 + 1) * 128],
                                    float(self._BKT_TH[j]), dcols[:, j:j + 1],
                                    ALU.is_ge, ALU.mult).then_inc(p.h["ve"], 1)
                p.emit("vector", facc1)
                vv = p.bump("ve")
                p.wait("vector", "ve", vv)

                def facc2(v, w2=w2):
                    v.tensor_add(base6[:, w2 * 128:(w2 + 1) * 128],
                                 base6[:, w2 * 128:(w2 + 1) * 128],
                                 base6[:, 512 + 128 * w2:512 + 128 * (w2 + 1)]).then_inc(p.h["ve"], 1)
                p.emit("vector", facc2)
                vv = p.bump("ve")
                p.wait("vector", "ve", vv)
        def fca1(v):
            v.tensor_scalar(base6[:, 512:640], dtile[:, 0:128], 0.0, BIG_NEG,
                            ALU.is_lt, ALU.mult).then_inc(p.h["ve"], 1)
        p.emit("vector", fca1)
        vv = p.bump("ve")
        p.wait("vector", "ve", vv)

        def fca2(v):
            v.tensor_add(base6[:, 0:128], base6[:, 0:128],
                         base6[:, 512:640]).then_inc(p.h["ve"], 1)
        p.emit("vector", fca2)
        vv = p.bump("ve")
        p.wait("vector", "ve", vv)

        def fct(v):
            v.tensor_scalar(base6[:, 256:384], self.ident[:, :], 0.0,
                            dcols[:, NJ2 + 1:NJ2 + 2], ALU.mult, ALU.add).then_inc(p.h["ve"], 1)
        p.emit("vector", fct)
        vv = p.bump("ve")

        def fbn(g):
            g.memset(base6[:, 384:512], BIG_NEG).then_inc(p.h["g2"], 1)
        p.emit("gpsimd", fbn)
        bn_g2 = p.bump("g2")
        p.wait("vector", "g2", bn_g2)
        p.wait("vector", "ve", vv)

        pairs = [(3, 0), (3, 1), (3, 2), (0, 2), (1, 2), (2, 2)]
        DIAG0, C1280, CONST0, BIGN0 = 0, 128, 256, 384
        offs = [DIAG0, C1280, CONST0, BIGN0]
        for si, (a0, b0) in enumerate(pairs):
            def fsel1(v, si=si, a0=a0):
                v.tensor_scalar(BANK[:, si * 128:(si + 1) * 128],
                                base6[:, offs[a0]:offs[a0] + 128],
                                hfc[:, 1:2], None, ALU.mult).then_inc(p.h["ve"], 1)
            p.emit("vector", fsel1)
            vv = p.bump("ve")
            p.wait("vector", "ve", vv)

            def fsel2(v, b0=b0):
                v.tensor_scalar(base6[:, 640:768],
                                base6[:, offs[b0]:offs[b0] + 128],
                                hfc[:, 0:1], None, ALU.mult).then_inc(p.h["ve"], 1)
            p.emit("vector", fsel2)
            vv = p.bump("ve")
            p.wait("vector", "ve", vv)

            def fsel3(v, si=si):
                v.tensor_add(BANK[:, si * 128:(si + 1) * 128],
                             BANK[:, si * 128:(si + 1) * 128],
                             base6[:, 640:768]).then_inc(p.h["ve"], 1)
            p.emit("vector", fsel3)
            vv = p.bump("ve")
            p.wait("vector", "ve", vv)
        bank_ve = p.mark("ve")

        self.pop_to(scbk)

        SIMF = self.sb("a1_simf", [128, 8 * 128], F32)
        PSI = self.sb("a1_psi", [128, 8 * 128], F32)
        WTB = self.sb("a1_wtb", [128, 8 * 128], BF16)
        OUT = self.sb("a1_out", [128, D], BF16)
        xon = self.sb("a1_xo", [128, 4 * 128], F32)
        xnn = self.sb("a1_xn", [128, 4 * 128], F32)
        ps_sim = [self.ps("a1_ps_s%d" % k2, [128, 128]) for k2 in range(4)]
        ps_o = [self.ps("a1_ps_o%d" % k2, [128, D]) for k2 in range(1)]
        ps_t = [self.ps("a1_ps_t%d" % k2, [128, 128], BF16) for k2 in range(2)]

        p.waits("tensor", a1h=p.mark("a1h"), ve=bank_ve, g2=p.mark("g2"))
        p.waits("scalar", g2=mub_g2)

        NQT = NT // 128
        LA = 6
        xo_m = {}
        st_m = {}
        xa_rel = {}
        pst_rel = [0, 0]
        ev_rel = {}

        def vop(fn):
            def g2f(v, fn=fn):
                fn(v).then_inc(p.h["ve"], 1)
            p.emit("vector", g2f)
            vv2 = p.bump("ve")
            return vv2

        sim_last = [0, 0, 0, 0]
        for qt in range(NQT):
            extent = 33 + qt
            po = ps_o[0]
            sim_rel = {}
            wtb_ve = {}
            psi_sc = {}

            def emit_sim(kt):
                r = extent - 1 - kt
                si = 0 if r == 0 else 1 if r == 1 else 2 if r <= 31 else \
                     3 if r == 32 else 4 if r == 33 else 5
                pss = ps_sim[kt % 4]
                p.wait("tensor", "ve", sim_rel[kt - 4] if kt >= 4 else sim_last[kt % 4])

                def fsim(t, kt=kt, pss=pss, qt=qt):
                    t.matmul(pss[:, :], K_all[:, kt * 128:(kt + 1) * 128],
                             Q_fm[:, qt * 128:(qt + 1) * 128],
                             start=True, stop=True).then_inc(p.h["mm"], 1)
                p.emit("tensor", fsim)
                mv2 = p.bump("mm")
                p.wait("vector", "mm", mv2)
                st4 = kt % 8
                if kt >= 8:
                    p.wait("vector", "sc", psi_sc[kt - 8])

                def fadd(v, st4=st4, si=si, pss=pss):
                    v.tensor_add(SIMF[:, st4 * 128:(st4 + 1) * 128], pss[:, :],
                                 BANK[:, si * 128:(si + 1) * 128]).then_inc(p.h["ve"], 1)
                p.emit("vector", fadd)
                sim_rel[kt] = p.bump("ve")
                sim_last[kt % 4] = sim_rel[kt]
                p.wait("scalar", "ve", sim_rel[kt])
                if kt >= 8:
                    p.wait("scalar", "ve", wtb_ve[kt - 8])

                def ferf(s2, st4=st4):
                    s2.activation(PSI[:, st4 * 128:(st4 + 1) * 128],
                                  SIMF[:, st4 * 128:(st4 + 1) * 128],
                                  getattr(AF, ERF_FN_NAME),
                                  bias=erfb[:, 0:1], scale=ERF_SC).then_inc(p.h["sc"], 1)
                p.emit("scalar", ferf)
                psi_sc[kt] = p.bump("sc")
                p.wait("vector", "sc", psi_sc[kt])
                if kt >= 8:
                    p.wait("vector", "mm", out_done[kt - 8])

                def fwtb(v, st4=st4):
                    v.tensor_scalar(WTB[:, st4 * 128:(st4 + 1) * 128],
                                    PSI[:, st4 * 128:(st4 + 1) * 128],
                                    0.5, 0.5, ALU.mult, ALU.add).then_inc(p.h["ve"], 1)
                p.emit("vector", fwtb)
                wtb_ve[kt] = p.bump("ve")

            out_done = {}

            def emit_out(kt):
                st4 = kt % 8
                p.wait("tensor", "ve", wtb_ve[kt])
                if kt == 0 and qt >= 1:
                    p.wait("tensor", "ve", ev_rel[qt - 1])

                def fo(t, kt=kt, st4=st4, po=po, extent=extent):
                    ins = None
                    for sl in range(2):
                        ins = t.matmul(po[:, sl * 512:(sl + 1) * 512],
                                       WTB[:, st4 * 128:(st4 + 1) * 128],
                                       V_all[:, kt * D + sl * 512:kt * D + (sl + 1) * 512],
                                       start=(kt == 0), stop=(kt == extent - 1))
                    ins.then_inc(p.h["mm"], 1)
                p.emit("tensor", fo)
                out_done[kt] = p.bump("mm")

            for j in range(extent + LA):
                if j < extent:
                    emit_sim(j)
                if j >= LA:
                    emit_out(j - LA)

            p.wait("vector", "mm", out_done[extent - 1])
            p.wait("vector", "mm", getattr(self, "_a1_last_ot", 0))
            ev_rel[qt] = vop(lambda v, po=po: v.tensor_copy(OUT[:, :], po[:, :]))
            p.wait("vector", "ve", ev_rel[qt])

            for dc in range(NDC):
                idx = qt * NDC + dc
                sl4 = idx % 4
                if idx >= 4:
                    p.wait("sync", "ve", xa_rel[idx - 4])
                xo_m[idx] = p.dma("sync", xon[:, sl4 * 128:(sl4 + 1) * 128],
                                  src.ap()[dc * 128:(dc + 1) * 128, qt * 128:(qt + 1) * 128],
                                  "xoa%d" % sl4)
                p.wait("tensor", "ve", pst_rel[dc % 2])
                p.wait("tensor", "ve", ev_rel[qt])

                def fot(t, dc=dc):
                    t.transpose(ps_t[dc % 2][:, :], OUT[:, dc * 128:(dc + 1) * 128],
                                self.ident_bf[:, :]).then_inc(p.h["mm"], 1)
                p.emit("tensor", fot)
                mo = p.bump("mm")
                p.wait("vector", "mm", mo)
                p.wait("vector", "xoa%d" % sl4, xo_m[idx])
                if idx >= 4:
                    p.wait("vector", "sna%d" % sl4, st_m[idx - 4])

                def fxa(v, dc=dc, sl4=sl4):
                    v.tensor_add(xnn[:, sl4 * 128:(sl4 + 1) * 128],
                                 ps_t[dc % 2][:, :],
                                 xon[:, sl4 * 128:(sl4 + 1) * 128]).then_inc(p.h["ve"], 1)
                p.emit("vector", fxa)
                xa_rel[idx] = p.bump("ve")
                pst_rel[dc % 2] = xa_rel[idx]
                p.wait("sync", "ve", xa_rel[idx])
                st_m[idx] = p.dma("sync",
                                  dst.ap()[dc * 128:(dc + 1) * 128, qt * 128:(qt + 1) * 128],
                                  xnn[:, sl4 * 128:(sl4 + 1) * 128], "sna%d" % sl4)
                if dc == NDC - 1:
                    self._a1_last_ot = mo

    def emit_ffn(self, layer, src, dst):
        for q in range(MK_QUARTERS if MK_QUARTERS else NQ):
            sc0 = self.scope()
            stat = self.sb("st_stat", [1, 4 * QN], F32)
            xt = self.sb("f_xt", [128, NDC * QN], BF16)
            sc1 = self.scope()
            SM, SA, st_sc = self.emit_stats(src, q * QN, QN, stat)
            self.pop_to(sc1)
            xt_ve = self.emit_xtilde(src, xt, q * QN, QN, stat, SM, SA, st_sc)
            self.pop_to(sc1)
            if MK_DBG and not self.dbg_done[0]:
                self.p.wait("gpsimd", "ve", xt_ve)
                self.p.dma("gpsimd", self.dbg_xt.ap()[:, :], xt[:, :], "wt")
                self.p.dma("gpsimd", self.dbg_stat.ap()[:, :], stat[0:1, :], "wt")
            self._ffn_quarter(layer, src, dst, q, xt, xt_ve)
            self.pop_to(sc0)

    def _ffn_quarter(self, layer, src, dst, q, xt, xt_ve):
        p = self.p
        i = self.inputs
        t0 = q * QN
        if layer == 0:
            NE, HC, NPE = L0_E, L0_HC, L0_E + 2
        else:
            NE, HC, NPE = L1_E, L1_HC, L1_E

        cwrow = self.sb("f_cwrow", [NE, QN], F32)
        cwfm = self.sb("f_cwfm", [NE, QN], BF16)
        gate_scope = self.scope()
        gwsb = self.sb("f_gw", [128, NDC, 16], BF16)
        gbsb = self.sb("f_gb", [1, NE], BF16)
        gsc = self.sb("f_gsc", [128, 8], F32)
        gst = self.sb("f_gst", [128, NE], F32)
        gtmp = self.sb("f_gtmp", [128, NE], F32)
        gtmp2 = self.sb("f_gtmp2", [128, NE], F32)
        gp = self.ps("f_gp", [128, NE])
        gtp = self.ps("f_gtp", [NE, 128])

        p.waits("gpsimd", mm=p.mark("mm"), sc=p.mark("sc"), ve=p.mark("ve"))
        p.sem("gld")
        gw_in = (i["gate_w"] if layer == 0 else i["Wgr"])
        lw = 0
        for c in range(NDC):
            lw = p.dma("gpsimd", gwsb[:, c, 0:NE], gw_in.ap()[:, c, :], "gld")
        lb = p.dma("gpsimd", gbsb[:, :],
                   (i["gate_b"] if layer == 0 else i["bg_1"]).ap()[:, :], "gld")
        p.waits("tensor", gld=lb, ve=xt_ve, g2=self.init_marks["g2"],
                ld=self.init_marks["ld"], ldg=self.init_marks["ldg"])

        NTT = QN // 128
        tile_done = {"ve": 0, "sc": 0}
        for tt in range(NTT):
            p.wait("tensor", "ve", tile_done["ve"])
            p.wait("tensor", "sc", tile_done["sc"])

            def fmm(t, tt=tt):
                for c in range(NDC):
                    t.matmul(gp[:, :], xt[:, c * QN + tt * 128:c * QN + (tt + 1) * 128],
                             gwsb[:, c, 0:NE], start=(c == 0), stop=False)
                t.matmul(gp[:, :], self.ones_row[0:1, 0:128], gbsb[0:1, :],
                         start=False, stop=True).then_inc(p.h["mm"], 1)
            p.emit("tensor", fmm)
            mv = p.bump("mm")

            p.wait("vector", "mm", mv)

            def fm1(v):
                v.tensor_reduce(gsc[:, 0:1], gp[:, :], AXL.X, ALU.max,
                                negate=True).then_inc(p.h["ve"], 1)
            p.emit("vector", fm1)
            v1 = p.bump("ve")

            if layer == 0:
                p.wait("scalar", "ve", v1)

                def fexp(s):
                    s.activation(gst[:, :], gp[:, :], AF.Exp, bias=gsc[:, 0:1],
                                 accum_out=gsc[:, 1:2]).then_inc(p.h["sc"], 1)
                p.emit("scalar", fexp)
                s1 = p.bump("sc")
                tile_done["sc"] = s1
                p.wait("vector", "sc", s1)

                def vop(fn):
                    def g(v, fn=fn):
                        fn(v).then_inc(p.h["ve"], 1)
                    p.emit("vector", g)
                    vv = p.bump("ve")
                    p.wait("vector", "ve", vv)
                    return vv

                vop(lambda v: v.reciprocal(gsc[:, 2:3], gsc[:, 1:2]))
                vop(lambda v: v.tensor_scalar(gst[:, :], gst[:, :], gsc[:, 2:3], None, ALU.mult))
                vop(lambda v: v.reduce_max(gsc[:, 3:4], gst[:, :], axis=AXL.X))
                vop(lambda v: v.tensor_scalar(gtmp[:, :], gst[:, :], gsc[:, 3:4], BIG_NEG,
                                              ALU.is_ge, ALU.mult))
                vop(lambda v: v.tensor_add(gtmp[:, :], gtmp[:, :], gst[:, :]))
                vop(lambda v: v.reduce_max(gsc[:, 4:5], gtmp[:, :], axis=AXL.X))
                vop(lambda v: v.tensor_scalar(gtmp2[:, :], gst[:, :], gsc[:, 4:5], None, ALU.is_ge))
                vc = vop(lambda v: v.tensor_mul(gtmp[:, :], gtmp2[:, :], gst[:, :]))
            else:
                p.wait("vector", "ve", v1)

                def vop(fn):
                    def g(v, fn=fn):
                        fn(v).then_inc(p.h["ve"], 1)
                    p.emit("vector", g)
                    vv = p.bump("ve")
                    p.wait("vector", "ve", vv)
                    return vv

                vop(lambda v: v.tensor_scalar(gtmp[:, :], gp[:, :], gsc[:, 0:1], None, ALU.add))
                vop(lambda v: v.tensor_scalar(gst[:, :], gtmp[:, :], 0.0, BIG_NEG, ALU.is_ge, ALU.mult))
                vop(lambda v: v.tensor_add(gst[:, :], gst[:, :], gtmp[:, :]))
                vs = vop(lambda v: v.tensor_reduce(gsc[:, 1:2], gst[:, :], AXL.X, ALU.max))
                p.wait("scalar", "ve", vs)

                def fden(s):
                    s.activation(gsc[:, 5:6], gsc[:, 1:2], AF.Exp).then_inc(p.h["sc"], 1)
                p.emit("scalar", fden)
                p.bump("sc")

                def fexp(s):
                    s.activation(gst[:, :], gtmp[:, :], AF.Exp).then_inc(p.h["sc"], 1)
                p.emit("scalar", fexp)
                se = p.bump("sc")
                tile_done["sc"] = se
                p.wait("vector", "sc", se)

                vop(lambda v: v.tensor_scalar(gsc[:, 6:7], gsc[:, 5:6], 1.0, None, ALU.add))
                vop(lambda v: v.reciprocal(gsc[:, 7:8], gsc[:, 6:7]))
                vop(lambda v: v.tensor_scalar(gtmp2[:, :], gtmp[:, :], gsc[:, 1:2], None, ALU.is_ge))
                vop(lambda v: v.tensor_mul(gtmp2[:, :], gtmp2[:, :], gst[:, :]))
                vc = vop(lambda v: v.tensor_scalar(gtmp2[:, :], gtmp2[:, :], gsc[:, 7:8], None, ALU.mult))

            p.wait("tensor", "ve", vc)
            trsrc = gtmp if layer == 0 else gtmp2

            def ftr(t, trsrc=trsrc):
                t.transpose(gtp[:, :], trsrc[:, :], self.ident[:, :]).then_inc(p.h["mm"], 1)
            p.emit("tensor", ftr)
            mt = p.bump("mm")
            p.wait("vector", "mm", mt)

            def fcpr(v, tt=tt):
                v.tensor_copy(cwrow[:, tt * 128:(tt + 1) * 128], gtp[0:NE, :])
                v.tensor_copy(cwfm[:, tt * 128:(tt + 1) * 128],
                              gtp[0:NE, :]).then_inc(p.h["ve"], 1)
            p.emit("vector", fcpr)
            tile_done["ve"] = p.bump("ve")
        gating_ve = p.mark("ve")
        if MK_DBG and not self.dbg_done[0]:
            p.wait("gpsimd", "ve", gating_ve)
            p.dma("gpsimd", self.dbg_cw.ap()[0:NE, :], cwrow[:, :], "wt")
        self.pop_to(gate_scope)

        y_acc = self.sb("f_yacc", [128, NDC * QN], F32)
        RING = 4
        W1C = self.sb("f_w1c", [128, RING * 1024], BF16)
        W3C = self.sb("f_w3c", [128, RING * 1024], BF16) if layer == 0 else None
        W2C = self.sb("f_w2c0", [128, HC * 1024], BF16)
        B1C = self.sb("f_b1c", [128, NPE * HC], F32)
        B3C = self.sb("f_b3c", [128, NPE * HC], F32) if layer == 0 else None
        B2S = self.sb("f_b2s", [NE, D], BF16)
        SB2S = self.sb("f_sb2s", [1, D], BF16) if layer == 0 else None
        CWB = self.sb("f_cwb", [128, 2 * QN], F32)
        AEB = self.sb("f_aeb", [128, HC * QN], BF16)
        NSTG = 4
        HSB = self.sb("f_hs", [128, NSTG * 512], F32)
        GCB = self.sb("f_gc", [128, NSTG * 512], F32) if layer == 0 else None

        psA = [self.ps("f_psA%d" % k, [128, 512]) for k in range(4)]
        psB = [self.ps("f_psB%d" % k, [128, 512]) for k in range(4)]

        p.waits("gpsimd", mm=p.mark("mm"), sc=p.mark("sc"), ve=p.mark("ve"))

        p.sem("bld")
        for e in range(NE):
            p.dma("gpsimd", B1C[:, e * HC:(e + 1) * HC],
                  (i["b1r"] if layer == 0 else i["b1_1r"]).ap()[e, :, :], "bld")
            if layer == 0:
                p.dma("gpsimd", B3C[:, e * HC:(e + 1) * HC], i["b3r"].ap()[e, :, :], "bld")
        if layer == 0:
            p.dma("gpsimd", B1C[:, NE * HC:NE * HC + L0_SHC], i["sb1r"].ap()[:, :], "bld")
            p.dma("gpsimd", B3C[:, NE * HC:NE * HC + L0_SHC], i["sb3r"].ap()[:, :], "bld")
            p.dma("gpsimd", SB2S[:, :], i["sb2"].ap()[:, :], "bld")
        bias_wt = p.dma("gpsimd", B2S[:, :],
                        (i["b2"] if layer == 0 else i["b2_1"]).ap()[:, :], "bld")

        sched = [(e, hh) for e in range(NPE) for hh in range(HC)]
        NCH = len(sched)

        w13_marks = [0] * NCH
        w13_rel = [0] * NCH
        w2_marks = [0] * NPE
        phaseB_rel = [0] * NPE
        evac_marks = [0] * NPE

        def bcol(e, hh, which):
            buf = B1C if which == 0 else B3C
            return buf[:, e * HC + hh:e * HC + hh + 1]

        def w1_src(e, hh):
            if layer == 0:
                if e < NE:
                    return (i["w1r"].ap()[e, hh, :, :], i["w3r"].ap()[e, hh, :, :])
                hh2 = (e - NE) * HC + hh
                return (i["sw1r"].ap()[hh2, :, :], i["sw3r"].ap()[hh2, :, :])
            return (i["W1r"].ap()[e, hh, :, :], None)

        def w2_src(e, hh):
            if layer == 0:
                if e < NE:
                    return bass.AP(i["w2"], (e * L0_H + hh * 128) * D, [[D, 128], [1, D]])
                hh2 = (e - NE) * HC + hh
                return bass.AP(i["sw2"], (hh2 * 128) * D, [[D, 128], [1, D]])
            return i["W2r"].ap()[e, hh, :, :]

        for _slot in range(RING):
            p.sem("w13s%d" % _slot)

        def emit_w13_load(k):
            e, hh = sched[k]
            slot = k % RING
            if k >= RING:
                p.wait("gpsimd", "mm", w13_rel[k - RING])
            s1, s3 = w1_src(e, hh)
            sname = "w13s%d" % slot
            v = p.dma("gpsimd", W1C[:, slot * 1024:(slot + 1) * 1024], s1, sname)
            if layer == 0:
                v = p.dma("gpsimd", W3C[:, slot * 1024:(slot + 1) * 1024], s3, sname)
            w13_marks[k] = (sname, v)

        p.sem("w2s")

        def emit_w2_load(e):
            if e >= 1:
                p.wait("gpsimd", "mm", phaseB_rel[e - 1])
            buf = W2C
            v = 0
            for hh in range(HC):
                v = p.dma("gpsimd", buf[:, hh * 1024:(hh + 1) * 1024], w2_src(e, hh), "w2s")
            w2_marks[e] = v

        for k in range(min(RING, NCH)):
            emit_w13_load(k)
        emit_w2_load(0)

        p.waits("tensor", bld=bias_wt, ve=gating_ve)
        p.waits("scalar", bld=bias_wt)

        a_ve = []
        grp = 0
        cwb_ready = [0, 0]
        last_evac = [0]

        def emit_cwb(e):
            if layer == 0 and e >= NE:
                return
            eslot = e % 2
            for sl in range(QSL):
                pp = psB[sl]

                def fb(t, e=e, sl=sl, pp=pp):
                    t.matmul(pp[:, :], self.onehot[0:NE, e * 128:(e + 1) * 128],
                             cwrow[:, sl * 512:(sl + 1) * 512],
                             start=True, stop=True).then_inc(p.h["mm"], 1)
                p.emit("tensor", fb)
                mv = p.bump("mm")
                p.wait("vector", "mm", mv)

                def fbc(v, eslot=eslot, sl=sl, pp=pp):
                    v.tensor_copy(CWB[:, eslot * QN + sl * 512:eslot * QN + (sl + 1) * 512],
                                  pp[:, :]).then_inc(p.h["ve"], 1)
                p.emit("vector", fbc)
                p.bump("ve")
            cwb_ready[e % 2] = p.mark("ve")

        emit_cwb(0)

        for k in range(NCH):
            e, hh = sched[k]
            slot = k % RING
            last_of_e = (hh == HC - 1)

            p.wait("tensor", *w13_marks[k])
            for ts in range(QSL):
                pa = psA[2 * (grp % 2)]
                pg = psA[2 * (grp % 2) + 1]
                stg = grp % NSTG
                gv = a_ve[grp - 2] if grp >= 2 else 0
                p.wait("tensor", "ve", max(gv, last_evac[0] if hh == 0 else 0))

                if layer == 0:
                    def fA(t, slot=slot, ts=ts, pa=pa, pg=pg):
                        ins = None
                        for c in range(NDC):
                            t.matmul(pa[:, :],
                                     W1C[:, slot * 1024 + c * 128:slot * 1024 + (c + 1) * 128],
                                     xt[:, c * QN + ts * 512:c * QN + ts * 512 + 512],
                                     start=(c == 0), stop=(c == NDC - 1))
                        for c in range(NDC):
                            ins = t.matmul(pg[:, :],
                                     W3C[:, slot * 1024 + c * 128:slot * 1024 + (c + 1) * 128],
                                     xt[:, c * QN + ts * 512:c * QN + ts * 512 + 512],
                                     start=(c == 0), stop=(c == NDC - 1))
                        ins.then_inc(p.h["mm"], 1)
                else:
                    def fA(t, slot=slot, ts=ts, pa=pa, pg=pg):
                        ins = None
                        for c in range(NDC):
                            ins = t.matmul(pa[:, :],
                                     W1C[:, slot * 1024 + c * 128:slot * 1024 + (c + 1) * 128],
                                     xt[:, c * QN + ts * 512:c * QN + ts * 512 + 512],
                                     start=(c == 0), stop=(c == NDC - 1))
                        ins.then_inc(p.h["mm"], 1)
                p.emit("tensor", fA)
                mA = p.bump("mm")

                if layer == 0:
                    if hh == 0 and ts == 0 and e < NE:
                        p.wait("vector", "ve", cwb_ready[e % 2])
                    p.wait("scalar", "mm", mA)
                    if grp >= NSTG:
                        p.wait("scalar", "ve", a_ve[grp - NSTG])

                    def fhs(s, pa=pa, stg=stg, b1c=bcol(e, hh, 0)):
                        s.activation(HSB[:, stg * 512:(stg + 1) * 512], pa[:, :],
                                     AF.Silu, bias=b1c).then_inc(p.h["sc"], 1)
                    p.emit("scalar", fhs)
                    sA = p.bump("sc")

                    p.wait("vector", "mm", mA)
                    if e < NE:
                        def fgc(v, pg=pg, stg=stg, b3c=bcol(e, hh, 1), eslot=e % 2, ts=ts):
                            v.scalar_tensor_tensor(
                                GCB[:, stg * 512:(stg + 1) * 512], pg[:, :], b3c,
                                CWB[:, eslot * QN + ts * 512:eslot * QN + ts * 512 + 512],
                                ALU.add, ALU.mult).then_inc(p.h["ve"], 1)
                    else:
                        def fgc(v, pg=pg, stg=stg, b3c=bcol(e, hh, 1)):
                            v.tensor_scalar(GCB[:, stg * 512:(stg + 1) * 512], pg[:, :],
                                            b3c, None, ALU.add).then_inc(p.h["ve"], 1)
                    p.emit("vector", fgc)
                    vgc = p.bump("ve")
                    p.wait("vector", "ve", vgc)
                    p.wait("vector", "sc", sA)

                    def fam(v, stg=stg, hh=hh, ts=ts):
                        v.tensor_mul(AEB[:, hh * QN + ts * 512:hh * QN + ts * 512 + 512],
                                     HSB[:, stg * 512:(stg + 1) * 512],
                                     GCB[:, stg * 512:(stg + 1) * 512]).then_inc(p.h["ve"], 1)
                    p.emit("vector", fam)
                    a_ve.append(p.bump("ve"))
                else:
                    if hh == 0 and ts == 0:
                        p.wait("vector", "ve", cwb_ready[e % 2])
                    p.wait("vector", "mm", mA)

                    def fre(v, pa=pa, stg=stg, b1c=bcol(e, hh, 0)):
                        v.tensor_scalar(HSB[:, stg * 512:(stg + 1) * 512], pa[:, :],
                                        b1c, 0.0, ALU.add, ALU.max).then_inc(p.h["ve"], 1)
                    p.emit("vector", fre)
                    vre = p.bump("ve")
                    p.wait("vector", "ve", vre)

                    def fam(v, stg=stg, hh=hh, ts=ts, eslot=e % 2):
                        v.tensor_mul(AEB[:, hh * QN + ts * 512:hh * QN + ts * 512 + 512],
                                     HSB[:, stg * 512:(stg + 1) * 512],
                                     CWB[:, eslot * QN + ts * 512:eslot * QN + ts * 512 + 512]).then_inc(p.h["ve"], 1)
                    p.emit("vector", fam)
                    a_ve.append(p.bump("ve"))
                grp += 1
            w13_rel[k] = p.mark("mm")

            if k + RING < NCH:
                emit_w13_load(k + RING)

            if last_of_e:
                w2buf = W2C
                p.wait("tensor", "w2s", w2_marks[e])
                p.wait("tensor", "ve", a_ve[-1])
                gi = 0
                evB = []
                for dh in range(2):
                    for ts in range(QSL):
                        pset = psB if gi % 2 == 0 else psA
                        if gi >= 2:
                            p.wait("tensor", "ve", evB[gi - 2])
                        gi += 1

                        def fB(t, e=e, dh=dh, ts=ts, w2buf=w2buf, pset=pset):
                            ins = None
                            for dc in range(4):
                                do = dh * 4 + dc
                                if e == 0:
                                    t.matmul(pset[dc][:, :], B2S[:, do * 128:(do + 1) * 128],
                                             cwfm[:, ts * 512:(ts + 1) * 512],
                                             start=True, stop=False)
                                    if layer == 0:
                                        t.matmul(pset[dc][:, :],
                                                 SB2S[0:1, do * 128:(do + 1) * 128],
                                                 self.ones_row[0:1, :],
                                                 start=False, stop=False)
                                for hh2 in range(HC):
                                    ins = t.matmul(pset[dc][:, :],
                                             w2buf[:, hh2 * 1024 + do * 128:hh2 * 1024 + (do + 1) * 128],
                                             AEB[:, hh2 * QN + ts * 512:hh2 * QN + ts * 512 + 512],
                                             start=(hh2 == 0 and e != 0),
                                             stop=(hh2 == HC - 1))
                            ins.then_inc(p.h["mm"], 1)
                        p.emit("tensor", fB)
                        mB = p.bump("mm")

                        p.wait("vector", "mm", mB)

                        def fev(v, e=e, dh=dh, ts=ts, pset=pset):
                            ins = None
                            for dc in range(4):
                                do = dh * 4 + dc
                                dsl = y_acc[:, do * QN + ts * 512:do * QN + ts * 512 + 512]
                                if e == 0:
                                    ins = v.tensor_copy(dsl, pset[dc][:, :])
                                else:
                                    ins = v.tensor_add(dsl, dsl, pset[dc][:, :])
                            ins.then_inc(p.h["ve"], 1)
                        p.emit("vector", fev)
                        evB.append(p.bump("ve"))
                evac_marks[e] = p.mark("ve")
                last_evac[0] = evac_marks[e]
                phaseB_rel[e] = p.mark("mm")
                if e + 1 < NPE:
                    emit_w2_load(e + 1)
                p.wait("tensor", "ve", evac_marks[e])
                if e + 1 < NPE:
                    emit_cwb(e + 1)

        if MK_DBG and not self.dbg_done[0]:
            self.dbg_done[0] = True
            p.wait("gpsimd", "ve", p.mark("ve"))
            p.dma("gpsimd", self.dbg_yacc.ap()[:, :], y_acc[:, :], "wt")
        xold = self.sb("f_xold", [128, 2 * QN], F32)
        xnew = self.sb("f_xnew", [128, 2 * QN], F32)
        yve = p.mark("ve")
        ld_m = {}
        res_marks = []
        st_prev = {}
        p.sem("xo0"); p.sem("xo1"); p.sem("sn0"); p.sem("sn1")

        def emit_load(c):
            if c >= 2:
                p.wait("sync", "ve", res_marks[c - 2])
            ld_m[c] = (("xo%d" % (c % 2)),
                       p.dma("sync", xold[:, (c % 2) * QN:(c % 2) * QN + QN],
                             src.ap()[c * 128:(c + 1) * 128, t0:t0 + QN], "xo%d" % (c % 2)))

        emit_load(0)
        emit_load(1)
        for c in range(NDC):
            p.wait("vector", *ld_m[c])
            p.wait("vector", "ve", yve)
            if c >= 2:
                p.wait("vector", *st_prev[c - 2])

            def fres(v, c=c):
                v.scalar_tensor_tensor(xnew[:, (c % 2) * QN:(c % 2) * QN + QN],
                                       y_acc[:, c * QN:(c + 1) * QN], 0.5,
                                       xold[:, (c % 2) * QN:(c % 2) * QN + QN],
                                       ALU.mult, ALU.add).then_inc(p.h["ve"], 1)
            p.emit("vector", fres)
            res_marks.append(p.bump("ve"))
            p.wait("sync", "ve", res_marks[-1])
            st_prev[c] = (("sn%d" % (c % 2)),
                          p.dma("sync", dst.ap()[c * 128:(c + 1) * 128, t0:t0 + QN],
                                xnew[:, (c % 2) * QN:(c % 2) * QN + QN], "sn%d" % (c % 2)))
            if c + 2 < NDC:
                emit_load(c + 2)


def build_nc(n_sublayers):
    b = Builder()
    p = b.p
    b.build_consts()

    chain = [("ffn", 0), ("attn", 0), ("ffn", 0), ("ffn", 1), ("attn", 1), ("ffn", 1)]
    ch_env = os.environ.get("MK_CHAIN", "")
    if ch_env:
        chain = [(c[:-1], int(c[-1])) for c in ch_env.split(",")]
    else:
        chain = chain[:n_sublayers]
    src = b.inputs["x0"]
    ping = 0
    for si, (kind, layer) in enumerate(chain):
        dst = b.out if si == len(chain) - 1 else (b.xA if ping == 0 else b.xB)
        if si < len(chain) - 1:
            ping ^= 1
        if kind == "ffn":
            b.emit_ffn(layer, src, dst)
        else:
            b.emit_attn(layer, src, dst)
        b.join_all()
        src = dst

    for snm in list(p.cnt):
        p.wait("sync", snm, p.mark(snm))
    with b.nc.Block() as block:
        p.finalize(block)
    b.pop_to(0)
    return b.nc


def _prep_shared(inp):
    c = np.ascontiguousarray
    f = np.float32
    d = {}
    d["gate_w"] = c(inp["L0_ffn_gate_w"].reshape(NDC, 128, L0_E).transpose(1, 0, 2)).astype(f)
    d["gate_b"] = c(inp["L0_ffn_gate_b"].reshape(1, L0_E)).astype(f)
    d["w1r"] = c(np.asarray(inp["L0_ffn_w1"]).reshape(L0_E, NDC, 128, L0_HC, 128)
                 .transpose(0, 3, 2, 1, 4).reshape(L0_E, L0_HC, 128, 1024)).astype(f)
    d["w3r"] = c(np.asarray(inp["L0_ffn_w3"]).reshape(L0_E, NDC, 128, L0_HC, 128)
                 .transpose(0, 3, 2, 1, 4).reshape(L0_E, L0_HC, 128, 1024)).astype(f)
    d["w2"] = c(np.asarray(inp["L0_ffn_w2"])).astype(f)
    d["b1r"] = c(np.asarray(inp["L0_ffn_b1"]).reshape(L0_E, L0_HC, 128).transpose(0, 2, 1)).astype(f)
    d["b3r"] = c(np.asarray(inp["L0_ffn_b3"]).reshape(L0_E, L0_HC, 128).transpose(0, 2, 1)).astype(f)
    d["b2"] = c(np.asarray(inp["L0_ffn_b2"])).astype(f)
    d["sw1r"] = c(np.asarray(inp["L0_ffn_sw1"]).reshape(NDC, 128, L0_SHC, 128)
                  .transpose(2, 1, 0, 3).reshape(L0_SHC, 128, 1024)).astype(f)
    d["sw3r"] = c(np.asarray(inp["L0_ffn_sw3"]).reshape(NDC, 128, L0_SHC, 128)
                  .transpose(2, 1, 0, 3).reshape(L0_SHC, 128, 1024)).astype(f)
    d["sw2"] = c(np.asarray(inp["L0_ffn_sw2"])).astype(f)
    d["sb1r"] = c(np.asarray(inp["L0_ffn_sb1"]).reshape(L0_SHC, 128).T).astype(f)
    d["sb3r"] = c(np.asarray(inp["L0_ffn_sb3"]).reshape(L0_SHC, 128).T).astype(f)
    d["sb2"] = c(np.asarray(inp["L0_ffn_sb2"]).reshape(1, D)).astype(f)
    d["WQr"] = c(np.asarray(inp["L0_attn_WQ"]).reshape(NDC, 128, D).transpose(1, 0, 2)).astype(f)
    d["WKr"] = c(np.asarray(inp["L0_attn_WK"]).reshape(NDC, 128, D).transpose(1, 0, 2)).astype(f)
    d["WVr"] = c(np.asarray(inp["L0_attn_WV"]).reshape(NDC, 128, D).transpose(1, 0, 2)).astype(f)
    d["wqkr"] = c(np.asarray(inp["L1_attn_wqk"]).reshape(NDC, 128, 128).transpose(1, 0, 2)).astype(f)
    d["bqk"] = c(np.asarray(inp["L1_attn_bqk"]).reshape(128, 1)).astype(f)
    gam = np.asarray(inp["L1_attn_gamma"])
    bet = np.asarray(inp["L1_attn_beta"])
    gb = np.stack([gam[0] / SEQ, bet[0] / SEQ, gam[1], bet[1]], axis=1)
    d["gbcols"] = c(gb).astype(f)
    d["wvr"] = c(np.asarray(inp["L1_attn_wv"]).reshape(NDC, 128, D).transpose(1, 0, 2)).astype(f)
    d["bvrow"] = c(np.asarray(inp["L1_attn_bv"]).reshape(1, D)).astype(f)
    d["relb"] = c(np.asarray(inp["L1_attn_rel_bias"]).reshape(1, NUM_BUCKETS)).astype(f)
    d["W1r"] = c(np.asarray(inp["L1_ffn_W1"]).reshape(L1_E, L1_HC, 128, NDC, 128)
                 .transpose(0, 1, 4, 3, 2).reshape(L1_E, L1_HC, 128, 1024)).astype(f)
    d["W2r"] = c(np.asarray(inp["L1_ffn_W2"]).reshape(L1_E, D, L1_HC, 128)
                 .transpose(0, 2, 3, 1).reshape(L1_E, L1_HC, 128, 1024)).astype(f)
    d["b1_1r"] = c(np.asarray(inp["L1_ffn_b1"]).reshape(L1_E, L1_HC, 128).transpose(0, 2, 1)).astype(f)
    d["b2_1"] = c(np.asarray(inp["L1_ffn_b2"])).astype(f)
    d["Wgr"] = c(np.asarray(inp["L1_ffn_Wg"]).T.reshape(NDC, 128, L1_E).transpose(1, 0, 2)).astype(f)
    d["bg_1"] = c(np.asarray(inp["L1_ffn_bg"]).reshape(1, L1_E)).astype(f)
    return d


def prepare_in_maps(inputs):
    x = np.asarray(inputs["L0_attn_inputs"])
    shared = _prep_shared(inputs)
    in_maps = []
    for c in range(NCORES):
        seq, half = c // 2, c % 2
        xs = np.ascontiguousarray(x[seq, half * NT:(half + 1) * NT, :].T).astype(np.float32)
        m = dict(shared)
        m["x0"] = xs
        m["hflag"] = np.array([[float(half)]], dtype=np.float32)
        in_maps.append(m)
    return in_maps


def kernel(**inputs):
    in_maps = prepare_in_maps(inputs)
    nc = build_nc(N_SUBLAYERS)
    res = run_bass_kernel_spmd(nc, in_maps, core_ids=list(range(NCORES)))
    outp = np.empty((4, SEQ, D), dtype=np.float32)
    for c in range(NCORES):
        seq, half = c // 2, c % 2
        outp[seq, half * NT:(half + 1) * NT, :] = res.results[c]["out"].T
    return outp
